# version 11
# baseline (speedup 1.0000x reference)
import sys
sys.path.insert(0, "/opt/trn_rl_repo")
import numpy as np
import concourse.bass as bass
import concourse.mybir as mybir
from concourse.bass_utils import run_bass_kernel_spmd



F32 = mybir.dt.float32
BF16 = mybir.dt.bfloat16
AF = mybir.ActivationFunctionType
ALU = mybir.AluOpType
AX = mybir.AxisListType

ENGINES = ("sync", "scalar", "vector", "tensor", "gpsimd")


class Sem:
    __slots__ = ("h", "n", "name")

    def __init__(self, h, name):
        self.h = h
        self.n = 0
        self.name = name


class Prog:
    def __init__(self, nc, stack):
        self.nc = nc
        self.stack = stack
        self.streams = {e: [] for e in ENGINES}
        self.ninstr = 0

    def sem(self, name):
        return Sem(self.stack.enter_context(self.nc.semaphore(name)), name)

    def sbuf(self, name, shape, dt):
        return self.stack.enter_context(self.nc.sbuf_tensor(name, shape, dt))

    def psum(self, name, shape, dt=F32):
        return self.stack.enter_context(self.nc.psum_tensor(name, shape, dt))

    def emit(self, eng, method, kwargs, wait=(), inc=None, inc_by=1):
        wait = [(s, v) for (s, v) in wait if s is not None and v is not None]
        tgt = None
        if inc is not None:
            inc.n += inc_by
            tgt = inc.n
        self.streams[eng].append((method, kwargs, wait, inc, inc_by))
        self.ninstr += 1 + len(wait)
        return tgt

    def dma(self, eng, out, in_, sem, wait=(), **kw):
        return self.emit(eng, "dma_start", dict(out=out, in_=in_, **kw),
                         wait=wait, inc=sem, inc_by=16)

    def wait(self, eng, *pairs):
        self.streams[eng].append(
            (None, None, [(s, v) for s, v in pairs if v is not None], None, 0))
        self.ninstr += len(pairs)

    def finalize(self, block):
        for eng in ENGINES:
            ops = self.streams[eng]
            if not ops:
                continue
            section = getattr(block, eng)

            def body(engine, _ops=ops):
                for method, kwargs, wait, inc, inc_by in _ops:
                    for s, v in wait:
                        engine.wait_ge(s.h, v)
                    if method is None:
                        continue
                    ins = getattr(engine, method)(**kwargs)
                    if inc is not None:
                        ins.then_inc(inc.h, inc_by)

            section(body)
        self.streams = {e: [] for e in ENGINES}


def mm_group(P, out, pairs, wait=(), inc=None):
    n = len(pairs)
    v = None
    for i, (l, r) in enumerate(pairs):
        v = P.emit("tensor", "matmul",
                   dict(out=out, lhsT=l, rhs=r, start=(i == 0), stop=(i == n - 1)),
                   wait=list(wait) if i == 0 else (),
                   inc=inc if i == n - 1 else None)
    return v


def rearr_kt(ap, p=128):
    return ap.rearrange("(kt p) o -> p kt o", p=p)


class PsumPool:
    def __init__(self):
        self.freed = {}

    def wait(self, t):
        f = self.freed.get(id(t))
        return [f] if f is not None else []

    def free(self, t, sem, val):
        self.freed[id(t)] = (sem, val)



H, T, DIN, NST, NE, II = 1024, 4096, 2048, 16, 8, 2048
KTH, KTI = 8, 16
TC = 512
NCH = 8
NIT = 11
INTERLEAVE_SCAN = True



def alloc_globals(P, nc):
    g = {"nc": nc}
    g["pp"] = PsumPool()
    for s in ("s_x", "s_w", "s_wfree", "s_pe", "s_evA", "s_evV", "s_st", "s_cast",
              "s_pre", "s_sc", "s_evP"):
        g[s] = P.sem(s)
    g["PS"] = [P.psum(f"PS{i}", [128, 512]) for i in range(8)]
    g["slots"] = [P.sbuf(f"slot{i}", [128, 4096], BF16) for i in range(3)]
    g["hc"] = 0
    g["ring"] = {}
    g["p45"] = 0
    g["p67"] = 0
    g["wb"] = P.sbuf("wb", [128, 512], F32)
    for nm in ("S1", "S5"):
        g[nm] = {
            "name": nm,
            "xin": P.sbuf(nm + "_xin", [128, 4096], F32),
            "xn": P.sbuf(nm + "_xn", [128, 4096], BF16),
            "pgu": P.sbuf(nm + "_pgu", [128, 4096], BF16),
            "wfm": P.sbuf(nm + "_wfm", [8, 512], BF16),
            "xin_war": [],
            "xn_pe_read": None,
            "wfm_pe_read": None,
            "vcomb_last": None,
            "tD_last": None,
        }
    g["sq32"] = [P.sbuf(f"sq32_{i}", [128, 512], F32) for i in range(2)]
    g["t32"] = [P.sbuf(f"t32_{i}", [128, 512], F32) for i in range(2)]
    g["rows"] = [P.sbuf(f"row{i}", [1, 512], F32) for i in range(3)]
    g["sq_war"] = {0: None, 1: None}
    g["t32_act_read"] = {0: None, 1: None}
    g["sel8"] = P.sbuf("sel8", [8, 1024], BF16)
    g["lg32"] = P.sbuf("lg32", [128, 32], F32)
    g["wr16"] = P.sbuf("wr16", [128, 64], BF16)
    g["bias_b32"] = P.sbuf("bias_b32", [128, 8], F32)
    sm = {}
    for n_, w in (("m12", 2), ("msk", 8), ("msked", 8), ("e12", 2), ("d", 1),
                  ("r", 1), ("t1", 1), ("w1", 1), ("w2", 1), ("w8", 8)):
        sm[n_] = P.sbuf("sm_" + n_, [128, w], F32)
    g["sm"] = sm
    g["ident32"] = P.sbuf("ident32", [128, 128], F32)
    g["ones_col32"] = P.sbuf("ones_col32", [128, 1], F32)
    g["ones_row32"] = P.sbuf("ones_row32", [1, 128], F32)
    g["ones16"] = P.sbuf("ones16", [1, 128], BF16)
    g["bias_row"] = P.sbuf("bias_row", [1, 8], F32)
    g["eps_col"] = P.sbuf("eps_col", [128, 1], F32)
    g["A_neg"] = P.sbuf("A_neg", [128, 256], F32)
    g["dtw32"] = P.sbuf("dtw32", [128, 16], F32)
    g["dtb32"] = P.sbuf("dtb32", [128, 16], F32)
    g["Dd32"] = P.sbuf("Dd32", [128, 16], F32)
    g["convw32"] = P.sbuf("convw32", [128, 64], F32)
    g["convb32"] = P.sbuf("convb32", [128, 16], F32)
    g["wxp16"] = P.sbuf("wxp16", [128, 16 * 33], BF16)
    g["xssm"] = [P.sbuf("xssm0", [128, 515], F32)]
    g["carry32"] = P.sbuf("carry32", [128, 48], F32)
    g["zs16"] = [P.sbuf("zs16_0", [128, 512], BF16)]
    g["xc16s"] = [P.sbuf("xc16s_0", [128, 512], BF16)]
    g["xc32s"] = [P.sbuf("xc32s_0", [128, 512], F32)]
    g["xps16"] = P.sbuf("xps16", [33, 512], BF16)
    g["xch"] = P.sbuf("xch", [128, 4096], F32)
    g["uh"] = P.sbuf("uh", [128, 4096], BF16)
    g["ug"] = P.sbuf("ug", [128, 8192], BF16)
    g["Bv_all"] = P.sbuf("Bv_all", [128, 16 * 512], BF16)
    g["Cv_all"] = P.sbuf("Cv_all", [128, 16 * 512], BF16)
    g["dtr16"] = P.sbuf("dtr16", [128, 512], BF16)
    g["dAb"] = [P.sbuf(f"dAb{i}", [128, 512], BF16) for i in range(2)]
    g["dBxb"] = [P.sbuf(f"dBxb{i}", [128, 512], BF16) for i in range(2)]
    g["hb"] = [P.sbuf(f"hb{i}", [128, 512], BF16) for i in range(2)]
    g["yst"] = [P.sbuf(f"yst{i}", [128, 512], BF16) for i in range(2)]
    g["szst"] = [P.sbuf(f"szst{i}", [128, 512], BF16) for i in range(2)]
    g["hstate"] = P.sbuf("hstate", [128, 256], BF16)
    g["xrow"] = [P.sbuf(f"xrow{i}", [1, 512], BF16) for i in range(2)]
    g["xrow2"] = [P.sbuf(f"xrow2_{i}", [1, 512], BF16) for i in range(2)]
    g["gate_last_c"] = {}
    g["ug_pe_read"] = None
    g["xc_store"] = {}
    g["sz_store"] = {}
    g["xp_store"] = {}
    g["xres1_store"] = {}
    g["xres2_store"] = {}
    g["cast_val"] = {}
    g["dth_act_read"] = None
    g["scan_dve_last"] = None
    g["gate_last"] = None
    return g


def declare_hbm(P, g, nc):
    d, b = {}, {}
    shapes = {"su": (H, II), "sg": (H, II), "sd": (II, H),
              "winx": (H, DIN), "winz": (H, DIN), "wout": (DIN, H),
              "xp": (DIN, 33)}
    for e in range(NE):
        shapes[f"up{e}"] = (H, II)
        shapes[f"gate{e}"] = (H, II)
        shapes[f"down{e}"] = (II, H)
    shapes["router"] = (H, 8)
    for name, shp in shapes.items():
        d[name] = nc.dram_tensor("f_" + name, list(shp), F32, kind="ExternalInput").ap()
        b[name] = nc.dram_tensor("b_" + name, list(shp), BF16, kind="Internal").ap()
    g["f_in"], g["wbf"] = d, b
    cn = {}
    for name, shp in (("dtw", (128, 16)), ("dtb", (128, 16)), ("Dd", (128, 16)),
                      ("convw", (128, 64)), ("convb", (128, 16)), ("alog", (128, 256)),
                      ("bias", (1, 8)), ("ident", (128, 128)), ("ones", (128, 1)),
                      ("ones_r", (1, 128))):
        cn[name] = nc.dram_tensor("c_" + name, list(shp), F32, kind="ExternalInput").ap()
    cn["sel8"] = nc.dram_tensor("c_sel8", [8, 1024], BF16, kind="ExternalInput").ap()
    g["c_in"] = cn
    g["xc_hbm"] = nc.dram_tensor("xc_hbm", [2048, 4096], F32, kind="Internal").ap()
    g["sz_hbm"] = nc.dram_tensor("sz_hbm", [2048, 4096], BF16, kind="Internal").ap()
    g["xp_hbm"] = nc.dram_tensor("xp_hbm", [33, 4096], BF16, kind="Internal").ap()


def emit_prelude(P, g):
    nc = g["nc"]
    s_pre, s_evV, s_evA, s_cast, s_pe = (g["s_pre"], g["s_evV"], g["s_evA"],
                                         g["s_cast"], g["s_pe"])
    cn = g["c_in"]
    P.dma("sync", g["ident32"][:], cn["ident"], s_pre)
    P.dma("sync", g["ones_col32"][:], cn["ones"], s_pre)
    P.dma("sync", g["ones_row32"][:], cn["ones_r"], s_pre)
    P.dma("sync", g["bias_row"][:], cn["bias"], s_pre)
    P.dma("sync", g["sel8"][:], cn["sel8"], s_pre)
    for name, dst in (("dtw", "dtw32"), ("dtb", "dtb32"), ("Dd", "Dd32"),
                      ("convw", "convw32"), ("convb", "convb32")):
        P.dma("sync", g[dst][:], cn[name], s_pre)
    v_alog = P.dma("sync", g["A_neg"][:], cn["alog"], s_pre)
    pre_consts = s_pre.n
    P.emit("vector", "memset", dict(ap=g["eps_col"][:], constant=1e-5), inc=s_evV)
    P.emit("vector", "memset", dict(ap=g["ones16"][:], constant=1.0), inc=s_evV)
    g["h_ready"] = P.emit("vector", "memset", dict(ap=g["hstate"][:], constant=0.0),
                          inc=s_evV)
    g["carry_ready"] = P.emit("vector", "memset", dict(ap=g["carry32"][:], constant=0.0),
                              inc=s_evV)
    va = P.emit("scalar", "activation",
                dict(out=g["A_neg"][:], in_=g["A_neg"][:], func=AF.Exp),
                wait=[(s_pre, v_alog)], inc=s_evA)
    g["A_ready"] = P.emit("vector", "tensor_scalar",
                          dict(out=g["A_neg"][:], in0=g["A_neg"][:], scalar1=-1.0,
                               scalar2=None, op0=ALU.mult),
                          wait=[(s_evA, va)], inc=s_evV)
    cast_order = ["router", "su", "sg", "sd"]
    for e in range(NE):
        cast_order += [f"up{e}", f"gate{e}", f"down{e}"]
    cast_order += ["winx", "winz", "wout", "xp"]
    for name in cast_order:
        g["cast_val"][name] = P.dma("gpsimd", g["wbf"][name], g["f_in"][name],
                                    s_cast)
    g["wr_ready"] = P.dma("sync",
                          g["wr16"][:].rearrange("p (kt e) -> p kt e", kt=KTH),
                          g["wbf"]["router"].rearrange("(kt p) e -> p kt e", p=128),
                          s_pre, wait=[(s_cast, g["cast_val"]["router"])])
    g["wxp_ready"] = P.dma("sync",
                           g["wxp16"][:].rearrange("p (kt o) -> p kt o", kt=KTI),
                           g["wbf"]["xp"].rearrange("(kt p) o -> p kt o", p=128),
                           s_pre, wait=[(s_cast, g["cast_val"]["xp"])])
    t = P.emit("tensor", "matmul",
               dict(out=g["PS"][0][:, 0:8], lhsT=g["ones_row32"][0:1, :],
                    rhs=g["bias_row"][0:1, :], start=True, stop=True),
               wait=[(s_pre, pre_consts)], inc=s_pe)
    v = P.emit("vector", "tensor_copy", dict(out=g["bias_b32"][:], in_=g["PS"][0][:, 0:8]),
               wait=[(s_pe, t)], inc=s_evV)
    g["pp"].free(g["PS"][0], s_evV, v)
    P.wait("tensor", (g["s_pre"], g["wr_ready"]))



def ps45(g):
    g["p45"] = 1 - g["p45"]
    return g["PS"][4 + g["p45"]]


def ps67(g):
    g["p67"] = 1 - g["p67"]
    return g["PS"][6 + g["p67"]]


NSLOT = 3


def iter_ring_tags(it):
    tags = []
    if 2 <= it <= NCH + 1:
        tags += [("wout", q) for q in range(4)]
    for e in range(-1, NE):
        nU, nG, nD = (("su", "sg", "sd") if e < 0 else
                      (f"up{e}", f"gate{e}", f"down{e}"))
        for h in (0, 1):
            tags += [(nU, 2 * h), (nG, 2 * h), (nU, 2 * h + 1), (nG, 2 * h + 1),
                     (nD, 2 * h), (nD, 2 * h + 1)]
    if it <= NCH - 1:
        for q in range(4):
            tags += [("winx", q), ("winz", q)]
    return tags


def emit_ring_loads(P, g, it):
    s_w, s_wfree, s_cast = g["s_w"], g["s_wfree"], g["s_cast"]
    for (name, q) in iter_ring_tags(it):
        hbm = g["wbf"][name]
        if hbm.shape[0] == 1024:
            src = rearr_kt(hbm)[:, :, q * 512:(q + 1) * 512]
            kt = 8
        else:
            src = rearr_kt(hbm)[:, q * 4:(q + 1) * 4, :]
            kt = 4
        m = g["hc"]
        g["hc"] += 1
        waits = ([(s_wfree, m - (NSLOT - 1))] if m >= NSLOT else [])
        waits.append((s_cast, g["cast_val"][name]))
        slot = g["slots"][m % NSLOT]
        v = P.dma("sync", slot[:].rearrange("p (kt o) -> p kt o", kt=kt), src, s_w,
                  wait=waits)
        g["ring"][(it, name, q)] = (slot, v)


def ring_free(P, g, n=1):
    P.wait("tensor", (g["s_pe"], g["s_pe"].n))
    for _ in range(n):
        P.emit("tensor", "nop", {}, inc=g["s_wfree"])



def emit_ln(P, g, *, xsl, xn_dst, xw, pe_fence, lo=6):
    pp = g["pp"]
    s_pe, s_evA, s_evV = g["s_pe"], g["s_evA"], g["s_evV"]
    PS, sq32, t32, rows = g["PS"], g["sq32"], g["t32"], g["rows"]
    pairs = [(g["ones_col32"][:, 0:1], xsl(kt)) for kt in range(KTH)]
    mm_group(P, PS[lo][0:1, :], pairs, wait=list(xw) + pp.wait(PS[lo]))
    sq_done = []
    last_sq = None
    for kt in range(KTH):
        w = list(xw) + ([(s_pe, pe_fence)] if kt == 0 else [])
        if kt == 0:
            w += [(s_evV, g["sq_war"][0]), (s_evV, g["sq_war"][1])]
        if kt >= 2:
            w.append((s_pe, sq_done[kt - 2]))
        va = P.emit("scalar", "activation",
                    dict(out=sq32[kt % 2][:], in_=xsl(kt), func=AF.Square),
                    wait=w, inc=s_evA)
        last_sq = va
        t = P.emit("tensor", "matmul",
                   dict(out=PS[lo + 1][0:1, :], lhsT=g["ones_col32"][:, 0:1],
                        rhs=sq32[kt % 2][:], start=(kt == 0), stop=(kt == KTH - 1)),
                   wait=[(s_evA, va)] + (pp.wait(PS[lo + 1]) if kt == 0 else []),
                   inc=s_pe)
        sq_done.append(t)
    vm = P.emit("vector", "tensor_scalar",
                dict(out=rows[0][0:1, :], in0=PS[lo][0:1, :], scalar1=1.0 / 1024,
                     scalar2=None, op0=ALU.mult),
                wait=[(s_pe, sq_done[-1])], inc=s_evV)
    vms = P.emit("vector", "tensor_scalar",
                 dict(out=rows[1][0:1, :], in0=PS[lo + 1][0:1, :], scalar1=1.0 / 1024,
                      scalar2=None, op0=ALU.mult), inc=s_evV)
    pp.free(PS[lo], s_evV, vms)
    pp.free(PS[lo + 1], s_evV, vms)
    v2 = P.emit("vector", "tensor_tensor",
                dict(out=rows[2][0:1, :], in0=rows[0][0:1, :], in1=rows[0][0:1, :],
                     op=ALU.mult),
                wait=[(s_evV, vm)], inc=s_evV)
    vvar = P.emit("vector", "tensor_tensor",
                  dict(out=rows[2][0:1, :], in0=rows[1][0:1, :], in1=rows[2][0:1, :],
                       op=ALU.subtract), wait=[(s_evV, v2)], inc=s_evV)
    vln = P.emit("scalar", "activation",
                 dict(out=rows[2][0:1, :], in_=rows[2][0:1, :], func=AF.Ln,
                      bias=g["eps_col"][0:1, 0:1]),
                 wait=[(s_evV, vvar)], inc=s_evA)
    vexp = P.emit("scalar", "activation",
                  dict(out=rows[1][0:1, :], in_=rows[2][0:1, :], func=AF.Exp, scale=-0.5),
                  wait=[(s_evA, vln)], inc=s_evA)
    va = P.emit("vector", "scalar_tensor_tensor",
                dict(out=rows[2][0:1, :], in0=rows[0][0:1, :], scalar=-1.0,
                     in1=rows[1][0:1, :], op0=ALU.mult, op1=ALU.mult),
                wait=[(s_evA, vexp)], inc=s_evV)
    P.emit("tensor", "matmul",
           dict(out=PS[lo][:], lhsT=g["ones_row32"][0:1, :], rhs=rows[1][0:1, :],
                start=True, stop=True),
           wait=[(s_evV, va)] + pp.wait(PS[lo]))
    tb = P.emit("tensor", "matmul",
                dict(out=PS[lo + 1][:], lhsT=g["ones_row32"][0:1, :], rhs=rows[2][0:1, :],
                     start=True, stop=True),
                wait=pp.wait(PS[lo + 1]), inc=s_pe)
    xn_ready = None
    for kt in range(KTH):
        vt = P.emit("vector", "tensor_tensor",
                    dict(out=t32[kt % 2][:], in0=xsl(kt), in1=PS[lo][:], op=ALU.mult),
                    wait=([(s_pe, tb)] if kt == 0 else []) +
                         ([(s_evA, g["t32_act_read"][kt % 2])] if kt < 2 else []),
                    inc=s_evV)
        xn_ready = P.emit("vector", "tensor_tensor",
                          dict(out=xn_dst(kt), in0=t32[kt % 2][:], in1=PS[lo + 1][:],
                               op=ALU.add),
                          wait=[(s_evV, vt)], inc=s_evV)
    g["pp"].free(PS[lo], s_evV, xn_ready)
    g["pp"].free(PS[lo + 1], s_evV, xn_ready)
    return dict(xn_ready=xn_ready, last_sq=last_sq, last_pe=sq_done[-1])



def emit_router(P, g, S, xn_ready, lo=6):
    pp = g["pp"]
    s_pe, s_evA, s_evV = g["s_pe"], g["s_evA"], g["s_evV"]
    PS, sm, lg32 = g["PS"], g["sm"], g["lg32"]
    xn = S["xn"]
    v = None
    for tt in range(TC // 128):
        pt = PS[lo + (tt % 2)]
        pairs = [(xn[:, kt * TC + tt * 128: kt * TC + tt * 128 + 128],
                  g["wr16"][:, kt * 8:(kt + 1) * 8]) for kt in range(KTH)]
        tl = mm_group(P, pt[:, 0:8], pairs,
                      wait=([(s_evV, xn_ready), (g["s_pre"], g["wr_ready"])]
                            if tt == 0 else []) + pp.wait(pt),
                      inc=s_pe)
        lsl = lg32[:, tt * 8:(tt + 1) * 8]
        v = P.emit("vector", "tensor_tensor",
                   dict(out=lsl, in0=pt[:, 0:8], in1=g["bias_b32"][:, 0:8], op=ALU.add),
                   wait=[(s_pe, tl)], inc=s_evV)
        pp.free(pt, s_evV, v)
        m12, msk, w8 = sm["m12"], sm["msk"], sm["w8"]
        vlsl = v
        v = P.emit("vector", "tensor_reduce",
                   dict(out=m12[:, 0:1], in_=lsl, axis=AX.X, op=ALU.max),
                   wait=[(s_evV, vlsl)], inc=s_evV)
        vmsk1 = P.emit("vector", "tensor_scalar",
                       dict(out=msk[:], in0=lsl, scalar1=m12[:, 0:1], scalar2=None,
                            op0=ALU.is_equal), wait=[(s_evV, v)], inc=s_evV)
        vq = P.emit("vector", "scalar_tensor_tensor",
                    dict(out=sm["msked"][:], in0=msk[:], scalar=-1e30, in1=lsl,
                         op0=ALU.mult, op1=ALU.add), wait=[(s_evV, vmsk1)], inc=s_evV)
        v = P.emit("vector", "tensor_reduce",
                   dict(out=m12[:, 1:2], in_=sm["msked"][:], axis=AX.X, op=ALU.max),
                   wait=[(s_evV, vq)], inc=s_evV)
        v = P.emit("scalar", "activation",
                   dict(out=sm["e12"][:], in_=m12[:], func=AF.Exp, scale=-1.0),
                   wait=[(s_evV, v)], inc=s_evA)
        v = P.emit("vector", "tensor_tensor",
                   dict(out=sm["d"][:], in0=sm["e12"][:, 0:1], in1=sm["e12"][:, 1:2],
                        op=ALU.add), wait=[(s_evA, v)], inc=s_evV)
        v = P.emit("vector", "tensor_scalar",
                   dict(out=sm["d"][:], in0=sm["d"][:], scalar1=2.0, scalar2=None,
                        op0=ALU.add), wait=[(s_evV, v)], inc=s_evV)
        v = P.emit("vector", "reciprocal", dict(out=sm["r"][:], in_=sm["d"][:]),
                   wait=[(s_evV, v)], inc=s_evV)
        v = P.emit("vector", "tensor_scalar",
                   dict(out=sm["r"][:], in0=sm["r"][:], scalar1=0.5, scalar2=None,
                        op0=ALU.mult), wait=[(s_evV, v)], inc=s_evV)
        vt1 = P.emit("vector", "tensor_scalar",
                     dict(out=sm["t1"][:], in0=sm["e12"][:, 1:2], scalar1=1.0,
                          scalar2=None, op0=ALU.add), wait=[(s_evV, v)], inc=s_evV)
        vw1 = P.emit("vector", "tensor_tensor",
                     dict(out=sm["w1"][:], in0=sm["t1"][:], in1=sm["r"][:], op=ALU.mult),
                     wait=[(s_evV, vt1)], inc=s_evV)
        vt1 = P.emit("vector", "tensor_scalar",
                     dict(out=sm["t1"][:], in0=sm["e12"][:, 0:1], scalar1=1.0,
                          scalar2=None, op0=ALU.add), wait=[(s_evV, vw1)], inc=s_evV)
        v = P.emit("vector", "tensor_tensor",
                   dict(out=sm["w2"][:], in0=sm["t1"][:], in1=sm["r"][:], op=ALU.mult),
                   wait=[(s_evV, vt1)], inc=s_evV)
        vw8 = P.emit("vector", "tensor_scalar",
                     dict(out=w8[:], in0=msk[:], scalar1=sm["w1"][:, 0:1], scalar2=None,
                          op0=ALU.mult), wait=[(s_evV, v), (s_evV, vmsk1)], inc=s_evV)
        vmsk2 = P.emit("vector", "tensor_scalar",
                       dict(out=msk[:], in0=lsl, scalar1=m12[:, 1:2], scalar2=None,
                            op0=ALU.is_equal), wait=[(s_evV, vw8)], inc=s_evV)
        v = P.emit("vector", "scalar_tensor_tensor",
                   dict(out=w8[:], in0=msk[:], scalar=sm["w2"][:, 0:1], in1=w8[:],
                        op0=ALU.mult, op1=ALU.add), wait=[(s_evV, vmsk2)], inc=s_evV)
        ptp = PS[lo + ((tt + 1) % 2)]
        t2 = P.emit("tensor", "transpose",
                    dict(out=ptp[0:8, 0:128], in_=w8[:], identity=g["ident32"][:]),
                    wait=[(s_evV, v)] + pp.wait(ptp), inc=s_pe)
        v = P.emit("vector", "tensor_copy",
                   dict(out=S["wfm"][:, tt * 128:(tt + 1) * 128], in_=ptp[0:8, 0:128]),
                   wait=[(s_pe, t2)] +
                        ([(s_pe, S["wfm_pe_read"])] if tt == 0 else []),
                   inc=s_evV)
        pp.free(ptp, s_evV, v)
    return v



def emit_ffn_prologue(P, g, S, src_r, c, extra_ld_wait=(), lo=6):
    s_x, s_evV = g["s_x"], g["s_evV"]
    vld = P.dma("scalar",
                S["xin"][:].rearrange("p (kt t) -> p kt t", kt=KTH),
                src_r[:, :, c * TC:(c + 1) * TC], s_x,
                wait=list(S["xin_war"]) + list(extra_ld_wait))
    pe_fence = g["s_pe"].n
    xw = [(s_x, vld)]

    def xsl(kt):
        return S["xin"][:, kt * TC:(kt + 1) * TC]
    ln = emit_ln(P, g, xsl=xsl,
                 xn_dst=lambda kt: S["xn"][:, kt * TC:(kt + 1) * TC],
                 xw=xw, pe_fence=pe_fence, lo=lo)
    wfm_v = emit_router(P, g, S, ln["xn_ready"], lo=lo)
    S["ln"] = ln
    S["wfm_v"] = wfm_v
    return ln, wfm_v



def emit_expert_pass(P, g, ctxs, it, pump=None, pump_dn=None):
    if pump is None:
        pump = lambda: None
    if pump_dn is None:
        pump_dn = pump
    pp = g["pp"]
    s_pe, s_evA, s_evV, s_w = g["s_pe"], g["s_evA"], g["s_evV"], g["s_w"]
    PS, sq32, t32 = g["PS"], g["sq32"], g["t32"]
    for e in range(-1, NE):
        nU, nG, nD = (("su", "sg", "sd") if e < 0 else
                      (f"up{e}", f"gate{e}", f"down{e}"))
        wb_ps = {}
        for S in ctxs:
            if e >= 0:
                pw = ps67(g)
                twb = P.emit("tensor", "matmul",
                             dict(out=pw[:], lhsT=g["sel8"][:, e * 128:(e + 1) * 128],
                                  rhs=S["wfm"][0:8, 0:512], start=True, stop=True),
                             wait=pp.wait(pw) +
                                  ([(s_evV, S["wfm_v"])] if e == 0 else []),
                             inc=s_pe)
                S["wfm_pe_read"] = twb
                wb_ps[id(S)] = (pw, twb)
        for half in (0, 1):
            slotUa, vUa = g["ring"][(it, nU, 2 * half)]
            slotGa, vGa = g["ring"][(it, nG, 2 * half)]
            slotUb, vUb = g["ring"][(it, nU, 2 * half + 1)]
            slotGb, vGb = g["ring"][(it, nG, 2 * half + 1)]
            slotDa, vDa = g["ring"][(it, nD, 2 * half)]
            slotDb, vDb = g["ring"][(it, nD, 2 * half + 1)]
            for qh in (0, 1):
                sU, sG = (slotUa, slotGa) if qh == 0 else (slotUb, slotGb)
                vU, vG = (vUa, vGa) if qh == 0 else (vUb, vGb)
                for S in ctxs:
                    for otl in range(4):
                        ot = qh * 4 + otl
                        pu, pg = PS[0 + (ot % 2)], PS[2 + (ot % 2)]
                        pairs_u = [(sU[:, kt * 512 + otl * 128: kt * 512 + otl * 128 + 128],
                                    S["xn"][:, kt * TC:(kt + 1) * TC]) for kt in range(KTH)]
                        tU = mm_group(P, pu[:], pairs_u,
                                      wait=pp.wait(pu) +
                                           ([(s_w, vU)] if otl == 0 else []),
                                      inc=s_pe)
                        pairs_g = [(sG[:, kt * 512 + otl * 128: kt * 512 + otl * 128 + 128],
                                    S["xn"][:, kt * TC:(kt + 1) * TC]) for kt in range(KTH)]
                        tG = mm_group(P, pg[:], pairs_g,
                                      wait=pp.wait(pg) +
                                           ([(s_w, vG)] if otl == 0 else []),
                                      inc=s_pe)
                        S["xn_pe_read"] = tG
                        vg = P.emit("scalar", "activation",
                                    dict(out=sq32[ot % 2][:], in_=pg[:], func=AF.Silu),
                                    wait=[(s_pe, tG), (s_evV, g["sq_war"][ot % 2])],
                                    inc=s_evA)
                        pp.free(pg, s_evA, vg)
                        wvp = [(s_evA, vg), (s_pe, tU)]
                        if ot == 0:
                            wvp.append((s_pe, S["tD_last"]))
                        vp = P.emit("vector", "tensor_tensor",
                                    dict(out=S["pgu"][:, ot * TC:(ot + 1) * TC],
                                         in0=sq32[ot % 2][:], in1=pu[:], op=ALU.mult),
                                    wait=wvp, inc=s_evV)
                        pp.free(pu, s_evV, vp)
                        g["sq_war"][ot % 2] = vp
                        S["vp_last"] = vp
                        pump()
                ring_free(P, g, n=2)
            for S in ctxs:
                if e >= 0:
                    pw, twb = wb_ps[id(S)]
                    vwb = P.emit("vector", "tensor_copy",
                                 dict(out=g["wb"][:], in_=pw[:]),
                                 wait=[(s_pe, twb)], inc=s_evV)
                    if half == 1:
                        pp.free(pw, s_evV, vwb)
                for oh in range(8):
                    pd = ps45(g)
                    pairs_d = ([(slotDa[:, kt * 1024 + oh * 128: kt * 1024 + oh * 128 + 128],
                                 S["pgu"][:, kt * TC:(kt + 1) * TC]) for kt in range(4)] +
                               [(slotDb[:, kt * 1024 + oh * 128: kt * 1024 + oh * 128 + 128],
                                 S["pgu"][:, (4 + kt) * TC:(5 + kt) * TC]) for kt in range(4)])
                    tD = mm_group(P, pd[:], pairs_d,
                                  wait=pp.wait(pd) + [(s_evV, S["vp_last"])] +
                                       ([(s_w, vDa), (s_w, vDb)] if oh == 0 else []),
                                  inc=s_pe)
                    S["tD_last"] = tD
                    if e < 0:
                        vd = P.emit("vector", "tensor_scalar",
                                    dict(out=t32[oh % 2][:], in0=pd[:], scalar1=0.5,
                                         scalar2=None, op0=ALU.mult),
                                    wait=[(s_pe, tD),
                                          (s_evA, g["t32_act_read"][oh % 2])],
                                    inc=s_evV)
                    else:
                        vd = P.emit("vector", "tensor_tensor",
                                    dict(out=t32[oh % 2][:], in0=pd[:], in1=g["wb"][:],
                                         op=ALU.mult),
                                    wait=[(s_pe, tD),
                                          (s_evA, g["t32_act_read"][oh % 2])],
                                    inc=s_evV)
                    vcomb = P.emit("vector", "tensor_tensor",
                                   dict(out=S["xin"][:, oh * TC:(oh + 1) * TC],
                                        in0=S["xin"][:, oh * TC:(oh + 1) * TC],
                                        in1=t32[oh % 2][:], op=ALU.add),
                                   wait=[(s_evV, vd)], inc=s_evV)
                    pp.free(pd, s_evV, vcomb)
                    S["vcomb_last"] = vcomb
                    pump_dn()
            ring_free(P, g, n=2)



def emit_p2a(P, g, it):
    c = it
    pp = g["pp"]
    s_pe, s_evA, s_evV, s_sc, s_w = (g["s_pe"], g["s_evA"], g["s_evV"], g["s_sc"],
                                     g["s_w"])
    PS = g["PS"]
    S1 = g["S1"]
    xn = S1["xn"]
    xssm, carry, t32 = g["xssm"], g["carry32"], g["t32"]
    xc_r = g["xc_hbm"].rearrange("(kt p) t -> p kt t", p=128)
    sz = g["sz_hbm"]
    cs = slice(c * TC, (c + 1) * TC)
    pe_fence = s_pe.n
    xw = [(s_evV, S1["vcomb_last"]), (s_pe, S1["xn_pe_read"])]
    ln2 = emit_ln(P, g, xsl=lambda kt: S1["xin"][:, kt * TC:(kt + 1) * TC],
                  xn_dst=lambda kt: xn[:, kt * TC:(kt + 1) * TC],
                  xw=xw, pe_fence=pe_fence)
    wxq = [g["ring"][(it, "winx", q)] for q in range(4)]
    wzq = [g["ring"][(it, "winz", q)] for q in range(4)]
    zdma_v = g.setdefault("zdma_v", [None])
    xssm_rd = g.setdefault("xssm_rd", [None])
    silu_v = g.setdefault("silu_v", [None])
    xc16_v = g.setdefault("xc16_v", [None])
    xc32_rd = g.setdefault("xc32_rd", [None])
    xcdma_v = g.setdefault("xcdma_v", [None])
    xp_stop = None
    for i in range(KTI):
        slot, vw = wxq[i // 4]
        otl = i % 4
        px = PS[0 + (i % 2)]
        pairs = [(slot[:, kt * 512 + otl * 128: kt * 512 + otl * 128 + 128],
                  xn[:, kt * TC:(kt + 1) * TC]) for kt in range(KTH)]
        tX = mm_group(P, px[:], pairs,
                      wait=pp.wait(px) + ([(s_w, vw)] if otl == 0 else []) +
                           ([(s_evV, ln2["xn_ready"])] if i == 0 else []),
                      inc=s_pe)
        xt = g["xssm"][0]
        wA = [(s_pe, tX)]
        if xssm_rd[0] is not None:
            wA.append((s_evV, xssm_rd[0]))
        vev = P.emit("scalar", "copy", dict(out=xt[:, 3:515], in_=px[:]),
                     wait=wA, inc=s_evA)
        pp.free(px, s_evA, vev)
        vcr = P.emit("vector", "tensor_copy",
                     dict(out=xt[:, 0:3], in_=carry[:, i * 3:i * 3 + 3]),
                     wait=([(s_evV, g["carry_ready"])] if (c == 0 and i < 3) else []) +
                          [(s_evA, vev)],
                     inc=s_evV)
        wT = [(s_evV, vcr)]
        if silu_v[0] is not None:
            wT.append((s_evA, silu_v[0]))
        vt = P.emit("vector", "tensor_scalar",
                    dict(out=t32[i % 2][:], in0=xt[:, 0:512],
                         scalar1=g["convw32"][:, 4 * i:4 * i + 1], scalar2=None,
                         op0=ALU.mult),
                    wait=wT + ([(s_evA, g["t32_act_read"][i % 2])] if i < 2 else []),
                    inc=s_evV)
        for k in range(1, 4):
            vt = P.emit("vector", "scalar_tensor_tensor",
                        dict(out=t32[i % 2][:], in0=xt[:, k:k + 512],
                             scalar=g["convw32"][:, 4 * i + k:4 * i + k + 1],
                             in1=t32[i % 2][:], op0=ALU.mult, op1=ALU.add),
                        wait=[(s_evV, vt)], inc=s_evV)
        vco = P.emit("vector", "tensor_copy",
                     dict(out=carry[:, i * 3:i * 3 + 3], in_=xt[:, 512:515]),
                     inc=s_evV)
        xssm_rd[0] = vco
        wS = [(s_evV, vt)]
        if xcdma_v[0] is not None:
            wS.append((s_sc, xcdma_v[0]))
        if xc32_rd[0] is not None:
            wS.append((s_evV, xc32_rd[0]))
        vs = P.emit("scalar", "activation",
                    dict(out=g["xc32s"][0][:], in_=t32[i % 2][:],
                         func=AF.Silu, bias=g["convb32"][:, i:i + 1]),
                    wait=wS, inc=s_evA)
        silu_v[0] = vs
        g["t32_act_read"][i % 2] = vs
        xcdma_v[0] = P.dma("scalar", xc_r[:, i, cs], g["xc32s"][0][:],
                           s_sc, wait=[(s_evA, vs)])
        vxc = P.emit("vector", "tensor_copy",
                     dict(out=g["xc16s"][0][:], in_=g["xc32s"][0][:]),
                     wait=[(s_evA, vs)] +
                          ([(s_pe, xc16_v[0])] if xc16_v[0] else []),
                     inc=s_evV)
        xc32_rd[0] = vxc
        xp_stop = P.emit("tensor", "matmul",
                         dict(out=PS[6][0:33, 0:512],
                              lhsT=g["wxp16"][:, i * 33:(i + 1) * 33],
                              rhs=g["xc16s"][0][:],
                              start=(i == 0), stop=(i == KTI - 1),
                              skip_group_check=True),
                         wait=[(s_evV, vxc)] +
                              ([(g["s_pre"], g["wxp_ready"])] if i == 0 else []) +
                              (pp.wait(PS[6]) if i == 0 else []),
                         inc=s_pe)
        xc16_v[0] = xp_stop
        slotz, vwz = wzq[i // 4]
        pz = PS[2 + (i % 2)]
        pairs = [(slotz[:, kt * 512 + otl * 128: kt * 512 + otl * 128 + 128],
                  xn[:, kt * TC:(kt + 1) * TC]) for kt in range(KTH)]
        tZ = mm_group(P, pz[:], pairs,
                      wait=pp.wait(pz) + ([(s_w, vwz)] if otl == 0 else []),
                      inc=s_pe)
        S1["xn_pe_read"] = tZ
        wZ = [(s_pe, tZ)]
        if zdma_v[0] is not None:
            wZ.append((s_sc, zdma_v[0]))
        vz = P.emit("scalar", "activation",
                    dict(out=g["zs16"][0][:], in_=pz[:], func=AF.Silu),
                    wait=wZ, inc=s_evA)
        pp.free(pz, s_evA, vz)
        zdma_v[0] = P.dma("scalar", sz[i * 128:(i + 1) * 128, cs],
                          g["zs16"][0][:], s_sc, wait=[(s_evA, vz)])
        if otl == 3:
            ring_free(P, g, n=2)
    g["sz_store"][c] = zdma_v[0]
    wxpv = [(s_pe, xp_stop)]
    if (c - 1) in g["xp_store"]:
        wxpv.append((s_sc, g["xp_store"][c - 1]))
    vxp = P.emit("vector", "tensor_copy",
                 dict(out=g["xps16"][0:33, 0:512], in_=PS[6][0:33, 0:512]),
                 wait=wxpv, inc=s_evV)
    pp.free(PS[6], s_evV, vxp)
    g["xp_store"][c] = P.dma("scalar", g["xp_hbm"][:, cs], g["xps16"][0:33, 0:512],
                             s_sc, wait=[(s_evV, vxp)])
    S1["xin_war"] = [(s_evV, ln2["xn_ready"]), (s_evA, ln2["last_sq"]),
                     (s_pe, ln2["last_pe"])]



def emit_outproj(P, g, it, xres1_r, xres2_r):
    c = it - 2
    pp = g["pp"]
    s_pe, s_evV, s_sc, s_w = g["s_pe"], g["s_evV"], g["s_sc"], g["s_w"]
    cs = slice(c * TC, (c + 1) * TC)
    xch = g["xch"]
    wq = [g["ring"][(it, "wout", q)] for q in range(4)]
    vxr = P.dma("scalar", xch[:].rearrange("p (kt t) -> p kt t", kt=KTH),
                xres1_r[:, :, cs], s_sc,
                wait=[(s_sc, g["xres1_store"][c]), (s_evV, g["gate_last"])])
    vev = None
    for qp in (0, 1):
        for oh in range(KTH):
            po = ps45(g)
            pairs = []
            for q in (2 * qp, 2 * qp + 1):
                slotq = wq[q][0]
                pairs += [(slotq[:, kt * 1024 + oh * 128: kt * 1024 + oh * 128 + 128],
                           g["ug"][:, (q * 4 + kt) * TC:(q * 4 + kt + 1) * TC])
                          for kt in range(4)]
            tO = mm_group(P, po[:], pairs,
                          wait=pp.wait(po) +
                               ([(s_w, wq[2 * qp][1]), (s_w, wq[2 * qp + 1][1])] +
                                ([(s_evV, g["gate_last_c"][c])] if qp == 0 else [])
                                if oh == 0 else []),
                          inc=s_pe)
            g["ug_pe_read"] = tO
            vev = P.emit("vector", "tensor_tensor",
                         dict(out=xch[:, oh * TC:(oh + 1) * TC],
                              in0=xch[:, oh * TC:(oh + 1) * TC], in1=po[:],
                              op=ALU.add),
                         wait=[(s_pe, tO)] + ([(s_sc, vxr)] if qp == 0 else []),
                         inc=s_evV)
            pp.free(po, s_evV, vev)
        ring_free(P, g, n=2)
    g["xres2_store"][c] = P.dma("scalar", xres2_r[:, :, cs],
                                xch[:].rearrange("p (kt t) -> p kt t", kt=KTH),
                                s_sc, wait=[(s_evV, vev)])
    g["xch_war"] = [(s_sc, g["xres2_store"][c]), (s_evV, vev)]



class Scan:
    RD = 2

    def __init__(self, P, g, it):
        self.P, self.g, self.it = P, g, it
        self.c = it - 1
        self.cs = slice(self.c * TC, (self.c + 1) * TC)
        self.i = 0
        self.scan_v = {}
        self.da_v = {}
        self.dbx_v = {}
        self.hst_v = {}
        self.vdt = [None, None]
        self.vu = [None, None]
        self.vyb = [None, None]
        self.yadd_v = [{}, {}]
        self.gate_v = [{}, {}]
        self.ymul_guard = {0: (g["s_evP"], g["s_evP"].n),
                           1: (g["s_evP"], g["s_evP"].n)}
        self.h1_prepped = False
        self.carry = 0.0
        self.pe_guard = g["s_pe"].n

    def prep(self):
        P, g, c = self.P, self.g, self.c
        pp = g["pp"]
        s_pe, s_evA, s_evV, s_sc = g["s_pe"], g["s_evA"], g["s_evV"], g["s_sc"]
        xrow, xrow2 = g["xrow"], g["xrow2"]
        Bv_all, Cv_all = g["Bv_all"], g["Cv_all"]
        relay_v = {0: None, 1: None}
        bc_evac = {}
        for r in range(33):
            par = r % 2
            src_row = g["xp_hbm"][r:r + 1, self.cs]
            wld = [(s_sc, g["xp_store"][c])] if r == 0 else []
            if relay_v[par] is not None:
                wld.append((s_evV, relay_v[par]))
            vld = P.dma("scalar", xrow[par][0:1, :], src_row, s_sc, wait=wld)
            wrel = [(s_sc, vld)]
            if (r - 2) in bc_evac:
                wrel.append((s_pe, bc_evac[r - 2][0]))
            rv = P.emit("vector", "tensor_copy",
                        dict(out=xrow2[par][0:1, :], in_=xrow[par][0:1, :]),
                        wait=wrel, inc=s_evV)
            relay_v[par] = rv
            pb = ps45(g)
            tb = P.emit("tensor", "matmul",
                        dict(out=pb[:], lhsT=g["ones16"][0:1, :],
                             rhs=xrow2[par][0:1, :], start=True, stop=True),
                        wait=pp.wait(pb) + [(s_evV, rv)], inc=s_pe)
            if r == 0:
                ve = P.emit("scalar", "copy", dict(out=g["dtr16"][:], in_=pb[:]),
                            wait=[(s_pe, tb)], inc=s_evA)
                self.vdtr_e = ve
            elif r <= 16:
                n = r - 1
                ve = P.emit("scalar", "copy",
                            dict(out=Bv_all[:, n * TC:(n + 1) * TC], in_=pb[:]),
                            wait=[(s_pe, tb)], inc=s_evA)
            else:
                n = r - 17
                ve = P.emit("scalar", "copy",
                            dict(out=Cv_all[:, n * TC:(n + 1) * TC], in_=pb[:]),
                            wait=[(s_pe, tb)], inc=s_evA)
            pp.free(pb, s_evA, ve)
            bc_evac[r] = (tb, ve)
        self.bc_done = bc_evac[32][1]
        self._prep_half(0)

    def _prep_half(self, half):
        P, g, c = self.P, self.g, self.c
        s_pe, s_evA, s_evV, s_sc = g["s_pe"], g["s_evA"], g["s_evV"], g["s_sc"]
        t32, sq32 = g["t32"], g["sq32"]
        xch, uh, ug = g["xch"], g["uh"], g["ug"]
        xc_r = g["xc_hbm"].rearrange("(kt p) t -> p kt t", p=128)
        wx = list(g.get("xch_war", []))
        if half == 1:
            wx = [(s_evV, self.gate_v[0][7]), (s_sc, g["xcdma_sc_last"])]
        else:
            wx.append((s_sc, g["xcdma_v"][0]))
        vxch = P.dma("scalar", xch[:].rearrange("p (kt t) -> p kt t", kt=KTH),
                     xc_r[:, half * 8:(half + 1) * 8, self.cs], s_sc, wait=wx)
        g["xcdma_sc_last"] = vxch
        dth = ug[:, half * 4096:(half + 1) * 4096]
        vdt = {}
        prevdt = {0: None, 1: None}
        pe_snap = self.pe_guard
        for j in range(8):
            d = half * 8 + j
            par = j % 2
            a_, b_ = t32[par][:], sq32[par][:]
            w0 = [(s_evA, self.vdtr_e)]
            if prevdt[par] is not None:
                w0.append((s_evV, prevdt[par]))
            if j < 2:
                w0.append((s_evA, g["t32_act_read"][par]))
                w0.append((s_evA, s_evA.n))
            v0 = P.emit("vector", "tensor_scalar",
                        dict(out=a_, in0=g["dtr16"][:],
                             scalar1=g["dtw32"][:, d:d + 1],
                             scalar2=g["dtb32"][:, d:d + 1],
                             op0=ALU.mult, op1=ALU.add),
                        wait=w0, inc=s_evV)
            v1 = P.emit("scalar", "activation",
                        dict(out=b_, in_=a_, func=AF.Abs),
                        wait=[(s_evV, v0), (s_evV, g["sq_war"][par])] +
                             ([(s_pe, pe_snap)] if j < 2 else []),
                        inc=s_evA)
            g["t32_act_read"][par] = v1
            v2 = P.emit("scalar", "activation",
                        dict(out=b_, in_=b_, func=AF.Exp, scale=-1.0),
                        wait=[(s_evA, v1)], inc=s_evA)
            v3 = P.emit("scalar", "activation",
                        dict(out=b_, in_=b_, func=AF.Ln, bias=1.0),
                        wait=[(s_evA, v2)], inc=s_evA)
            v4 = P.emit("vector", "tensor_scalar",
                        dict(out=a_, in0=a_, scalar1=0.0, scalar2=None,
                             op0=ALU.max),
                        wait=[(s_evA, v1)], inc=s_evV)
            v5 = P.emit("vector", "tensor_tensor",
                        dict(out=dth[:, j * TC:(j + 1) * TC], in0=a_, in1=b_,
                             op=ALU.add),
                        wait=[(s_evV, v4), (s_evA, v3)] +
                             ([(s_pe, g["ug_pe_read"])] if j == 0 else []),
                        inc=s_evV)
            vdt[j] = v5
            prevdt[par] = v5
            g["sq_war"][par] = v5
        vu, vyb = {}, {}
        for j in range(8):
            vu[j] = P.emit("vector", "tensor_tensor",
                           dict(out=uh[:, j * TC:(j + 1) * TC],
                                in0=dth[:, j * TC:(j + 1) * TC],
                                in1=xch[:, j * TC:(j + 1) * TC], op=ALU.mult),
                           wait=[(s_evV, vdt[j]), (s_sc, vxch)], inc=s_evV)
        for j in range(8):
            d = half * 8 + j
            vyb[j] = P.emit("vector", "tensor_scalar",
                            dict(out=xch[:, j * TC:(j + 1) * TC],
                                 in0=xch[:, j * TC:(j + 1) * TC],
                                 scalar1=g["Dd32"][:, d:d + 1], scalar2=None,
                                 op0=ALU.mult),
                            wait=[(s_evV, vu[j])], inc=s_evV)
        self.vdt[half] = vdt
        self.vu[half] = vu
        self.vyb[half] = vyb

    def _tail(self, i):
        P, g = self.P, self.g
        s_evA, s_evV, s_evP = g["s_evA"], g["s_evV"], g["s_evP"]
        half = i // 128
        li = i % 128
        n_, j_ = divmod(li, 8)
        d_ = half * 8 + j_
        ib = i % self.RD
        col = d_ * NST + n_
        dA, dBx, hb = g["dAb"], g["dBxb"], g["hb"]
        xch = g["xch"]
        sv = P.emit("vector", "tensor_tensor_scan",
                    dict(out=hb[ib][:], data0=dA[ib][:], data1=dBx[ib][:],
                         initial=g["hstate"][:, col:col + 1],
                         op0=ALU.mult, op1=ALU.add),
                    wait=[(s_evA, self.da_v[i]), (s_evV, self.dbx_v[i])] +
                         ([(s_evV, self.hst_v[i - self.RD])] if i >= self.RD else
                          ([(s_evV, g["h_ready"])] if self.c == 0 else [])),
                    inc=s_evV)
        self.scan_v[i] = sv
        mv = P.emit("vector", "tensor_tensor",
                    dict(out=g["yst"][i % 2][:], in0=hb[ib][:],
                         in1=g["Cv_all"][:, n_ * TC:(n_ + 1) * TC], op=ALU.mult),
                    wait=[(s_evV, sv), self.ymul_guard[i % 2]],
                    inc=s_evV)
        av = P.emit("gpsimd", "tensor_tensor",
                    dict(out=xch[:, j_ * TC:(j_ + 1) * TC],
                         in0=xch[:, j_ * TC:(j_ + 1) * TC],
                         in1=g["yst"][i % 2][:], op=ALU.add),
                    wait=[(s_evV, mv), (s_evV, self.vyb[half][j_])],
                    inc=s_evP)
        self.yadd_v[half][j_] = av
        self.ymul_guard[i % 2] = (s_evP, av)
        self.hst_v[i] = P.emit("vector", "tensor_copy",
                               dict(out=g["hstate"][:, col:col + 1],
                                    in_=hb[ib][:, 511:512]),
                               wait=[(s_evV, sv)], inc=s_evV)

    def _unit(self, i):
        P, g = self.P, self.g
        s_evA, s_evV = g["s_evA"], g["s_evV"]
        half = i // 128
        li = i % 128
        n, j = divmod(li, 8)
        ib = i % self.RD
        dth = g["ug"][:, half * 4096:(half + 1) * 4096]
        wA = [(s_evV, self.vdt[half][j])]
        if i >= self.RD:
            wA.append((s_evV, self.scan_v[i - self.RD]))
        if self.c == 0 and i == 0:
            wA.append((s_evV, g["A_ready"]))
        if i == 0:
            wA.append((s_evA, self.bc_done))
        self.da_v[i] = P.emit(
            "scalar", "activation",
            dict(out=g["dAb"][ib][:], in_=dth[:, j * TC:(j + 1) * TC],
                 func=AF.Exp,
                 scale=g["A_neg"][:, (half * 8 + j) * NST + n:
                                  (half * 8 + j) * NST + n + 1]),
            wait=wA, inc=s_evA)
        wB = [(s_evV, self.vu[half][j])]
        if i == 0:
            wB.append((s_evA, self.bc_done))
        if i >= self.RD:
            wB.append((s_evV, self.scan_v[i - self.RD]))
        self.dbx_v[i] = P.emit(
            "vector", "tensor_tensor",
            dict(out=g["dBxb"][ib][:], in0=g["uh"][:, j * TC:(j + 1) * TC],
                 in1=g["Bv_all"][:, n * TC:(n + 1) * TC], op=ALU.mult),
            wait=wB, inc=s_evV)

    def _gate(self, half):
        P, g, c = self.P, self.g, self.c
        s_pe, s_evV, s_evP, s_sc = g["s_pe"], g["s_evV"], g["s_evP"], g["s_sc"]
        sz = g["sz_hbm"]
        xch, ug = g["xch"], g["ug"]
        szdma = {}
        for j in range(2):
            d = half * 8 + j
            szdma[j] = P.dma("scalar", g["szst"][j % 2][:],
                             sz[d * 128:(d + 1) * 128, self.cs], s_sc,
                             wait=[(s_sc, g["sz_store"][c])] +
                                  ([(s_evV, g["szst_war"])]
                                   if g.get("szst_war") else []))
        for j in range(8):
            d = half * 8 + j
            self.gate_v[half][j] = P.emit(
                "vector", "tensor_tensor",
                dict(out=ug[:, d * TC:(d + 1) * TC],
                     in0=xch[:, j * TC:(j + 1) * TC],
                     in1=g["szst"][j % 2][:], op=ALU.mult),
                wait=[(s_evP, self.yadd_v[half][j]), (s_sc, szdma[j])],
                inc=s_evV)
            if j + 2 < 8:
                szdma[j + 2] = P.dma("scalar", g["szst"][(j + 2) % 2][:],
                                     sz[(d + 2) * 128:(d + 3) * 128, self.cs],
                                     s_sc, wait=[(s_evV, self.gate_v[half][j])])
        g["szst_war"] = self.gate_v[half][7]
        g["gate_last"] = self.gate_v[half][7]

    def pump_n(self, k):
        while k > 0 and self.i < 256:
            i = self.i
            if i == 128 and not self.h1_prepped:
                self._tail(127)
                self._gate(0)
                self._prep_half(1)
                self.h1_prepped = True
            self._unit(i)
            if i not in (0, 128):
                self._tail(i - 1)
            self.i += 1
            k -= 1

    def make_pump(self, sites, share=1.0):
        per = 256.0 * share / max(sites, 1)

        def pump():
            self.carry += per
            n = int(self.carry)
            if n:
                self.carry -= n
                self.pump_n(n)
        return pump

    def finish(self):
        g = self.g
        self.pump_n(256)
        self._tail(255)
        self._gate(1)
        g["gate_last_c"][self.c] = g["gate_last"]
        g["xch_war"] = [(g["s_evV"], g["gate_last"])]



def build_program():
    nc = bass.Bass("TRN2", target_bir_lowering=False)
    x_T = nc.dram_tensor("x_T", [H, T], F32, kind="ExternalInput").ap()
    out_T = nc.dram_tensor("out_T", [H, T], F32, kind="ExternalOutput").ap()
    xres1 = nc.dram_tensor("xres1", [H, T], F32, kind="Internal").ap()
    xres2 = nc.dram_tensor("xres2", [H, T], F32, kind="Internal").ap()
    from contextlib import ExitStack
    with ExitStack() as stack:
        P = Prog(nc, stack)
        g = alloc_globals(P, nc)
        declare_hbm(P, g, nc)
        emit_prelude(P, g)
        x_r = x_T.rearrange("(kt p) t -> p kt t", p=128)
        out_r = out_T.rearrange("(kt p) t -> p kt t", p=128)
        xr1_r = xres1.rearrange("(kt p) t -> p kt t", p=128)
        xr2_r = xres2.rearrange("(kt p) t -> p kt t", p=128)
        S1, S5 = g["S1"], g["S5"]
        s_sc, s_st, s_evV, s_evA, s_pe = (g["s_sc"], g["s_st"], g["s_evV"],
                                          g["s_evA"], g["s_pe"])
        for it in range(NIT):
            emit_ring_loads(P, g, it)
            if 2 <= it <= NCH + 1:
                emit_outproj(P, g, it, xr1_r, xr2_r)
            ctxs = []
            if it <= NCH - 1:
                emit_ffn_prologue(P, g, S1, x_r, it)
                ctxs.append(S1)
            if 3 <= it:
                c5 = it - 3
                emit_ffn_prologue(P, g, S5, xr2_r, c5,
                                  extra_ld_wait=[(s_sc, g["xres2_store"][c5])],
                                  lo=0)
                ctxs.append(S5)
            scan = None
            if 1 <= it <= NCH:
                scan = Scan(P, g, it)
                scan.prep()
            pump = pump_dn = None
            if scan and INTERLEAVE_SCAN:
                nslots = len(ctxs) * 144
                pump = scan.make_pump(nslots, share=0.85)
                pump_dn = scan.make_pump(nslots, share=0.15)
            emit_expert_pass(P, g, ctxs, it, pump=pump, pump_dn=pump_dn)
            if scan:
                scan.finish()
            if it <= NCH - 1:
                g["xres1_store"][it] = P.dma(
                    "scalar", xr1_r[:, :, it * TC:(it + 1) * TC],
                    S1["xin"][:].rearrange("p (kt t) -> p kt t", kt=KTH),
                    s_sc, wait=[(s_evV, S1["vcomb_last"])])
            if 3 <= it:
                c5 = it - 3
                vo = P.dma("scalar", out_r[:, :, c5 * TC:(c5 + 1) * TC],
                           S5["xin"][:].rearrange("p (kt t) -> p kt t", kt=KTH),
                           s_st, wait=[(s_evV, S5["vcomb_last"])])
                S5["xin_war"] = [(s_st, vo), (s_evV, S5["ln"]["xn_ready"]),
                                 (s_evA, S5["ln"]["last_sq"]),
                                 (s_pe, S5["xn_pe_read"])]
            if it <= NCH - 1:
                emit_p2a(P, g, it)
                S1["xin_war"] = S1["xin_war"] + [(s_sc, g["xres1_store"][it])]
        P.wait("gpsimd", (s_st, s_st.n))
        P.wait("sync", (s_st, s_st.n))
        P.wait("scalar", (s_st, s_st.n))
        block = stack.enter_context(nc.Block())
        P.finalize(block)
    return nc


_NC_CACHE = []


def kernel(L0_attn_x, L0_attn_in_proj_w, L0_attn_conv_w, L0_attn_conv_b,
           L0_attn_x_proj_w, L0_attn_dt_proj_w, L0_attn_dt_proj_b, L0_attn_A_log,
           L0_attn_D, L0_attn_out_proj_w, L0_ffn_shared_gate, L0_ffn_shared_up,
           L0_ffn_shared_down, L0_ffn_gate_w, L0_ffn_up_w, L0_ffn_down_w,
           L0_ffn_router_w, L0_ffn_routing_bias):
    import ml_dtypes
    if not _NC_CACHE:
        _NC_CACHE.append(build_program())
    nc = _NC_CACHE[0]
    f32 = lambda a: np.ascontiguousarray(np.asarray(a, dtype=np.float32))

    def lay(vec):
        return np.ascontiguousarray(np.asarray(vec, np.float32).reshape(16, 128).T)

    WT = f32(L0_attn_in_proj_w).T
    base = {
        "f_su": f32(np.asarray(L0_ffn_shared_up).T),
        "f_sg": f32(np.asarray(L0_ffn_shared_gate).T),
        "f_sd": f32(np.asarray(L0_ffn_shared_down).T),
        "f_router": f32(np.asarray(L0_ffn_router_w).T),
        "c_bias": f32(L0_ffn_routing_bias).reshape(1, 8),
        "c_sel8": np.stack([(np.arange(8)[:, None] == e) * np.ones((8, 128), np.float32)
                            for e in range(8)], axis=1).reshape(8, 1024
                           ).astype(ml_dtypes.bfloat16),
        "c_ident": np.eye(128, dtype=np.float32),
        "c_ones": np.ones((128, 1), np.float32),
        "c_ones_r": np.ones((1, 128), np.float32),
        "f_winx": f32(WT[:, :2048]),
        "f_winz": f32(WT[:, 2048:]),
        "f_wout": f32(np.asarray(L0_attn_out_proj_w).T),
        "f_xp": f32(np.asarray(L0_attn_x_proj_w).T),
        "c_dtw": lay(np.asarray(L0_attn_dt_proj_w)[:, 0]),
        "c_dtb": lay(L0_attn_dt_proj_b),
        "c_Dd": lay(L0_attn_D),
        "c_convb": lay(L0_attn_conv_b),
        "c_convw": np.ascontiguousarray(
            np.asarray(L0_attn_conv_w, np.float32)[:, 0, :]
            .reshape(16, 128, 4).transpose(1, 0, 2).reshape(128, 64)),
        "c_alog": np.ascontiguousarray(
            np.asarray(L0_attn_A_log, np.float32)
            .reshape(16, 128, 16).transpose(1, 0, 2).reshape(128, 256)),
    }
    for e in range(NE):
        base[f"f_up{e}"] = f32(np.asarray(L0_ffn_up_w)[e])
        base[f"f_gate{e}"] = f32(np.asarray(L0_ffn_gate_w)[e])
        base[f"f_down{e}"] = f32(np.asarray(L0_ffn_down_w)[e])
    x = np.asarray(L0_attn_x, np.float32)
    in_maps = [dict(base, x_T=np.ascontiguousarray(x[b].T)) for b in range(8)]
    res = run_bass_kernel_spmd(nc, in_maps, core_ids=list(range(8)))
    out = np.stack([np.asarray(res.results[b]["out_T"]).T for b in range(8)])
    return out.astype(np.float32)


# version 12 (final)
# speedup vs baseline: 1.0199x; 1.0199x over previous
import sys
sys.path.insert(0, "/opt/trn_rl_repo")
import numpy as np
import concourse.bass as bass
import concourse.mybir as mybir
from concourse.bass_utils import run_bass_kernel_spmd



F32 = mybir.dt.float32
BF16 = mybir.dt.bfloat16
AF = mybir.ActivationFunctionType
ALU = mybir.AluOpType
AX = mybir.AxisListType

ENGINES = ("sync", "scalar", "vector", "tensor", "gpsimd")


class Sem:
    __slots__ = ("h", "n", "name")

    def __init__(self, h, name):
        self.h = h
        self.n = 0
        self.name = name


class Prog:
    def __init__(self, nc, stack):
        self.nc = nc
        self.stack = stack
        self.streams = {e: [] for e in ENGINES}
        self.ninstr = 0

    def sem(self, name):
        return Sem(self.stack.enter_context(self.nc.semaphore(name)), name)

    def sbuf(self, name, shape, dt):
        return self.stack.enter_context(self.nc.sbuf_tensor(name, shape, dt))

    def psum(self, name, shape, dt=F32):
        return self.stack.enter_context(self.nc.psum_tensor(name, shape, dt))

    def emit(self, eng, method, kwargs, wait=(), inc=None, inc_by=1):
        wait = [(s, v) for (s, v) in wait if s is not None and v is not None]
        tgt = None
        if inc is not None:
            inc.n += inc_by
            tgt = inc.n
        self.streams[eng].append((method, kwargs, wait, inc, inc_by))
        self.ninstr += 1 + len(wait)
        return tgt

    def dma(self, eng, out, in_, sem, wait=(), **kw):
        return self.emit(eng, "dma_start", dict(out=out, in_=in_, **kw),
                         wait=wait, inc=sem, inc_by=16)

    def wait(self, eng, *pairs):
        self.streams[eng].append(
            (None, None, [(s, v) for s, v in pairs if v is not None], None, 0))
        self.ninstr += len(pairs)

    def finalize(self, block):
        for eng in ENGINES:
            ops = self.streams[eng]
            if not ops:
                continue
            section = getattr(block, eng)

            def body(engine, _ops=ops):
                for method, kwargs, wait, inc, inc_by in _ops:
                    for s, v in wait:
                        engine.wait_ge(s.h, v)
                    if method is None:
                        continue
                    ins = getattr(engine, method)(**kwargs)
                    if inc is not None:
                        ins.then_inc(inc.h, inc_by)

            section(body)
        self.streams = {e: [] for e in ENGINES}


def mm_group(P, out, pairs, wait=(), inc=None):
    n = len(pairs)
    v = None
    for i, (l, r) in enumerate(pairs):
        v = P.emit("tensor", "matmul",
                   dict(out=out, lhsT=l, rhs=r, start=(i == 0), stop=(i == n - 1)),
                   wait=list(wait) if i == 0 else (),
                   inc=inc if i == n - 1 else None)
    return v


def rearr_kt(ap, p=128):
    return ap.rearrange("(kt p) o -> p kt o", p=p)


class PsumPool:
    def __init__(self):
        self.freed = {}

    def wait(self, t):
        f = self.freed.get(id(t))
        return [f] if f is not None else []

    def free(self, t, sem, val):
        self.freed[id(t)] = (sem, val)



H, T, DIN, NST, NE, II = 1024, 4096, 2048, 16, 8, 2048
KTH, KTI = 8, 16
TC = 512
NCH = 8
NIT = 11
INTERLEAVE_SCAN = True



def alloc_globals(P, nc):
    g = {"nc": nc}
    g["pp"] = PsumPool()
    for s in ("s_x", "s_w", "s_wfree", "s_pe", "s_evA", "s_evV", "s_st", "s_cast",
              "s_pre", "s_sc", "s_evP", "s_wr", "s_wxp",
              "s_w0", "s_w1", "s_w2"):
        g[s] = P.sem(s)
    g["PS"] = [P.psum(f"PS{i}", [128, 512]) for i in range(8)]
    g["slots"] = [P.sbuf(f"slot{i}", [128, 4096], BF16) for i in range(3)]
    g["hc"] = 0
    g["ring"] = {}
    g["p45"] = 0
    g["p67"] = 0
    g["wb"] = P.sbuf("wb", [128, 512], F32)
    for nm in ("S1", "S5"):
        g[nm] = {
            "name": nm,
            "xin": P.sbuf(nm + "_xin", [128, 4096], F32),
            "xn": P.sbuf(nm + "_xn", [128, 4096], BF16),
            "pgu": P.sbuf(nm + "_pgu", [128, 4096], BF16),
            "wfm": P.sbuf(nm + "_wfm", [8, 512], BF16),
            "xin_war": [],
            "xn_pe_read": None,
            "wfm_pe_read": None,
            "vcomb_last": None,
            "tD_last": None,
        }
    g["sq32"] = [P.sbuf(f"sq32_{i}", [128, 512], F32) for i in range(2)]
    g["t32"] = [P.sbuf(f"t32_{i}", [128, 512], F32) for i in range(2)]
    g["rows"] = [P.sbuf(f"row{i}", [1, 512], F32) for i in range(3)]
    g["sq_war"] = {0: None, 1: None}
    g["t32_act_read"] = {0: None, 1: None}
    g["sel8"] = P.sbuf("sel8", [8, 1024], BF16)
    g["lg32"] = P.sbuf("lg32", [128, 32], F32)
    g["wr16"] = P.sbuf("wr16", [128, 64], BF16)
    g["bias_b32"] = P.sbuf("bias_b32", [128, 8], F32)
    sm = {}
    for n_, w in (("m12", 2), ("msk", 8), ("msked", 8), ("e12", 2), ("d", 1),
                  ("r", 1), ("t1", 1), ("w1", 1), ("w2", 1), ("w8", 8)):
        sm[n_] = P.sbuf("sm_" + n_, [128, w], F32)
    g["sm"] = sm
    g["ident32"] = P.sbuf("ident32", [128, 128], F32)
    g["ones_col32"] = P.sbuf("ones_col32", [128, 1], F32)
    g["ones_row32"] = P.sbuf("ones_row32", [1, 128], F32)
    g["ones16"] = P.sbuf("ones16", [1, 128], BF16)
    g["bias_row"] = P.sbuf("bias_row", [1, 8], F32)
    g["eps_col"] = P.sbuf("eps_col", [128, 1], F32)
    g["A_neg"] = P.sbuf("A_neg", [128, 256], F32)
    g["dtw32"] = P.sbuf("dtw32", [128, 16], F32)
    g["dtb32"] = P.sbuf("dtb32", [128, 16], F32)
    g["Dd32"] = P.sbuf("Dd32", [128, 16], F32)
    g["convw32"] = P.sbuf("convw32", [128, 64], F32)
    g["convb32"] = P.sbuf("convb32", [128, 16], F32)
    g["wxp16"] = P.sbuf("wxp16", [128, 16 * 33], BF16)
    g["xssm"] = [P.sbuf("xssm0", [128, 515], F32)]
    g["carry32"] = P.sbuf("carry32", [128, 48], F32)
    g["zs16"] = [P.sbuf("zs16_0", [128, 512], BF16)]
    g["xc16s"] = [P.sbuf("xc16s_0", [128, 512], BF16)]
    g["xc32s"] = [P.sbuf("xc32s_0", [128, 512], F32)]
    g["xps16"] = P.sbuf("xps16", [33, 512], BF16)
    g["xch"] = P.sbuf("xch", [128, 4096], F32)
    g["uh"] = P.sbuf("uh", [128, 4096], BF16)
    g["ug"] = P.sbuf("ug", [128, 8192], BF16)
    g["Bv_all"] = P.sbuf("Bv_all", [128, 16 * 512], BF16)
    g["Cv_all"] = P.sbuf("Cv_all", [128, 16 * 512], BF16)
    g["dtr16"] = P.sbuf("dtr16", [128, 512], BF16)
    g["dAb"] = [P.sbuf(f"dAb{i}", [128, 512], BF16) for i in range(2)]
    g["dBxb"] = [P.sbuf(f"dBxb{i}", [128, 512], BF16) for i in range(2)]
    g["hb"] = [P.sbuf(f"hb{i}", [128, 512], BF16) for i in range(2)]
    g["yst"] = [P.sbuf(f"yst{i}", [128, 512], BF16) for i in range(2)]
    g["szst"] = [P.sbuf(f"szst{i}", [128, 512], BF16) for i in range(2)]
    g["hstate"] = P.sbuf("hstate", [128, 256], BF16)
    g["xrow"] = [P.sbuf(f"xrow{i}", [1, 512], BF16) for i in range(2)]
    g["xrow2"] = [P.sbuf(f"xrow2_{i}", [1, 512], BF16) for i in range(2)]
    g["gate_last_c"] = {}
    g["ug_pe_read"] = None
    g["xc_store"] = {}
    g["sz_store"] = {}
    g["xp_store"] = {}
    g["xres1_store"] = {}
    g["xres2_store"] = {}
    g["cast_val"] = {}
    g["dth_act_read"] = None
    g["scan_dve_last"] = None
    g["gate_last"] = None
    return g


def declare_hbm(P, g, nc):
    d, b = {}, {}
    shapes = {"su": (H, II), "sg": (H, II), "sd": (II, H),
              "winx": (H, DIN), "winz": (H, DIN), "wout": (DIN, H),
              "xp": (DIN, 33)}
    for e in range(NE):
        shapes[f"up{e}"] = (H, II)
        shapes[f"gate{e}"] = (H, II)
        shapes[f"down{e}"] = (II, H)
    shapes["router"] = (H, 8)
    for name, shp in shapes.items():
        d[name] = nc.dram_tensor("f_" + name, list(shp), F32, kind="ExternalInput").ap()
        b[name] = nc.dram_tensor("b_" + name, list(shp), BF16, kind="Internal").ap()
    g["f_in"], g["wbf"] = d, b
    cn = {}
    for name, shp in (("dtw", (128, 16)), ("dtb", (128, 16)), ("Dd", (128, 16)),
                      ("convw", (128, 64)), ("convb", (128, 16)), ("alog", (128, 256)),
                      ("bias", (1, 8)), ("ident", (128, 128)), ("ones", (128, 1)),
                      ("ones_r", (1, 128))):
        cn[name] = nc.dram_tensor("c_" + name, list(shp), F32, kind="ExternalInput").ap()
    cn["sel8"] = nc.dram_tensor("c_sel8", [8, 1024], BF16, kind="ExternalInput").ap()
    g["c_in"] = cn
    g["xc_hbm"] = nc.dram_tensor("xc_hbm", [2048, 4096], F32, kind="Internal").ap()
    g["sz_hbm"] = nc.dram_tensor("sz_hbm", [2048, 4096], BF16, kind="Internal").ap()
    g["xp_hbm"] = nc.dram_tensor("xp_hbm", [33, 4096], BF16, kind="Internal").ap()


def emit_prelude(P, g):
    nc = g["nc"]
    s_pre, s_evV, s_evA, s_cast, s_pe = (g["s_pre"], g["s_evV"], g["s_evA"],
                                         g["s_cast"], g["s_pe"])
    cn = g["c_in"]
    P.dma("sync", g["ident32"][:], cn["ident"], s_pre)
    P.dma("sync", g["ones_col32"][:], cn["ones"], s_pre)
    P.dma("sync", g["ones_row32"][:], cn["ones_r"], s_pre)
    P.dma("sync", g["bias_row"][:], cn["bias"], s_pre)
    P.dma("sync", g["sel8"][:], cn["sel8"], s_pre)
    for name, dst in (("dtw", "dtw32"), ("dtb", "dtb32"), ("Dd", "Dd32"),
                      ("convw", "convw32"), ("convb", "convb32")):
        P.dma("sync", g[dst][:], cn[name], s_pre)
    v_alog = P.dma("sync", g["A_neg"][:], cn["alog"], s_pre)
    pre_consts = s_pre.n
    P.emit("vector", "memset", dict(ap=g["eps_col"][:], constant=1e-5), inc=s_evV)
    P.emit("vector", "memset", dict(ap=g["ones16"][:], constant=1.0), inc=s_evV)
    g["h_ready"] = P.emit("vector", "memset", dict(ap=g["hstate"][:], constant=0.0),
                          inc=s_evV)
    g["carry_ready"] = P.emit("vector", "memset", dict(ap=g["carry32"][:], constant=0.0),
                              inc=s_evV)
    va = P.emit("scalar", "activation",
                dict(out=g["A_neg"][:], in_=g["A_neg"][:], func=AF.Exp),
                wait=[(s_pre, v_alog)], inc=s_evA)
    g["A_ready"] = P.emit("vector", "tensor_scalar",
                          dict(out=g["A_neg"][:], in0=g["A_neg"][:], scalar1=-1.0,
                               scalar2=None, op0=ALU.mult),
                          wait=[(s_evA, va)], inc=s_evV)
    cast_order = ["router", "su", "sg", "sd"]
    for e in range(NE):
        cast_order += [f"up{e}", f"gate{e}", f"down{e}"]
    cast_order += ["winx", "winz", "wout", "xp"]
    prev_cast = 0
    for name in cast_order:
        g["cast_val"][name] = P.dma("gpsimd", g["wbf"][name], g["f_in"][name],
                                    s_cast,
                                    wait=([(s_cast, prev_cast)] if prev_cast else []))
        prev_cast = g["cast_val"][name]
    g["wr_ready"] = P.dma("sync",
                          g["wr16"][:].rearrange("p (kt e) -> p kt e", kt=KTH),
                          g["wbf"]["router"].rearrange("(kt p) e -> p kt e", p=128),
                          g["s_wr"], wait=[(s_cast, g["cast_val"]["router"])])
    g["wxp_ready"] = P.dma("sync",
                           g["wxp16"][:].rearrange("p (kt o) -> p kt o", kt=KTI),
                           g["wbf"]["xp"].rearrange("(kt p) o -> p kt o", p=128),
                           g["s_wxp"], wait=[(s_cast, g["cast_val"]["xp"])])
    t = P.emit("tensor", "matmul",
               dict(out=g["PS"][0][:, 0:8], lhsT=g["ones_row32"][0:1, :],
                    rhs=g["bias_row"][0:1, :], start=True, stop=True),
               wait=[(s_pre, pre_consts)], inc=s_pe)
    v = P.emit("vector", "tensor_copy", dict(out=g["bias_b32"][:], in_=g["PS"][0][:, 0:8]),
               wait=[(s_pe, t)], inc=s_evV)
    g["pp"].free(g["PS"][0], s_evV, v)
    P.wait("tensor", (g["s_wr"], g["wr_ready"]))



def ps45(g):
    g["p45"] = 1 - g["p45"]
    return g["PS"][4 + g["p45"]]


def ps67(g):
    g["p67"] = 1 - g["p67"]
    return g["PS"][6 + g["p67"]]


NSLOT = 3


def iter_ring_tags(it):
    tags = []
    if 2 <= it <= NCH + 1:
        tags += [("wout", q) for q in range(4)]
    for e in range(-1, NE):
        nU, nG, nD = (("su", "sg", "sd") if e < 0 else
                      (f"up{e}", f"gate{e}", f"down{e}"))
        for h in (0, 1):
            tags += [(nU, 2 * h), (nG, 2 * h), (nU, 2 * h + 1), (nG, 2 * h + 1),
                     (nD, 2 * h), (nD, 2 * h + 1)]
    if it <= NCH - 1:
        for q in range(4):
            tags += [("winx", q), ("winz", q)]
    return tags


def emit_ring_loads(P, g, it):
    s_wfree, s_cast = g["s_wfree"], g["s_cast"]
    subs = [g["s_w0"], g["s_w1"], g["s_w2"]]
    for (name, q) in iter_ring_tags(it):
        hbm = g["wbf"][name]
        if hbm.shape[0] == 1024:
            src = rearr_kt(hbm)[:, :, q * 512:(q + 1) * 512]
            kt = 8
        else:
            src = rearr_kt(hbm)[:, q * 4:(q + 1) * 4, :]
            kt = 4
        m = g["hc"]
        g["hc"] += 1
        sem = subs[m % NSLOT]
        waits = ([(s_wfree, m - (NSLOT - 1))] if m >= NSLOT else [])
        if m >= NSLOT:
            waits.append((sem, 16 * (m // NSLOT)))
        waits.append((s_cast, g["cast_val"][name]))
        slot = g["slots"][m % NSLOT]
        v = P.dma("sync", slot[:].rearrange("p (kt o) -> p kt o", kt=kt), src, sem,
                  wait=waits)
        g["ring"][(it, name, q)] = (slot, (sem, v))


def ring_free(P, g, n=1):
    P.wait("tensor", (g["s_pe"], g["s_pe"].n))
    for _ in range(n):
        P.emit("tensor", "nop", {}, inc=g["s_wfree"])



def emit_ln(P, g, *, xsl, xn_dst, xw, pe_fence, lo=6):
    pp = g["pp"]
    s_pe, s_evA, s_evV = g["s_pe"], g["s_evA"], g["s_evV"]
    PS, sq32, t32, rows = g["PS"], g["sq32"], g["t32"], g["rows"]
    pairs = [(g["ones_col32"][:, 0:1], xsl(kt)) for kt in range(KTH)]
    mm_group(P, PS[lo][0:1, :], pairs, wait=list(xw) + pp.wait(PS[lo]))
    sq_done = []
    last_sq = None
    for kt in range(KTH):
        w = list(xw) + ([(s_pe, pe_fence)] if kt == 0 else [])
        if kt == 0:
            w += [(s_evV, g["sq_war"][0]), (s_evV, g["sq_war"][1])]
        if kt >= 2:
            w.append((s_pe, sq_done[kt - 2]))
        va = P.emit("scalar", "activation",
                    dict(out=sq32[kt % 2][:], in_=xsl(kt), func=AF.Square),
                    wait=w, inc=s_evA)
        last_sq = va
        t = P.emit("tensor", "matmul",
                   dict(out=PS[lo + 1][0:1, :], lhsT=g["ones_col32"][:, 0:1],
                        rhs=sq32[kt % 2][:], start=(kt == 0), stop=(kt == KTH - 1)),
                   wait=[(s_evA, va)] + (pp.wait(PS[lo + 1]) if kt == 0 else []),
                   inc=s_pe)
        sq_done.append(t)
    vm = P.emit("vector", "tensor_scalar",
                dict(out=rows[0][0:1, :], in0=PS[lo][0:1, :], scalar1=1.0 / 1024,
                     scalar2=None, op0=ALU.mult),
                wait=[(s_pe, sq_done[-1])], inc=s_evV)
    vms = P.emit("vector", "tensor_scalar",
                 dict(out=rows[1][0:1, :], in0=PS[lo + 1][0:1, :], scalar1=1.0 / 1024,
                      scalar2=None, op0=ALU.mult), inc=s_evV)
    pp.free(PS[lo], s_evV, vms)
    pp.free(PS[lo + 1], s_evV, vms)
    v2 = P.emit("vector", "tensor_tensor",
                dict(out=rows[2][0:1, :], in0=rows[0][0:1, :], in1=rows[0][0:1, :],
                     op=ALU.mult),
                wait=[(s_evV, vm)], inc=s_evV)
    vvar = P.emit("vector", "tensor_tensor",
                  dict(out=rows[2][0:1, :], in0=rows[1][0:1, :], in1=rows[2][0:1, :],
                       op=ALU.subtract), wait=[(s_evV, v2)], inc=s_evV)
    vln = P.emit("scalar", "activation",
                 dict(out=rows[2][0:1, :], in_=rows[2][0:1, :], func=AF.Ln,
                      bias=g["eps_col"][0:1, 0:1]),
                 wait=[(s_evV, vvar)], inc=s_evA)
    vexp = P.emit("scalar", "activation",
                  dict(out=rows[1][0:1, :], in_=rows[2][0:1, :], func=AF.Exp, scale=-0.5),
                  wait=[(s_evA, vln)], inc=s_evA)
    va = P.emit("vector", "scalar_tensor_tensor",
                dict(out=rows[2][0:1, :], in0=rows[0][0:1, :], scalar=-1.0,
                     in1=rows[1][0:1, :], op0=ALU.mult, op1=ALU.mult),
                wait=[(s_evA, vexp)], inc=s_evV)
    P.emit("tensor", "matmul",
           dict(out=PS[lo][:], lhsT=g["ones_row32"][0:1, :], rhs=rows[1][0:1, :],
                start=True, stop=True),
           wait=[(s_evV, va)] + pp.wait(PS[lo]))
    tb = P.emit("tensor", "matmul",
                dict(out=PS[lo + 1][:], lhsT=g["ones_row32"][0:1, :], rhs=rows[2][0:1, :],
                     start=True, stop=True),
                wait=pp.wait(PS[lo + 1]), inc=s_pe)
    xn_ready = None
    for kt in range(KTH):
        vt = P.emit("vector", "tensor_tensor",
                    dict(out=t32[kt % 2][:], in0=xsl(kt), in1=PS[lo][:], op=ALU.mult),
                    wait=([(s_pe, tb)] if kt == 0 else []) +
                         ([(s_evA, g["t32_act_read"][kt % 2])] if kt < 2 else []),
                    inc=s_evV)
        xn_ready = P.emit("vector", "tensor_tensor",
                          dict(out=xn_dst(kt), in0=t32[kt % 2][:], in1=PS[lo + 1][:],
                               op=ALU.add),
                          wait=[(s_evV, vt)], inc=s_evV)
    g["pp"].free(PS[lo], s_evV, xn_ready)
    g["pp"].free(PS[lo + 1], s_evV, xn_ready)
    return dict(xn_ready=xn_ready, last_sq=last_sq, last_pe=sq_done[-1])



def emit_router(P, g, S, xn_ready, lo=6):
    pp = g["pp"]
    s_pe, s_evA, s_evV = g["s_pe"], g["s_evA"], g["s_evV"]
    PS, sm, lg32 = g["PS"], g["sm"], g["lg32"]
    xn = S["xn"]
    v = None
    for tt in range(TC // 128):
        pt = PS[lo + (tt % 2)]
        pairs = [(xn[:, kt * TC + tt * 128: kt * TC + tt * 128 + 128],
                  g["wr16"][:, kt * 8:(kt + 1) * 8]) for kt in range(KTH)]
        tl = mm_group(P, pt[:, 0:8], pairs,
                      wait=([(s_evV, xn_ready), (g["s_wr"], g["wr_ready"])]
                            if tt == 0 else []) + pp.wait(pt),
                      inc=s_pe)
        lsl = lg32[:, tt * 8:(tt + 1) * 8]
        v = P.emit("vector", "tensor_tensor",
                   dict(out=lsl, in0=pt[:, 0:8], in1=g["bias_b32"][:, 0:8], op=ALU.add),
                   wait=[(s_pe, tl)], inc=s_evV)
        pp.free(pt, s_evV, v)
        m12, msk, w8 = sm["m12"], sm["msk"], sm["w8"]
        vlsl = v
        v = P.emit("vector", "tensor_reduce",
                   dict(out=m12[:, 0:1], in_=lsl, axis=AX.X, op=ALU.max),
                   wait=[(s_evV, vlsl)], inc=s_evV)
        vmsk1 = P.emit("vector", "tensor_scalar",
                       dict(out=msk[:], in0=lsl, scalar1=m12[:, 0:1], scalar2=None,
                            op0=ALU.is_equal), wait=[(s_evV, v)], inc=s_evV)
        vq = P.emit("vector", "scalar_tensor_tensor",
                    dict(out=sm["msked"][:], in0=msk[:], scalar=-1e30, in1=lsl,
                         op0=ALU.mult, op1=ALU.add), wait=[(s_evV, vmsk1)], inc=s_evV)
        v = P.emit("vector", "tensor_reduce",
                   dict(out=m12[:, 1:2], in_=sm["msked"][:], axis=AX.X, op=ALU.max),
                   wait=[(s_evV, vq)], inc=s_evV)
        v = P.emit("scalar", "activation",
                   dict(out=sm["e12"][:], in_=m12[:], func=AF.Exp, scale=-1.0),
                   wait=[(s_evV, v)], inc=s_evA)
        v = P.emit("vector", "tensor_tensor",
                   dict(out=sm["d"][:], in0=sm["e12"][:, 0:1], in1=sm["e12"][:, 1:2],
                        op=ALU.add), wait=[(s_evA, v)], inc=s_evV)
        v = P.emit("vector", "tensor_scalar",
                   dict(out=sm["d"][:], in0=sm["d"][:], scalar1=2.0, scalar2=None,
                        op0=ALU.add), wait=[(s_evV, v)], inc=s_evV)
        v = P.emit("vector", "reciprocal", dict(out=sm["r"][:], in_=sm["d"][:]),
                   wait=[(s_evV, v)], inc=s_evV)
        v = P.emit("vector", "tensor_scalar",
                   dict(out=sm["r"][:], in0=sm["r"][:], scalar1=0.5, scalar2=None,
                        op0=ALU.mult), wait=[(s_evV, v)], inc=s_evV)
        vt1 = P.emit("vector", "tensor_scalar",
                     dict(out=sm["t1"][:], in0=sm["e12"][:, 1:2], scalar1=1.0,
                          scalar2=None, op0=ALU.add), wait=[(s_evV, v)], inc=s_evV)
        vw1 = P.emit("vector", "tensor_tensor",
                     dict(out=sm["w1"][:], in0=sm["t1"][:], in1=sm["r"][:], op=ALU.mult),
                     wait=[(s_evV, vt1)], inc=s_evV)
        vt1 = P.emit("vector", "tensor_scalar",
                     dict(out=sm["t1"][:], in0=sm["e12"][:, 0:1], scalar1=1.0,
                          scalar2=None, op0=ALU.add), wait=[(s_evV, vw1)], inc=s_evV)
        v = P.emit("vector", "tensor_tensor",
                   dict(out=sm["w2"][:], in0=sm["t1"][:], in1=sm["r"][:], op=ALU.mult),
                   wait=[(s_evV, vt1)], inc=s_evV)
        vw8 = P.emit("vector", "tensor_scalar",
                     dict(out=w8[:], in0=msk[:], scalar1=sm["w1"][:, 0:1], scalar2=None,
                          op0=ALU.mult), wait=[(s_evV, v), (s_evV, vmsk1)], inc=s_evV)
        vmsk2 = P.emit("vector", "tensor_scalar",
                       dict(out=msk[:], in0=lsl, scalar1=m12[:, 1:2], scalar2=None,
                            op0=ALU.is_equal), wait=[(s_evV, vw8)], inc=s_evV)
        v = P.emit("vector", "scalar_tensor_tensor",
                   dict(out=w8[:], in0=msk[:], scalar=sm["w2"][:, 0:1], in1=w8[:],
                        op0=ALU.mult, op1=ALU.add), wait=[(s_evV, vmsk2)], inc=s_evV)
        ptp = PS[lo + ((tt + 1) % 2)]
        t2 = P.emit("tensor", "transpose",
                    dict(out=ptp[0:8, 0:128], in_=w8[:], identity=g["ident32"][:]),
                    wait=[(s_evV, v)] + pp.wait(ptp), inc=s_pe)
        v = P.emit("vector", "tensor_copy",
                   dict(out=S["wfm"][:, tt * 128:(tt + 1) * 128], in_=ptp[0:8, 0:128]),
                   wait=[(s_pe, t2)] +
                        ([(s_pe, S["wfm_pe_read"])] if tt == 0 else []),
                   inc=s_evV)
        pp.free(ptp, s_evV, v)
    return v



def emit_ffn_prologue(P, g, S, src_r, c, extra_ld_wait=(), lo=6):
    s_x, s_evV = g["s_x"], g["s_evV"]
    vld = P.dma("scalar",
                S["xin"][:].rearrange("p (kt t) -> p kt t", kt=KTH),
                src_r[:, :, c * TC:(c + 1) * TC], s_x,
                wait=list(S["xin_war"]) + list(extra_ld_wait))
    pe_fence = g["s_pe"].n
    xw = [(s_x, vld)]

    def xsl(kt):
        return S["xin"][:, kt * TC:(kt + 1) * TC]
    ln = emit_ln(P, g, xsl=xsl,
                 xn_dst=lambda kt: S["xn"][:, kt * TC:(kt + 1) * TC],
                 xw=xw, pe_fence=pe_fence, lo=lo)
    wfm_v = emit_router(P, g, S, ln["xn_ready"], lo=lo)
    S["ln"] = ln
    S["wfm_v"] = wfm_v
    return ln, wfm_v



def emit_expert_pass(P, g, ctxs, it, pump=None, pump_dn=None):
    if pump is None:
        pump = lambda: None
    if pump_dn is None:
        pump_dn = pump
    pp = g["pp"]
    s_pe, s_evA, s_evV, s_w = g["s_pe"], g["s_evA"], g["s_evV"], g["s_w"]
    PS, sq32, t32 = g["PS"], g["sq32"], g["t32"]
    for e in range(-1, NE):
        nU, nG, nD = (("su", "sg", "sd") if e < 0 else
                      (f"up{e}", f"gate{e}", f"down{e}"))
        wb_ps = {}
        for S in ctxs:
            if e >= 0:
                pw = ps67(g)
                twb = P.emit("tensor", "matmul",
                             dict(out=pw[:], lhsT=g["sel8"][:, e * 128:(e + 1) * 128],
                                  rhs=S["wfm"][0:8, 0:512], start=True, stop=True),
                             wait=pp.wait(pw) +
                                  ([(s_evV, S["wfm_v"])] if e == 0 else []),
                             inc=s_pe)
                S["wfm_pe_read"] = twb
                wb_ps[id(S)] = (pw, twb)
        for half in (0, 1):
            slotUa, vUa = g["ring"][(it, nU, 2 * half)]
            slotGa, vGa = g["ring"][(it, nG, 2 * half)]
            slotUb, vUb = g["ring"][(it, nU, 2 * half + 1)]
            slotGb, vGb = g["ring"][(it, nG, 2 * half + 1)]
            slotDa, vDa = g["ring"][(it, nD, 2 * half)]
            slotDb, vDb = g["ring"][(it, nD, 2 * half + 1)]
            for qh in (0, 1):
                sU, sG = (slotUa, slotGa) if qh == 0 else (slotUb, slotGb)
                vU, vG = (vUa, vGa) if qh == 0 else (vUb, vGb)
                for S in ctxs:
                    for otl in range(4):
                        ot = qh * 4 + otl
                        pu, pg = PS[0 + (ot % 2)], PS[2 + (ot % 2)]
                        pairs_u = [(sU[:, kt * 512 + otl * 128: kt * 512 + otl * 128 + 128],
                                    S["xn"][:, kt * TC:(kt + 1) * TC]) for kt in range(KTH)]
                        tU = mm_group(P, pu[:], pairs_u,
                                      wait=pp.wait(pu) +
                                           ([vU] if otl == 0 else []),
                                      inc=s_pe)
                        pairs_g = [(sG[:, kt * 512 + otl * 128: kt * 512 + otl * 128 + 128],
                                    S["xn"][:, kt * TC:(kt + 1) * TC]) for kt in range(KTH)]
                        tG = mm_group(P, pg[:], pairs_g,
                                      wait=pp.wait(pg) +
                                           ([vG] if otl == 0 else []),
                                      inc=s_pe)
                        S["xn_pe_read"] = tG
                        vg = P.emit("scalar", "activation",
                                    dict(out=sq32[ot % 2][:], in_=pg[:], func=AF.Silu),
                                    wait=[(s_pe, tG), (s_evV, g["sq_war"][ot % 2])],
                                    inc=s_evA)
                        pp.free(pg, s_evA, vg)
                        wvp = [(s_evA, vg), (s_pe, tU)]
                        if ot == 0:
                            wvp.append((s_pe, S["tD_last"]))
                        vp = P.emit("vector", "tensor_tensor",
                                    dict(out=S["pgu"][:, ot * TC:(ot + 1) * TC],
                                         in0=sq32[ot % 2][:], in1=pu[:], op=ALU.mult),
                                    wait=wvp, inc=s_evV)
                        pp.free(pu, s_evV, vp)
                        g["sq_war"][ot % 2] = vp
                        S["vp_last"] = vp
                        pump()
                ring_free(P, g, n=2)
            for S in ctxs:
                if e >= 0:
                    pw, twb = wb_ps[id(S)]
                    vwb = P.emit("vector", "tensor_copy",
                                 dict(out=g["wb"][:], in_=pw[:]),
                                 wait=[(s_pe, twb)], inc=s_evV)
                    if half == 1:
                        pp.free(pw, s_evV, vwb)
                for oh in range(8):
                    pd = ps45(g)
                    pairs_d = ([(slotDa[:, kt * 1024 + oh * 128: kt * 1024 + oh * 128 + 128],
                                 S["pgu"][:, kt * TC:(kt + 1) * TC]) for kt in range(4)] +
                               [(slotDb[:, kt * 1024 + oh * 128: kt * 1024 + oh * 128 + 128],
                                 S["pgu"][:, (4 + kt) * TC:(5 + kt) * TC]) for kt in range(4)])
                    tD = mm_group(P, pd[:], pairs_d,
                                  wait=pp.wait(pd) + [(s_evV, S["vp_last"])] +
                                       ([vDa, vDb] if oh == 0 else []),
                                  inc=s_pe)
                    S["tD_last"] = tD
                    if e < 0:
                        vd = P.emit("vector", "tensor_scalar",
                                    dict(out=t32[oh % 2][:], in0=pd[:], scalar1=0.5,
                                         scalar2=None, op0=ALU.mult),
                                    wait=[(s_pe, tD),
                                          (s_evA, g["t32_act_read"][oh % 2])],
                                    inc=s_evV)
                    else:
                        vd = P.emit("vector", "tensor_tensor",
                                    dict(out=t32[oh % 2][:], in0=pd[:], in1=g["wb"][:],
                                         op=ALU.mult),
                                    wait=[(s_pe, tD),
                                          (s_evA, g["t32_act_read"][oh % 2])],
                                    inc=s_evV)
                    vcomb = P.emit("vector", "tensor_tensor",
                                   dict(out=S["xin"][:, oh * TC:(oh + 1) * TC],
                                        in0=S["xin"][:, oh * TC:(oh + 1) * TC],
                                        in1=t32[oh % 2][:], op=ALU.add),
                                   wait=[(s_evV, vd)], inc=s_evV)
                    pp.free(pd, s_evV, vcomb)
                    S["vcomb_last"] = vcomb
                    pump_dn()
            ring_free(P, g, n=2)



def emit_p2a(P, g, it):
    c = it
    pp = g["pp"]
    s_pe, s_evA, s_evV, s_sc, s_w = (g["s_pe"], g["s_evA"], g["s_evV"], g["s_sc"],
                                     g["s_w"])
    PS = g["PS"]
    S1 = g["S1"]
    xn = S1["xn"]
    xssm, carry, t32 = g["xssm"], g["carry32"], g["t32"]
    xc_r = g["xc_hbm"].rearrange("(kt p) t -> p kt t", p=128)
    sz = g["sz_hbm"]
    cs = slice(c * TC, (c + 1) * TC)
    pe_fence = s_pe.n
    xw = [(s_evV, S1["vcomb_last"]), (s_pe, S1["xn_pe_read"])]
    ln2 = emit_ln(P, g, xsl=lambda kt: S1["xin"][:, kt * TC:(kt + 1) * TC],
                  xn_dst=lambda kt: xn[:, kt * TC:(kt + 1) * TC],
                  xw=xw, pe_fence=pe_fence)
    wxq = [g["ring"][(it, "winx", q)] for q in range(4)]
    wzq = [g["ring"][(it, "winz", q)] for q in range(4)]
    zdma_v = g.setdefault("zdma_v", [None])
    xssm_rd = g.setdefault("xssm_rd", [None])
    silu_v = g.setdefault("silu_v", [None])
    xc16_v = g.setdefault("xc16_v", [None])
    xc32_rd = g.setdefault("xc32_rd", [None])
    xcdma_v = g.setdefault("xcdma_v", [None])
    xp_stop = None
    for i in range(KTI):
        slot, vw = wxq[i // 4]
        otl = i % 4
        px = PS[0 + (i % 2)]
        pairs = [(slot[:, kt * 512 + otl * 128: kt * 512 + otl * 128 + 128],
                  xn[:, kt * TC:(kt + 1) * TC]) for kt in range(KTH)]
        tX = mm_group(P, px[:], pairs,
                      wait=pp.wait(px) + ([vw] if otl == 0 else []) +
                           ([(s_evV, ln2["xn_ready"])] if i == 0 else []),
                      inc=s_pe)
        xt = g["xssm"][0]
        wA = [(s_pe, tX)]
        if xssm_rd[0] is not None:
            wA.append((s_evV, xssm_rd[0]))
        vev = P.emit("scalar", "copy", dict(out=xt[:, 3:515], in_=px[:]),
                     wait=wA, inc=s_evA)
        pp.free(px, s_evA, vev)
        vcr = P.emit("vector", "tensor_copy",
                     dict(out=xt[:, 0:3], in_=carry[:, i * 3:i * 3 + 3]),
                     wait=([(s_evV, g["carry_ready"])] if (c == 0 and i < 3) else []) +
                          [(s_evA, vev)],
                     inc=s_evV)
        wT = [(s_evV, vcr)]
        if silu_v[0] is not None:
            wT.append((s_evA, silu_v[0]))
        vt = P.emit("vector", "tensor_scalar",
                    dict(out=t32[i % 2][:], in0=xt[:, 0:512],
                         scalar1=g["convw32"][:, 4 * i:4 * i + 1], scalar2=None,
                         op0=ALU.mult),
                    wait=wT + ([(s_evA, g["t32_act_read"][i % 2])] if i < 2 else []),
                    inc=s_evV)
        for k in range(1, 4):
            vt = P.emit("vector", "scalar_tensor_tensor",
                        dict(out=t32[i % 2][:], in0=xt[:, k:k + 512],
                             scalar=g["convw32"][:, 4 * i + k:4 * i + k + 1],
                             in1=t32[i % 2][:], op0=ALU.mult, op1=ALU.add),
                        wait=[(s_evV, vt)], inc=s_evV)
        vco = P.emit("vector", "tensor_copy",
                     dict(out=carry[:, i * 3:i * 3 + 3], in_=xt[:, 512:515]),
                     inc=s_evV)
        xssm_rd[0] = vco
        wS = [(s_evV, vt)]
        if xcdma_v[0] is not None:
            wS.append((s_sc, xcdma_v[0]))
        if xc32_rd[0] is not None:
            wS.append((s_evV, xc32_rd[0]))
        vs = P.emit("scalar", "activation",
                    dict(out=g["xc32s"][0][:], in_=t32[i % 2][:],
                         func=AF.Silu, bias=g["convb32"][:, i:i + 1]),
                    wait=wS, inc=s_evA)
        silu_v[0] = vs
        g["t32_act_read"][i % 2] = vs
        xcdma_v[0] = P.dma("scalar", xc_r[:, i, cs], g["xc32s"][0][:],
                           s_sc, wait=[(s_evA, vs)])
        vxc = P.emit("vector", "tensor_copy",
                     dict(out=g["xc16s"][0][:], in_=g["xc32s"][0][:]),
                     wait=[(s_evA, vs)] +
                          ([(s_pe, xc16_v[0])] if xc16_v[0] else []),
                     inc=s_evV)
        xc32_rd[0] = vxc
        xp_stop = P.emit("tensor", "matmul",
                         dict(out=PS[6][0:33, 0:512],
                              lhsT=g["wxp16"][:, i * 33:(i + 1) * 33],
                              rhs=g["xc16s"][0][:],
                              start=(i == 0), stop=(i == KTI - 1),
                              skip_group_check=True),
                         wait=[(s_evV, vxc)] +
                              ([(g["s_wxp"], g["wxp_ready"])] if i == 0 else []) +
                              (pp.wait(PS[6]) if i == 0 else []),
                         inc=s_pe)
        xc16_v[0] = xp_stop
        slotz, vwz = wzq[i // 4]
        pz = PS[2 + (i % 2)]
        pairs = [(slotz[:, kt * 512 + otl * 128: kt * 512 + otl * 128 + 128],
                  xn[:, kt * TC:(kt + 1) * TC]) for kt in range(KTH)]
        tZ = mm_group(P, pz[:], pairs,
                      wait=pp.wait(pz) + ([vwz] if otl == 0 else []),
                      inc=s_pe)
        S1["xn_pe_read"] = tZ
        wZ = [(s_pe, tZ)]
        if zdma_v[0] is not None:
            wZ.append((s_sc, zdma_v[0]))
        vz = P.emit("scalar", "activation",
                    dict(out=g["zs16"][0][:], in_=pz[:], func=AF.Silu),
                    wait=wZ, inc=s_evA)
        pp.free(pz, s_evA, vz)
        zdma_v[0] = P.dma("scalar", sz[i * 128:(i + 1) * 128, cs],
                          g["zs16"][0][:], s_sc, wait=[(s_evA, vz)])
        if otl == 3:
            ring_free(P, g, n=2)
    g["sz_store"][c] = zdma_v[0]
    wxpv = [(s_pe, xp_stop)]
    if (c - 1) in g["xp_store"]:
        wxpv.append((s_sc, g["xp_store"][c - 1]))
    vxp = P.emit("vector", "tensor_copy",
                 dict(out=g["xps16"][0:33, 0:512], in_=PS[6][0:33, 0:512]),
                 wait=wxpv, inc=s_evV)
    pp.free(PS[6], s_evV, vxp)
    g["xp_store"][c] = P.dma("scalar", g["xp_hbm"][:, cs], g["xps16"][0:33, 0:512],
                             s_sc, wait=[(s_evV, vxp)])
    S1["xin_war"] = [(s_evV, ln2["xn_ready"]), (s_evA, ln2["last_sq"]),
                     (s_pe, ln2["last_pe"])]



def emit_outproj(P, g, it, xres1_r, xres2_r):
    c = it - 2
    pp = g["pp"]
    s_pe, s_evV, s_sc, s_w = g["s_pe"], g["s_evV"], g["s_sc"], g["s_w"]
    cs = slice(c * TC, (c + 1) * TC)
    xch = g["xch"]
    wq = [g["ring"][(it, "wout", q)] for q in range(4)]
    vxr = P.dma("scalar", xch[:].rearrange("p (kt t) -> p kt t", kt=KTH),
                xres1_r[:, :, cs], s_sc,
                wait=[(s_sc, g["xres1_store"][c]), (s_evV, g["gate_last"])])
    vev = None
    for qp in (0, 1):
        for oh in range(KTH):
            po = ps45(g)
            pairs = []
            for q in (2 * qp, 2 * qp + 1):
                slotq = wq[q][0]
                pairs += [(slotq[:, kt * 1024 + oh * 128: kt * 1024 + oh * 128 + 128],
                           g["ug"][:, (q * 4 + kt) * TC:(q * 4 + kt + 1) * TC])
                          for kt in range(4)]
            tO = mm_group(P, po[:], pairs,
                          wait=pp.wait(po) +
                               ([wq[2 * qp][1], wq[2 * qp + 1][1]] +
                                ([(s_evV, g["gate_last_c"][c])] if qp == 0 else [])
                                if oh == 0 else []),
                          inc=s_pe)
            g["ug_pe_read"] = tO
            vev = P.emit("vector", "tensor_tensor",
                         dict(out=xch[:, oh * TC:(oh + 1) * TC],
                              in0=xch[:, oh * TC:(oh + 1) * TC], in1=po[:],
                              op=ALU.add),
                         wait=[(s_pe, tO)] + ([(s_sc, vxr)] if qp == 0 else []),
                         inc=s_evV)
            pp.free(po, s_evV, vev)
        ring_free(P, g, n=2)
    g["xres2_store"][c] = P.dma("scalar", xres2_r[:, :, cs],
                                xch[:].rearrange("p (kt t) -> p kt t", kt=KTH),
                                s_sc, wait=[(s_evV, vev)])
    g["xch_war"] = [(s_sc, g["xres2_store"][c]), (s_evV, vev)]



class Scan:
    RD = 2

    def __init__(self, P, g, it):
        self.P, self.g, self.it = P, g, it
        self.c = it - 1
        self.cs = slice(self.c * TC, (self.c + 1) * TC)
        self.i = 0
        self.scan_v = {}
        self.da_v = {}
        self.dbx_v = {}
        self.hst_v = {}
        self.vdt = [None, None]
        self.vu = [None, None]
        self.vyb = [None, None]
        self.yadd_v = [{}, {}]
        self.gate_v = [{}, {}]
        self.ymul_guard = {0: (g["s_evP"], g["s_evP"].n),
                           1: (g["s_evP"], g["s_evP"].n)}
        self.h1_prepped = False
        self.carry = 0.0
        self.pe_guard = g["s_pe"].n

    def prep(self):
        P, g, c = self.P, self.g, self.c
        pp = g["pp"]
        s_pe, s_evA, s_evV, s_sc = g["s_pe"], g["s_evA"], g["s_evV"], g["s_sc"]
        xrow, xrow2 = g["xrow"], g["xrow2"]
        Bv_all, Cv_all = g["Bv_all"], g["Cv_all"]
        relay_v = {0: None, 1: None}
        bc_evac = {}
        for r in range(33):
            par = r % 2
            src_row = g["xp_hbm"][r:r + 1, self.cs]
            wld = [(s_sc, g["xp_store"][c])] if r == 0 else []
            if relay_v[par] is not None:
                wld.append((s_evV, relay_v[par]))
            vld = P.dma("scalar", xrow[par][0:1, :], src_row, s_sc, wait=wld)
            wrel = [(s_sc, vld)]
            if (r - 2) in bc_evac:
                wrel.append((s_pe, bc_evac[r - 2][0]))
            rv = P.emit("vector", "tensor_copy",
                        dict(out=xrow2[par][0:1, :], in_=xrow[par][0:1, :]),
                        wait=wrel, inc=s_evV)
            relay_v[par] = rv
            pb = ps45(g)
            tb = P.emit("tensor", "matmul",
                        dict(out=pb[:], lhsT=g["ones16"][0:1, :],
                             rhs=xrow2[par][0:1, :], start=True, stop=True),
                        wait=pp.wait(pb) + [(s_evV, rv)], inc=s_pe)
            if r == 0:
                ve = P.emit("scalar", "copy", dict(out=g["dtr16"][:], in_=pb[:]),
                            wait=[(s_pe, tb)], inc=s_evA)
                self.vdtr_e = ve
            elif r <= 16:
                n = r - 1
                ve = P.emit("scalar", "copy",
                            dict(out=Bv_all[:, n * TC:(n + 1) * TC], in_=pb[:]),
                            wait=[(s_pe, tb)], inc=s_evA)
            else:
                n = r - 17
                ve = P.emit("scalar", "copy",
                            dict(out=Cv_all[:, n * TC:(n + 1) * TC], in_=pb[:]),
                            wait=[(s_pe, tb)], inc=s_evA)
            pp.free(pb, s_evA, ve)
            bc_evac[r] = (tb, ve)
        self.bc_done = bc_evac[32][1]
        self._prep_half(0)

    def _prep_half(self, half):
        P, g, c = self.P, self.g, self.c
        s_pe, s_evA, s_evV, s_sc = g["s_pe"], g["s_evA"], g["s_evV"], g["s_sc"]
        t32, sq32 = g["t32"], g["sq32"]
        xch, uh, ug = g["xch"], g["uh"], g["ug"]
        xc_r = g["xc_hbm"].rearrange("(kt p) t -> p kt t", p=128)
        wx = list(g.get("xch_war", []))
        if half == 1:
            wx = [(s_evV, self.gate_v[0][7]), (s_sc, g["xcdma_sc_last"])]
        else:
            wx.append((s_sc, g["xcdma_v"][0]))
        vxch = P.dma("scalar", xch[:].rearrange("p (kt t) -> p kt t", kt=KTH),
                     xc_r[:, half * 8:(half + 1) * 8, self.cs], s_sc, wait=wx)
        g["xcdma_sc_last"] = vxch
        dth = ug[:, half * 4096:(half + 1) * 4096]
        vdt = {}
        prevdt = {0: None, 1: None}
        pe_snap = self.pe_guard
        for j in range(8):
            d = half * 8 + j
            par = j % 2
            a_, b_ = t32[par][:], sq32[par][:]
            w0 = [(s_evA, self.vdtr_e)]
            if prevdt[par] is not None:
                w0.append((s_evV, prevdt[par]))
            if j < 2:
                w0.append((s_evA, g["t32_act_read"][par]))
                w0.append((s_evA, s_evA.n))
            v0 = P.emit("vector", "tensor_scalar",
                        dict(out=a_, in0=g["dtr16"][:],
                             scalar1=g["dtw32"][:, d:d + 1],
                             scalar2=g["dtb32"][:, d:d + 1],
                             op0=ALU.mult, op1=ALU.add),
                        wait=w0, inc=s_evV)
            v1 = P.emit("scalar", "activation",
                        dict(out=b_, in_=a_, func=AF.Abs),
                        wait=[(s_evV, v0), (s_evV, g["sq_war"][par])] +
                             ([(s_pe, pe_snap)] if j < 2 else []),
                        inc=s_evA)
            g["t32_act_read"][par] = v1
            v2 = P.emit("scalar", "activation",
                        dict(out=b_, in_=b_, func=AF.Exp, scale=-1.0),
                        wait=[(s_evA, v1)], inc=s_evA)
            v3 = P.emit("scalar", "activation",
                        dict(out=b_, in_=b_, func=AF.Ln, bias=1.0),
                        wait=[(s_evA, v2)], inc=s_evA)
            v4 = P.emit("vector", "tensor_scalar",
                        dict(out=a_, in0=a_, scalar1=0.0, scalar2=None,
                             op0=ALU.max),
                        wait=[(s_evA, v1)], inc=s_evV)
            v5 = P.emit("vector", "tensor_tensor",
                        dict(out=dth[:, j * TC:(j + 1) * TC], in0=a_, in1=b_,
                             op=ALU.add),
                        wait=[(s_evV, v4), (s_evA, v3)] +
                             ([(s_pe, g["ug_pe_read"])] if j == 0 else []),
                        inc=s_evV)
            vdt[j] = v5
            prevdt[par] = v5
            g["sq_war"][par] = v5
        vu, vyb = {}, {}
        for j in range(8):
            vu[j] = P.emit("vector", "tensor_tensor",
                           dict(out=uh[:, j * TC:(j + 1) * TC],
                                in0=dth[:, j * TC:(j + 1) * TC],
                                in1=xch[:, j * TC:(j + 1) * TC], op=ALU.mult),
                           wait=[(s_evV, vdt[j]), (s_sc, vxch)], inc=s_evV)
        for j in range(8):
            d = half * 8 + j
            vyb[j] = P.emit("vector", "tensor_scalar",
                            dict(out=xch[:, j * TC:(j + 1) * TC],
                                 in0=xch[:, j * TC:(j + 1) * TC],
                                 scalar1=g["Dd32"][:, d:d + 1], scalar2=None,
                                 op0=ALU.mult),
                            wait=[(s_evV, vu[j])], inc=s_evV)
        self.vdt[half] = vdt
        self.vu[half] = vu
        self.vyb[half] = vyb

    def _tail(self, i):
        P, g = self.P, self.g
        s_evA, s_evV, s_evP = g["s_evA"], g["s_evV"], g["s_evP"]
        half = i // 128
        li = i % 128
        n_, j_ = divmod(li, 8)
        d_ = half * 8 + j_
        ib = i % self.RD
        col = d_ * NST + n_
        dA, dBx, hb = g["dAb"], g["dBxb"], g["hb"]
        xch = g["xch"]
        sv = P.emit("vector", "tensor_tensor_scan",
                    dict(out=hb[ib][:], data0=dA[ib][:], data1=dBx[ib][:],
                         initial=g["hstate"][:, col:col + 1],
                         op0=ALU.mult, op1=ALU.add),
                    wait=[(s_evA, self.da_v[i]), (s_evV, self.dbx_v[i])] +
                         ([(s_evV, self.hst_v[i - self.RD])] if i >= self.RD else
                          ([(s_evV, g["h_ready"])] if self.c == 0 else [])),
                    inc=s_evV)
        self.scan_v[i] = sv
        mv = P.emit("vector", "tensor_tensor",
                    dict(out=g["yst"][i % 2][:], in0=hb[ib][:],
                         in1=g["Cv_all"][:, n_ * TC:(n_ + 1) * TC], op=ALU.mult),
                    wait=[(s_evV, sv), self.ymul_guard[i % 2]],
                    inc=s_evV)
        av = P.emit("gpsimd", "tensor_tensor",
                    dict(out=xch[:, j_ * TC:(j_ + 1) * TC],
                         in0=xch[:, j_ * TC:(j_ + 1) * TC],
                         in1=g["yst"][i % 2][:], op=ALU.add),
                    wait=[(s_evV, mv), (s_evV, self.vyb[half][j_])],
                    inc=s_evP)
        self.yadd_v[half][j_] = av
        self.ymul_guard[i % 2] = (s_evP, av)
        self.hst_v[i] = P.emit("vector", "tensor_copy",
                               dict(out=g["hstate"][:, col:col + 1],
                                    in_=hb[ib][:, 511:512]),
                               wait=[(s_evV, sv)], inc=s_evV)

    def _unit(self, i):
        P, g = self.P, self.g
        s_evA, s_evV = g["s_evA"], g["s_evV"]
        half = i // 128
        li = i % 128
        n, j = divmod(li, 8)
        ib = i % self.RD
        dth = g["ug"][:, half * 4096:(half + 1) * 4096]
        wA = [(s_evV, self.vdt[half][j])]
        if i >= self.RD:
            wA.append((s_evV, self.scan_v[i - self.RD]))
        if self.c == 0 and i == 0:
            wA.append((s_evV, g["A_ready"]))
        if i == 0:
            wA.append((s_evA, self.bc_done))
        self.da_v[i] = P.emit(
            "scalar", "activation",
            dict(out=g["dAb"][ib][:], in_=dth[:, j * TC:(j + 1) * TC],
                 func=AF.Exp,
                 scale=g["A_neg"][:, (half * 8 + j) * NST + n:
                                  (half * 8 + j) * NST + n + 1]),
            wait=wA, inc=s_evA)
        wB = [(s_evV, self.vu[half][j])]
        if i == 0:
            wB.append((s_evA, self.bc_done))
        if i >= self.RD:
            wB.append((s_evV, self.scan_v[i - self.RD]))
        self.dbx_v[i] = P.emit(
            "vector", "tensor_tensor",
            dict(out=g["dBxb"][ib][:], in0=g["uh"][:, j * TC:(j + 1) * TC],
                 in1=g["Bv_all"][:, n * TC:(n + 1) * TC], op=ALU.mult),
            wait=wB, inc=s_evV)

    def _gate(self, half):
        P, g, c = self.P, self.g, self.c
        s_pe, s_evV, s_evP, s_sc = g["s_pe"], g["s_evV"], g["s_evP"], g["s_sc"]
        sz = g["sz_hbm"]
        xch, ug = g["xch"], g["ug"]
        szdma = {}
        for j in range(2):
            d = half * 8 + j
            szdma[j] = P.dma("scalar", g["szst"][j % 2][:],
                             sz[d * 128:(d + 1) * 128, self.cs], s_sc,
                             wait=[(s_sc, g["sz_store"][c])] +
                                  ([(s_evV, g["szst_war"])]
                                   if g.get("szst_war") else []))
        for j in range(8):
            d = half * 8 + j
            self.gate_v[half][j] = P.emit(
                "vector", "tensor_tensor",
                dict(out=ug[:, d * TC:(d + 1) * TC],
                     in0=xch[:, j * TC:(j + 1) * TC],
                     in1=g["szst"][j % 2][:], op=ALU.mult),
                wait=[(s_evP, self.yadd_v[half][j]), (s_sc, szdma[j])],
                inc=s_evV)
            if j + 2 < 8:
                szdma[j + 2] = P.dma("scalar", g["szst"][(j + 2) % 2][:],
                                     sz[(d + 2) * 128:(d + 3) * 128, self.cs],
                                     s_sc, wait=[(s_evV, self.gate_v[half][j])])
        g["szst_war"] = self.gate_v[half][7]
        g["gate_last"] = self.gate_v[half][7]

    def pump_n(self, k):
        while k > 0 and self.i < 256:
            i = self.i
            if i == 128 and not self.h1_prepped:
                self._tail(127)
                self._gate(0)
                self._prep_half(1)
                self.h1_prepped = True
            self._unit(i)
            if i not in (0, 128):
                self._tail(i - 1)
            self.i += 1
            k -= 1

    def make_pump(self, sites, share=1.0):
        per = 256.0 * share / max(sites, 1)

        def pump():
            self.carry += per
            n = int(self.carry)
            if n:
                self.carry -= n
                self.pump_n(n)
        return pump

    def finish(self):
        g = self.g
        self.pump_n(256)
        self._tail(255)
        self._gate(1)
        g["gate_last_c"][self.c] = g["gate_last"]
        g["xch_war"] = [(g["s_evV"], g["gate_last"])]



def build_program():
    nc = bass.Bass("TRN2", target_bir_lowering=False)
    x_T = nc.dram_tensor("x_T", [H, T], F32, kind="ExternalInput").ap()
    out_T = nc.dram_tensor("out_T", [H, T], F32, kind="ExternalOutput").ap()
    xres1 = nc.dram_tensor("xres1", [H, T], F32, kind="Internal").ap()
    xres2 = nc.dram_tensor("xres2", [H, T], F32, kind="Internal").ap()
    from contextlib import ExitStack
    with ExitStack() as stack:
        P = Prog(nc, stack)
        g = alloc_globals(P, nc)
        declare_hbm(P, g, nc)
        emit_prelude(P, g)
        x_r = x_T.rearrange("(kt p) t -> p kt t", p=128)
        out_r = out_T.rearrange("(kt p) t -> p kt t", p=128)
        xr1_r = xres1.rearrange("(kt p) t -> p kt t", p=128)
        xr2_r = xres2.rearrange("(kt p) t -> p kt t", p=128)
        S1, S5 = g["S1"], g["S5"]
        s_sc, s_st, s_evV, s_evA, s_pe = (g["s_sc"], g["s_st"], g["s_evV"],
                                          g["s_evA"], g["s_pe"])
        for it in range(NIT):
            emit_ring_loads(P, g, it)
            if 2 <= it <= NCH + 1:
                emit_outproj(P, g, it, xr1_r, xr2_r)
            ctxs = []
            if it <= NCH - 1:
                emit_ffn_prologue(P, g, S1, x_r, it)
                ctxs.append(S1)
            if 3 <= it:
                c5 = it - 3
                emit_ffn_prologue(P, g, S5, xr2_r, c5,
                                  extra_ld_wait=[(s_sc, g["xres2_store"][c5])],
                                  lo=0)
                ctxs.append(S5)
            scan = None
            if 1 <= it <= NCH:
                scan = Scan(P, g, it)
                scan.prep()
            pump = pump_dn = None
            if scan and INTERLEAVE_SCAN:
                nslots = len(ctxs) * 144
                pump = scan.make_pump(nslots, share=0.85)
                pump_dn = scan.make_pump(nslots, share=0.15)
            emit_expert_pass(P, g, ctxs, it, pump=pump, pump_dn=pump_dn)
            if scan:
                scan.finish()
            if it <= NCH - 1:
                g["xres1_store"][it] = P.dma(
                    "scalar", xr1_r[:, :, it * TC:(it + 1) * TC],
                    S1["xin"][:].rearrange("p (kt t) -> p kt t", kt=KTH),
                    s_sc, wait=[(s_evV, S1["vcomb_last"])])
            if 3 <= it:
                c5 = it - 3
                vo = P.dma("scalar", out_r[:, :, c5 * TC:(c5 + 1) * TC],
                           S5["xin"][:].rearrange("p (kt t) -> p kt t", kt=KTH),
                           s_st, wait=[(s_evV, S5["vcomb_last"])])
                S5["xin_war"] = [(s_st, vo), (s_evV, S5["ln"]["xn_ready"]),
                                 (s_evA, S5["ln"]["last_sq"]),
                                 (s_pe, S5["xn_pe_read"])]
            if it <= NCH - 1:
                emit_p2a(P, g, it)
                S1["xin_war"] = S1["xin_war"] + [(s_sc, g["xres1_store"][it])]
        P.wait("gpsimd", (s_st, s_st.n))
        P.wait("sync", (s_st, s_st.n))
        P.wait("scalar", (s_st, s_st.n))
        block = stack.enter_context(nc.Block())
        P.finalize(block)
    return nc


_NC_CACHE = []


def kernel(L0_attn_x, L0_attn_in_proj_w, L0_attn_conv_w, L0_attn_conv_b,
           L0_attn_x_proj_w, L0_attn_dt_proj_w, L0_attn_dt_proj_b, L0_attn_A_log,
           L0_attn_D, L0_attn_out_proj_w, L0_ffn_shared_gate, L0_ffn_shared_up,
           L0_ffn_shared_down, L0_ffn_gate_w, L0_ffn_up_w, L0_ffn_down_w,
           L0_ffn_router_w, L0_ffn_routing_bias):
    import ml_dtypes
    if not _NC_CACHE:
        _NC_CACHE.append(build_program())
    nc = _NC_CACHE[0]
    f32 = lambda a: np.ascontiguousarray(np.asarray(a, dtype=np.float32))

    def lay(vec):
        return np.ascontiguousarray(np.asarray(vec, np.float32).reshape(16, 128).T)

    WT = f32(L0_attn_in_proj_w).T
    base = {
        "f_su": f32(np.asarray(L0_ffn_shared_up).T),
        "f_sg": f32(np.asarray(L0_ffn_shared_gate).T),
        "f_sd": f32(np.asarray(L0_ffn_shared_down).T),
        "f_router": f32(np.asarray(L0_ffn_router_w).T),
        "c_bias": f32(L0_ffn_routing_bias).reshape(1, 8),
        "c_sel8": np.stack([(np.arange(8)[:, None] == e) * np.ones((8, 128), np.float32)
                            for e in range(8)], axis=1).reshape(8, 1024
                           ).astype(ml_dtypes.bfloat16),
        "c_ident": np.eye(128, dtype=np.float32),
        "c_ones": np.ones((128, 1), np.float32),
        "c_ones_r": np.ones((1, 128), np.float32),
        "f_winx": f32(WT[:, :2048]),
        "f_winz": f32(WT[:, 2048:]),
        "f_wout": f32(np.asarray(L0_attn_out_proj_w).T),
        "f_xp": f32(np.asarray(L0_attn_x_proj_w).T),
        "c_dtw": lay(np.asarray(L0_attn_dt_proj_w)[:, 0]),
        "c_dtb": lay(L0_attn_dt_proj_b),
        "c_Dd": lay(L0_attn_D),
        "c_convb": lay(L0_attn_conv_b),
        "c_convw": np.ascontiguousarray(
            np.asarray(L0_attn_conv_w, np.float32)[:, 0, :]
            .reshape(16, 128, 4).transpose(1, 0, 2).reshape(128, 64)),
        "c_alog": np.ascontiguousarray(
            np.asarray(L0_attn_A_log, np.float32)
            .reshape(16, 128, 16).transpose(1, 0, 2).reshape(128, 256)),
    }
    for e in range(NE):
        base[f"f_up{e}"] = f32(np.asarray(L0_ffn_up_w)[e])
        base[f"f_gate{e}"] = f32(np.asarray(L0_ffn_gate_w)[e])
        base[f"f_down{e}"] = f32(np.asarray(L0_ffn_down_w)[e])
    x = np.asarray(L0_attn_x, np.float32)
    in_maps = [dict(base, x_T=np.ascontiguousarray(x[b].T)) for b in range(8)]
    res = run_bass_kernel_spmd(nc, in_maps, core_ids=list(range(8)))
    out = np.stack([np.asarray(res.results[b]["out_T"]).T for b in range(8)])
    return out.astype(np.float32)


# version 13
# speedup vs baseline: 1.0245x; 1.0045x over previous
import sys
sys.path.insert(0, "/opt/trn_rl_repo")
import numpy as np
import concourse.bass as bass
import concourse.mybir as mybir
from concourse.bass_utils import run_bass_kernel_spmd



F32 = mybir.dt.float32
BF16 = mybir.dt.bfloat16
AF = mybir.ActivationFunctionType
ALU = mybir.AluOpType
AX = mybir.AxisListType

ENGINES = ("sync", "scalar", "vector", "tensor", "gpsimd")


class Sem:
    __slots__ = ("h", "n", "name")

    def __init__(self, h, name):
        self.h = h
        self.n = 0
        self.name = name


class Prog:
    def __init__(self, nc, stack):
        self.nc = nc
        self.stack = stack
        self.streams = {e: [] for e in ENGINES}
        self.ninstr = 0

    def sem(self, name):
        return Sem(self.stack.enter_context(self.nc.semaphore(name)), name)

    def sbuf(self, name, shape, dt):
        return self.stack.enter_context(self.nc.sbuf_tensor(name, shape, dt))

    def psum(self, name, shape, dt=F32):
        return self.stack.enter_context(self.nc.psum_tensor(name, shape, dt))

    def emit(self, eng, method, kwargs, wait=(), inc=None, inc_by=1):
        wait = [(s, v) for (s, v) in wait if s is not None and v is not None]
        tgt = None
        if inc is not None:
            inc.n += inc_by
            tgt = inc.n
        self.streams[eng].append((method, kwargs, wait, inc, inc_by))
        self.ninstr += 1 + len(wait)
        return tgt

    def dma(self, eng, out, in_, sem, wait=(), **kw):
        return self.emit(eng, "dma_start", dict(out=out, in_=in_, **kw),
                         wait=wait, inc=sem, inc_by=16)

    def wait(self, eng, *pairs):
        self.streams[eng].append(
            (None, None, [(s, v) for s, v in pairs if v is not None], None, 0))
        self.ninstr += len(pairs)

    def finalize(self, block):
        for eng in ENGINES:
            ops = self.streams[eng]
            if not ops:
                continue
            section = getattr(block, eng)

            def body(engine, _ops=ops):
                for method, kwargs, wait, inc, inc_by in _ops:
                    for s, v in wait:
                        engine.wait_ge(s.h, v)
                    if method is None:
                        continue
                    ins = getattr(engine, method)(**kwargs)
                    if inc is not None:
                        ins.then_inc(inc.h, inc_by)

            section(body)
        self.streams = {e: [] for e in ENGINES}


def mm_group(P, out, pairs, wait=(), inc=None):
    n = len(pairs)
    v = None
    for i, (l, r) in enumerate(pairs):
        v = P.emit("tensor", "matmul",
                   dict(out=out, lhsT=l, rhs=r, start=(i == 0), stop=(i == n - 1)),
                   wait=list(wait) if i == 0 else (),
                   inc=inc if i == n - 1 else None)
    return v


def rearr_kt(ap, p=128):
    return ap.rearrange("(kt p) o -> p kt o", p=p)


class PsumPool:
    def __init__(self):
        self.freed = {}

    def wait(self, t):
        f = self.freed.get(id(t))
        return [f] if f is not None else []

    def free(self, t, sem, val):
        self.freed[id(t)] = (sem, val)



H, T, DIN, NST, NE, II = 1024, 4096, 2048, 16, 8, 2048
KTH, KTI = 8, 16
TC = 512
NCH = 8
NIT = 11
INTERLEAVE_SCAN = True



def alloc_globals(P, nc):
    g = {"nc": nc}
    g["pp"] = PsumPool()
    for s in ("s_x", "s_w", "s_wfree", "s_pe", "s_evA", "s_evV", "s_st", "s_cast",
              "s_pre", "s_sc", "s_evP", "s_wr", "s_wxp",
              "s_w0", "s_w1", "s_w2"):
        g[s] = P.sem(s)
    g["PS"] = [P.psum(f"PS{i}", [128, 512]) for i in range(8)]
    g["slots"] = [P.sbuf(f"slot{i}", [128, 4096], BF16) for i in range(3)]
    g["hc"] = 0
    g["ring"] = {}
    g["p45"] = 0
    g["p67"] = 0
    g["wb"] = P.sbuf("wb", [128, 512], F32)
    for nm in ("S1", "S5"):
        g[nm] = {
            "name": nm,
            "xin": P.sbuf(nm + "_xin", [128, 4096], F32),
            "xn": P.sbuf(nm + "_xn", [128, 4096], BF16),
            "pgu": P.sbuf(nm + "_pgu", [128, 4096], BF16),
            "wfm": P.sbuf(nm + "_wfm", [8, 512], BF16),
            "xin_war": [],
            "xn_pe_read": None,
            "wfm_pe_read": None,
            "vcomb_last": None,
            "tD_last": None,
        }
    g["sq32"] = [P.sbuf(f"sq32_{i}", [128, 512], F32) for i in range(2)]
    g["t32"] = [P.sbuf(f"t32_{i}", [128, 512], F32) for i in range(2)]
    g["rows"] = [P.sbuf(f"row{i}", [1, 512], F32) for i in range(3)]
    g["sq_war"] = {0: None, 1: None}
    g["t32_act_read"] = {0: None, 1: None}
    g["sel8"] = P.sbuf("sel8", [8, 1024], BF16)
    g["lg32"] = P.sbuf("lg32", [128, 32], F32)
    g["wr16"] = P.sbuf("wr16", [128, 64], BF16)
    g["bias_b32"] = P.sbuf("bias_b32", [128, 8], F32)
    sm = {}
    for n_, w in (("m12", 2), ("msk", 8), ("msked", 8), ("e12", 2), ("d", 1),
                  ("r", 1), ("t1", 1), ("w1", 1), ("w2", 1), ("w8", 8)):
        sm[n_] = P.sbuf("sm_" + n_, [128, w], F32)
    g["sm"] = sm
    g["ident32"] = P.sbuf("ident32", [128, 128], F32)
    g["ones_col32"] = P.sbuf("ones_col32", [128, 1], F32)
    g["ones_row32"] = P.sbuf("ones_row32", [1, 128], F32)
    g["ones16"] = P.sbuf("ones16", [1, 128], BF16)
    g["bias_row"] = P.sbuf("bias_row", [1, 8], F32)
    g["eps_col"] = P.sbuf("eps_col", [128, 1], F32)
    g["A_neg"] = P.sbuf("A_neg", [128, 256], F32)
    g["dtw32"] = P.sbuf("dtw32", [128, 16], F32)
    g["dtb32"] = P.sbuf("dtb32", [128, 16], F32)
    g["Dd32"] = P.sbuf("Dd32", [128, 16], F32)
    g["convw32"] = P.sbuf("convw32", [128, 64], F32)
    g["convb32"] = P.sbuf("convb32", [128, 16], F32)
    g["wxp16"] = P.sbuf("wxp16", [128, 16 * 33], BF16)
    g["xssm"] = [P.sbuf("xssm0", [128, 515], F32)]
    g["carry32"] = P.sbuf("carry32", [128, 48], F32)
    g["zs16"] = [P.sbuf("zs16_0", [128, 512], BF16)]
    g["xc16s"] = [P.sbuf("xc16s_0", [128, 512], BF16)]
    g["xc32s"] = [P.sbuf("xc32s_0", [128, 512], F32)]
    g["xps16"] = P.sbuf("xps16", [33, 512], BF16)
    g["xch"] = P.sbuf("xch", [128, 4096], F32)
    g["uh"] = P.sbuf("uh", [128, 4096], BF16)
    g["ug"] = P.sbuf("ug", [128, 8192], BF16)
    g["Bv_all"] = P.sbuf("Bv_all", [128, 16 * 512], BF16)
    g["Cv_all"] = P.sbuf("Cv_all", [128, 16 * 512], BF16)
    g["dtr16"] = P.sbuf("dtr16", [128, 512], BF16)
    g["dAb"] = [P.sbuf(f"dAb{i}", [128, 512], BF16) for i in range(2)]
    g["dBxb"] = [P.sbuf(f"dBxb{i}", [128, 512], BF16) for i in range(2)]
    g["hb"] = [P.sbuf(f"hb{i}", [128, 512], BF16) for i in range(2)]
    g["yst"] = [P.sbuf(f"yst{i}", [128, 512], BF16) for i in range(2)]
    g["szst"] = [P.sbuf(f"szst{i}", [128, 512], BF16) for i in range(2)]
    g["hstate"] = P.sbuf("hstate", [128, 256], BF16)
    g["xrow"] = [P.sbuf(f"xrow{i}", [1, 512], BF16) for i in range(2)]
    g["xrow2"] = [P.sbuf(f"xrow2_{i}", [1, 512], BF16) for i in range(2)]
    g["gate_last_c"] = {}
    g["ug_pe_read"] = None
    g["xc_store"] = {}
    g["sz_store"] = {}
    g["xp_store"] = {}
    g["xres1_store"] = {}
    g["xres2_store"] = {}
    g["cast_val"] = {}
    g["dth_act_read"] = None
    g["scan_dve_last"] = None
    g["gate_last"] = None
    return g


def declare_hbm(P, g, nc):
    d, b = {}, {}
    shapes = {"su": (H, II), "sg": (H, II), "sd": (II, H),
              "winx": (H, DIN), "winz": (H, DIN), "wout": (DIN, H),
              "xp": (DIN, 33)}
    for e in range(NE):
        shapes[f"up{e}"] = (H, II)
        shapes[f"gate{e}"] = (H, II)
        shapes[f"down{e}"] = (II, H)
    shapes["router"] = (H, 8)
    for name, shp in shapes.items():
        d[name] = nc.dram_tensor("f_" + name, list(shp), F32, kind="ExternalInput").ap()
        b[name] = nc.dram_tensor("b_" + name, list(shp), BF16, kind="Internal").ap()
    g["f_in"], g["wbf"] = d, b
    cn = {}
    for name, shp in (("dtw", (128, 16)), ("dtb", (128, 16)), ("Dd", (128, 16)),
                      ("convw", (128, 64)), ("convb", (128, 16)), ("alog", (128, 256)),
                      ("bias", (1, 8)), ("ident", (128, 128)), ("ones", (128, 1)),
                      ("ones_r", (1, 128))):
        cn[name] = nc.dram_tensor("c_" + name, list(shp), F32, kind="ExternalInput").ap()
    cn["sel8"] = nc.dram_tensor("c_sel8", [8, 1024], BF16, kind="ExternalInput").ap()
    g["c_in"] = cn
    g["xc_hbm"] = nc.dram_tensor("xc_hbm", [2048, 4096], F32, kind="Internal").ap()
    g["sz_hbm"] = nc.dram_tensor("sz_hbm", [2048, 4096], BF16, kind="Internal").ap()
    g["xp_hbm"] = nc.dram_tensor("xp_hbm", [33, 4096], BF16, kind="Internal").ap()


def emit_prelude(P, g):
    nc = g["nc"]
    s_pre, s_evV, s_evA, s_cast, s_pe = (g["s_pre"], g["s_evV"], g["s_evA"],
                                         g["s_cast"], g["s_pe"])
    cn = g["c_in"]
    P.dma("sync", g["ident32"][:], cn["ident"], s_pre)
    P.dma("sync", g["ones_col32"][:], cn["ones"], s_pre)
    P.dma("sync", g["ones_row32"][:], cn["ones_r"], s_pre)
    P.dma("sync", g["bias_row"][:], cn["bias"], s_pre)
    P.dma("sync", g["sel8"][:], cn["sel8"], s_pre)
    for name, dst in (("dtw", "dtw32"), ("dtb", "dtb32"), ("Dd", "Dd32"),
                      ("convw", "convw32"), ("convb", "convb32")):
        P.dma("sync", g[dst][:], cn[name], s_pre)
    v_alog = P.dma("sync", g["A_neg"][:], cn["alog"], s_pre)
    pre_consts = s_pre.n
    P.emit("vector", "memset", dict(ap=g["eps_col"][:], constant=1e-5), inc=s_evV)
    P.emit("vector", "memset", dict(ap=g["ones16"][:], constant=1.0), inc=s_evV)
    g["h_ready"] = P.emit("vector", "memset", dict(ap=g["hstate"][:], constant=0.0),
                          inc=s_evV)
    g["carry_ready"] = P.emit("vector", "memset", dict(ap=g["carry32"][:], constant=0.0),
                              inc=s_evV)
    va = P.emit("scalar", "activation",
                dict(out=g["A_neg"][:], in_=g["A_neg"][:], func=AF.Exp),
                wait=[(s_pre, v_alog)], inc=s_evA)
    g["A_ready"] = P.emit("vector", "tensor_scalar",
                          dict(out=g["A_neg"][:], in0=g["A_neg"][:], scalar1=-1.0,
                               scalar2=None, op0=ALU.mult),
                          wait=[(s_evA, va)], inc=s_evV)
    cast_order = ["router", "su", "sg", "sd"]
    for e in range(NE):
        cast_order += [f"up{e}", f"gate{e}", f"down{e}"]
    cast_order += ["winx", "winz", "wout", "xp"]
    prev_cast = 0
    for name in cast_order:
        g["cast_val"][name] = P.dma("gpsimd", g["wbf"][name], g["f_in"][name],
                                    s_cast,
                                    wait=([(s_cast, prev_cast)] if prev_cast else []))
        prev_cast = g["cast_val"][name]
    g["wr_ready"] = P.dma("sync",
                          g["wr16"][:].rearrange("p (kt e) -> p kt e", kt=KTH),
                          g["wbf"]["router"].rearrange("(kt p) e -> p kt e", p=128),
                          g["s_wr"], wait=[(s_cast, g["cast_val"]["router"])])
    g["wxp_ready"] = P.dma("sync",
                           g["wxp16"][:].rearrange("p (kt o) -> p kt o", kt=KTI),
                           g["wbf"]["xp"].rearrange("(kt p) o -> p kt o", p=128),
                           g["s_wxp"], wait=[(s_cast, g["cast_val"]["xp"])])
    t = P.emit("tensor", "matmul",
               dict(out=g["PS"][0][:, 0:8], lhsT=g["ones_row32"][0:1, :],
                    rhs=g["bias_row"][0:1, :], start=True, stop=True),
               wait=[(s_pre, pre_consts)], inc=s_pe)
    v = P.emit("vector", "tensor_copy", dict(out=g["bias_b32"][:], in_=g["PS"][0][:, 0:8]),
               wait=[(s_pe, t)], inc=s_evV)
    g["pp"].free(g["PS"][0], s_evV, v)
    P.wait("tensor", (g["s_wr"], g["wr_ready"]))



def ps45(g):
    g["p45"] = 1 - g["p45"]
    return g["PS"][4 + g["p45"]]


def ps67(g):
    g["p67"] = 1 - g["p67"]
    return g["PS"][6 + g["p67"]]


NSLOT = 3


def iter_ring_tags(it):
    tags = []
    if 2 <= it <= NCH + 1:
        tags += [("wout", q) for q in range(4)]
    for e in range(-1, NE):
        nU, nG, nD = (("su", "sg", "sd") if e < 0 else
                      (f"up{e}", f"gate{e}", f"down{e}"))
        for h in (0, 1):
            tags += [(nU, 2 * h), (nG, 2 * h), (nU, 2 * h + 1), (nG, 2 * h + 1),
                     (nD, 2 * h), (nD, 2 * h + 1)]
    if it <= NCH - 1:
        for q in range(4):
            tags += [("winx", q), ("winz", q)]
    return tags


def emit_ring_loads(P, g, it):
    s_wfree, s_cast = g["s_wfree"], g["s_cast"]
    subs = [g["s_w0"], g["s_w1"], g["s_w2"]]
    for (name, q) in iter_ring_tags(it):
        hbm = g["wbf"][name]
        if hbm.shape[0] == 1024:
            src = rearr_kt(hbm)[:, :, q * 512:(q + 1) * 512]
            kt = 8
        else:
            src = rearr_kt(hbm)[:, q * 4:(q + 1) * 4, :]
            kt = 4
        m = g["hc"]
        g["hc"] += 1
        sem = subs[m % NSLOT]
        waits = ([(s_wfree, m - (NSLOT - 1))] if m >= NSLOT else [])
        if m >= NSLOT:
            waits.append((sem, 16 * (m // NSLOT)))
        waits.append((s_cast, g["cast_val"][name]))
        slot = g["slots"][m % NSLOT]
        v = P.dma("sync", slot[:].rearrange("p (kt o) -> p kt o", kt=kt), src, sem,
                  wait=waits)
        g["ring"][(it, name, q)] = (slot, (sem, v))


def ring_free(P, g, n=1):
    P.wait("tensor", (g["s_pe"], g["s_pe"].n))
    for _ in range(n):
        P.emit("tensor", "nop", {}, inc=g["s_wfree"])



def emit_ln(P, g, *, xsl, xn_dst, xw, pe_fence, lo=6):
    pp = g["pp"]
    s_pe, s_evA, s_evV = g["s_pe"], g["s_evA"], g["s_evV"]
    PS, sq32, t32, rows = g["PS"], g["sq32"], g["t32"], g["rows"]
    pairs = [(g["ones_col32"][:, 0:1], xsl(kt)) for kt in range(KTH)]
    mm_group(P, PS[lo][0:1, :], pairs, wait=list(xw) + pp.wait(PS[lo]))
    sq_done = []
    last_sq = None
    for kt in range(KTH):
        w = list(xw) + ([(s_pe, pe_fence)] if kt == 0 else [])
        if kt == 0:
            w += [(s_evV, g["sq_war"][0]), (s_evV, g["sq_war"][1])]
        if kt >= 2:
            w.append((s_pe, sq_done[kt - 2]))
        va = P.emit("scalar", "activation",
                    dict(out=sq32[kt % 2][:], in_=xsl(kt), func=AF.Square),
                    wait=w, inc=s_evA)
        last_sq = va
        t = P.emit("tensor", "matmul",
                   dict(out=PS[lo + 1][0:1, :], lhsT=g["ones_col32"][:, 0:1],
                        rhs=sq32[kt % 2][:], start=(kt == 0), stop=(kt == KTH - 1)),
                   wait=[(s_evA, va)] + (pp.wait(PS[lo + 1]) if kt == 0 else []),
                   inc=s_pe)
        sq_done.append(t)
    vm = P.emit("vector", "tensor_scalar",
                dict(out=rows[0][0:1, :], in0=PS[lo][0:1, :], scalar1=1.0 / 1024,
                     scalar2=None, op0=ALU.mult),
                wait=[(s_pe, sq_done[-1])], inc=s_evV)
    vms = P.emit("vector", "tensor_scalar",
                 dict(out=rows[1][0:1, :], in0=PS[lo + 1][0:1, :], scalar1=1.0 / 1024,
                      scalar2=None, op0=ALU.mult), inc=s_evV)
    pp.free(PS[lo], s_evV, vms)
    pp.free(PS[lo + 1], s_evV, vms)
    v2 = P.emit("vector", "tensor_tensor",
                dict(out=rows[2][0:1, :], in0=rows[0][0:1, :], in1=rows[0][0:1, :],
                     op=ALU.mult),
                wait=[(s_evV, vm)], inc=s_evV)
    vvar = P.emit("vector", "tensor_tensor",
                  dict(out=rows[2][0:1, :], in0=rows[1][0:1, :], in1=rows[2][0:1, :],
                       op=ALU.subtract), wait=[(s_evV, v2)], inc=s_evV)
    vln = P.emit("scalar", "activation",
                 dict(out=rows[2][0:1, :], in_=rows[2][0:1, :], func=AF.Ln,
                      bias=g["eps_col"][0:1, 0:1]),
                 wait=[(s_evV, vvar)], inc=s_evA)
    vexp = P.emit("scalar", "activation",
                  dict(out=rows[1][0:1, :], in_=rows[2][0:1, :], func=AF.Exp, scale=-0.5),
                  wait=[(s_evA, vln)], inc=s_evA)
    va = P.emit("vector", "scalar_tensor_tensor",
                dict(out=rows[2][0:1, :], in0=rows[0][0:1, :], scalar=-1.0,
                     in1=rows[1][0:1, :], op0=ALU.mult, op1=ALU.mult),
                wait=[(s_evA, vexp)], inc=s_evV)
    P.emit("tensor", "matmul",
           dict(out=PS[lo][:], lhsT=g["ones_row32"][0:1, :], rhs=rows[1][0:1, :],
                start=True, stop=True),
           wait=[(s_evV, va)] + pp.wait(PS[lo]))
    tb = P.emit("tensor", "matmul",
                dict(out=PS[lo + 1][:], lhsT=g["ones_row32"][0:1, :], rhs=rows[2][0:1, :],
                     start=True, stop=True),
                wait=pp.wait(PS[lo + 1]), inc=s_pe)
    xn_ready = None
    for kt in range(KTH):
        vt = P.emit("vector", "tensor_tensor",
                    dict(out=t32[kt % 2][:], in0=xsl(kt), in1=PS[lo][:], op=ALU.mult),
                    wait=([(s_pe, tb)] if kt == 0 else []) +
                         ([(s_evA, g["t32_act_read"][kt % 2])] if kt < 2 else []),
                    inc=s_evV)
        xn_ready = P.emit("vector", "tensor_tensor",
                          dict(out=xn_dst(kt), in0=t32[kt % 2][:], in1=PS[lo + 1][:],
                               op=ALU.add),
                          wait=[(s_evV, vt)], inc=s_evV)
    g["pp"].free(PS[lo], s_evV, xn_ready)
    g["pp"].free(PS[lo + 1], s_evV, xn_ready)
    return dict(xn_ready=xn_ready, last_sq=last_sq, last_pe=sq_done[-1])



def emit_router(P, g, S, xn_ready, lo=6):
    pp = g["pp"]
    s_pe, s_evA, s_evV = g["s_pe"], g["s_evA"], g["s_evV"]
    PS, sm, lg32 = g["PS"], g["sm"], g["lg32"]
    xn = S["xn"]
    v = None
    for tt in range(TC // 128):
        pt = PS[lo + (tt % 2)]
        pairs = [(xn[:, kt * TC + tt * 128: kt * TC + tt * 128 + 128],
                  g["wr16"][:, kt * 8:(kt + 1) * 8]) for kt in range(KTH)]
        tl = mm_group(P, pt[:, 0:8], pairs,
                      wait=([(s_evV, xn_ready), (g["s_wr"], g["wr_ready"])]
                            if tt == 0 else []) + pp.wait(pt),
                      inc=s_pe)
        lsl = lg32[:, tt * 8:(tt + 1) * 8]
        v = P.emit("vector", "tensor_tensor",
                   dict(out=lsl, in0=pt[:, 0:8], in1=g["bias_b32"][:, 0:8], op=ALU.add),
                   wait=[(s_pe, tl)], inc=s_evV)
        pp.free(pt, s_evV, v)
        m12, msk, w8 = sm["m12"], sm["msk"], sm["w8"]
        vlsl = v
        v = P.emit("vector", "tensor_reduce",
                   dict(out=m12[:, 0:1], in_=lsl, axis=AX.X, op=ALU.max),
                   wait=[(s_evV, vlsl)], inc=s_evV)
        vmsk1 = P.emit("vector", "tensor_scalar",
                       dict(out=msk[:], in0=lsl, scalar1=m12[:, 0:1], scalar2=None,
                            op0=ALU.is_equal), wait=[(s_evV, v)], inc=s_evV)
        vq = P.emit("vector", "scalar_tensor_tensor",
                    dict(out=sm["msked"][:], in0=msk[:], scalar=-1e30, in1=lsl,
                         op0=ALU.mult, op1=ALU.add), wait=[(s_evV, vmsk1)], inc=s_evV)
        v = P.emit("vector", "tensor_reduce",
                   dict(out=m12[:, 1:2], in_=sm["msked"][:], axis=AX.X, op=ALU.max),
                   wait=[(s_evV, vq)], inc=s_evV)
        v = P.emit("scalar", "activation",
                   dict(out=sm["e12"][:], in_=m12[:], func=AF.Exp, scale=-1.0),
                   wait=[(s_evV, v)], inc=s_evA)
        v = P.emit("vector", "tensor_tensor",
                   dict(out=sm["d"][:], in0=sm["e12"][:, 0:1], in1=sm["e12"][:, 1:2],
                        op=ALU.add), wait=[(s_evA, v)], inc=s_evV)
        v = P.emit("vector", "tensor_scalar",
                   dict(out=sm["d"][:], in0=sm["d"][:], scalar1=2.0, scalar2=None,
                        op0=ALU.add), wait=[(s_evV, v)], inc=s_evV)
        v = P.emit("vector", "reciprocal", dict(out=sm["r"][:], in_=sm["d"][:]),
                   wait=[(s_evV, v)], inc=s_evV)
        v = P.emit("vector", "tensor_scalar",
                   dict(out=sm["r"][:], in0=sm["r"][:], scalar1=0.5, scalar2=None,
                        op0=ALU.mult), wait=[(s_evV, v)], inc=s_evV)
        vt1 = P.emit("vector", "tensor_scalar",
                     dict(out=sm["t1"][:], in0=sm["e12"][:, 1:2], scalar1=1.0,
                          scalar2=None, op0=ALU.add), wait=[(s_evV, v)], inc=s_evV)
        vw1 = P.emit("vector", "tensor_tensor",
                     dict(out=sm["w1"][:], in0=sm["t1"][:], in1=sm["r"][:], op=ALU.mult),
                     wait=[(s_evV, vt1)], inc=s_evV)
        vt1 = P.emit("vector", "tensor_scalar",
                     dict(out=sm["t1"][:], in0=sm["e12"][:, 0:1], scalar1=1.0,
                          scalar2=None, op0=ALU.add), wait=[(s_evV, vw1)], inc=s_evV)
        v = P.emit("vector", "tensor_tensor",
                   dict(out=sm["w2"][:], in0=sm["t1"][:], in1=sm["r"][:], op=ALU.mult),
                   wait=[(s_evV, vt1)], inc=s_evV)
        vw8 = P.emit("vector", "tensor_scalar",
                     dict(out=w8[:], in0=msk[:], scalar1=sm["w1"][:, 0:1], scalar2=None,
                          op0=ALU.mult), wait=[(s_evV, v), (s_evV, vmsk1)], inc=s_evV)
        vmsk2 = P.emit("vector", "tensor_scalar",
                       dict(out=msk[:], in0=lsl, scalar1=m12[:, 1:2], scalar2=None,
                            op0=ALU.is_equal), wait=[(s_evV, vw8)], inc=s_evV)
        v = P.emit("vector", "scalar_tensor_tensor",
                   dict(out=w8[:], in0=msk[:], scalar=sm["w2"][:, 0:1], in1=w8[:],
                        op0=ALU.mult, op1=ALU.add), wait=[(s_evV, vmsk2)], inc=s_evV)
        ptp = PS[lo + ((tt + 1) % 2)]
        t2 = P.emit("tensor", "transpose",
                    dict(out=ptp[0:8, 0:128], in_=w8[:], identity=g["ident32"][:]),
                    wait=[(s_evV, v)] + pp.wait(ptp), inc=s_pe)
        v = P.emit("vector", "tensor_copy",
                   dict(out=S["wfm"][:, tt * 128:(tt + 1) * 128], in_=ptp[0:8, 0:128]),
                   wait=[(s_pe, t2)] +
                        ([(s_pe, S["wfm_pe_read"])] if tt == 0 else []),
                   inc=s_evV)
        pp.free(ptp, s_evV, v)
    return v



def emit_ffn_prologue(P, g, S, src_r, c, extra_ld_wait=(), lo=6):
    s_x, s_evV = g["s_x"], g["s_evV"]
    vld = P.dma("scalar",
                S["xin"][:].rearrange("p (kt t) -> p kt t", kt=KTH),
                src_r[:, :, c * TC:(c + 1) * TC], s_x,
                wait=list(S["xin_war"]) + list(extra_ld_wait))
    pe_fence = g["s_pe"].n
    xw = [(s_x, vld)]

    def xsl(kt):
        return S["xin"][:, kt * TC:(kt + 1) * TC]
    ln = emit_ln(P, g, xsl=xsl,
                 xn_dst=lambda kt: S["xn"][:, kt * TC:(kt + 1) * TC],
                 xw=xw, pe_fence=pe_fence, lo=lo)
    wfm_v = emit_router(P, g, S, ln["xn_ready"], lo=lo)
    S["ln"] = ln
    S["wfm_v"] = wfm_v
    return ln, wfm_v



def emit_expert_pass(P, g, ctxs, it, pump=None, pump_dn=None):
    if pump is None:
        pump = lambda: None
    if pump_dn is None:
        pump_dn = pump
    pp = g["pp"]
    s_pe, s_evA, s_evV, s_w = g["s_pe"], g["s_evA"], g["s_evV"], g["s_w"]
    PS, sq32, t32 = g["PS"], g["sq32"], g["t32"]
    for e in range(-1, NE):
        nU, nG, nD = (("su", "sg", "sd") if e < 0 else
                      (f"up{e}", f"gate{e}", f"down{e}"))
        wb_ps = {}
        for S in ctxs:
            if e >= 0:
                pw = ps67(g)
                twb = P.emit("tensor", "matmul",
                             dict(out=pw[:], lhsT=g["sel8"][:, e * 128:(e + 1) * 128],
                                  rhs=S["wfm"][0:8, 0:512], start=True, stop=True),
                             wait=pp.wait(pw) +
                                  ([(s_evV, S["wfm_v"])] if e == 0 else []),
                             inc=s_pe)
                S["wfm_pe_read"] = twb
                wb_ps[id(S)] = (pw, twb)
        for half in (0, 1):
            slotUa, vUa = g["ring"][(it, nU, 2 * half)]
            slotGa, vGa = g["ring"][(it, nG, 2 * half)]
            slotUb, vUb = g["ring"][(it, nU, 2 * half + 1)]
            slotGb, vGb = g["ring"][(it, nG, 2 * half + 1)]
            slotDa, vDa = g["ring"][(it, nD, 2 * half)]
            slotDb, vDb = g["ring"][(it, nD, 2 * half + 1)]
            for qh in (0, 1):
                sU, sG = (slotUa, slotGa) if qh == 0 else (slotUb, slotGb)
                vU, vG = (vUa, vGa) if qh == 0 else (vUb, vGb)
                for S in ctxs:
                    for otl in range(4):
                        ot = qh * 4 + otl
                        pu, pg = PS[0 + (ot % 2)], PS[2 + (ot % 2)]
                        pairs_u = [(sU[:, kt * 512 + otl * 128: kt * 512 + otl * 128 + 128],
                                    S["xn"][:, kt * TC:(kt + 1) * TC]) for kt in range(KTH)]
                        tU = mm_group(P, pu[:], pairs_u,
                                      wait=pp.wait(pu) +
                                           ([vU] if otl == 0 else []),
                                      inc=s_pe)
                        pairs_g = [(sG[:, kt * 512 + otl * 128: kt * 512 + otl * 128 + 128],
                                    S["xn"][:, kt * TC:(kt + 1) * TC]) for kt in range(KTH)]
                        tG = mm_group(P, pg[:], pairs_g,
                                      wait=pp.wait(pg) +
                                           ([vG] if otl == 0 else []),
                                      inc=s_pe)
                        S["xn_pe_read"] = tG
                        vg = P.emit("scalar", "activation",
                                    dict(out=sq32[ot % 2][:], in_=pg[:], func=AF.Silu),
                                    wait=[(s_pe, tG), (s_evV, g["sq_war"][ot % 2])],
                                    inc=s_evA)
                        pp.free(pg, s_evA, vg)
                        wvp = [(s_evA, vg), (s_pe, tU)]
                        if ot == 0:
                            wvp.append((s_pe, S["tD_last"]))
                        vp = P.emit("vector", "tensor_tensor",
                                    dict(out=S["pgu"][:, ot * TC:(ot + 1) * TC],
                                         in0=sq32[ot % 2][:], in1=pu[:], op=ALU.mult),
                                    wait=wvp, inc=s_evV)
                        pp.free(pu, s_evV, vp)
                        g["sq_war"][ot % 2] = vp
                        S["vp_last"] = vp
                        pump()
                ring_free(P, g, n=2)
            for S in ctxs:
                if e >= 0:
                    pw, twb = wb_ps[id(S)]
                    vwb = P.emit("vector", "tensor_copy",
                                 dict(out=g["wb"][:], in_=pw[:]),
                                 wait=[(s_pe, twb)], inc=s_evV)
                    if half == 1:
                        pp.free(pw, s_evV, vwb)
                for oh in range(8):
                    pd = ps45(g)
                    pairs_d = ([(slotDa[:, kt * 1024 + oh * 128: kt * 1024 + oh * 128 + 128],
                                 S["pgu"][:, kt * TC:(kt + 1) * TC]) for kt in range(4)] +
                               [(slotDb[:, kt * 1024 + oh * 128: kt * 1024 + oh * 128 + 128],
                                 S["pgu"][:, (4 + kt) * TC:(5 + kt) * TC]) for kt in range(4)])
                    tD = mm_group(P, pd[:], pairs_d,
                                  wait=pp.wait(pd) + [(s_evV, S["vp_last"])] +
                                       ([vDa, vDb] if oh == 0 else []),
                                  inc=s_pe)
                    S["tD_last"] = tD
                    if e < 0:
                        vd = P.emit("vector", "tensor_scalar",
                                    dict(out=t32[oh % 2][:], in0=pd[:], scalar1=0.5,
                                         scalar2=None, op0=ALU.mult),
                                    wait=[(s_pe, tD),
                                          (s_evA, g["t32_act_read"][oh % 2])],
                                    inc=s_evV)
                    else:
                        vd = P.emit("vector", "tensor_tensor",
                                    dict(out=t32[oh % 2][:], in0=pd[:], in1=g["wb"][:],
                                         op=ALU.mult),
                                    wait=[(s_pe, tD),
                                          (s_evA, g["t32_act_read"][oh % 2])],
                                    inc=s_evV)
                    vcomb = P.emit("vector", "tensor_tensor",
                                   dict(out=S["xin"][:, oh * TC:(oh + 1) * TC],
                                        in0=S["xin"][:, oh * TC:(oh + 1) * TC],
                                        in1=t32[oh % 2][:], op=ALU.add),
                                   wait=[(s_evV, vd)], inc=s_evV)
                    pp.free(pd, s_evV, vcomb)
                    S["vcomb_last"] = vcomb
                    pump_dn()
            ring_free(P, g, n=2)



def emit_p2a(P, g, it):
    c = it
    pp = g["pp"]
    s_pe, s_evA, s_evV, s_sc, s_w = (g["s_pe"], g["s_evA"], g["s_evV"], g["s_sc"],
                                     g["s_w"])
    PS = g["PS"]
    S1 = g["S1"]
    xn = S1["xn"]
    xssm, carry, t32 = g["xssm"], g["carry32"], g["t32"]
    xc_r = g["xc_hbm"].rearrange("(kt p) t -> p kt t", p=128)
    sz = g["sz_hbm"]
    cs = slice(c * TC, (c + 1) * TC)
    pe_fence = s_pe.n
    xw = [(s_evV, S1["vcomb_last"]), (s_pe, S1["xn_pe_read"])]
    ln2 = emit_ln(P, g, xsl=lambda kt: S1["xin"][:, kt * TC:(kt + 1) * TC],
                  xn_dst=lambda kt: xn[:, kt * TC:(kt + 1) * TC],
                  xw=xw, pe_fence=pe_fence)
    wxq = [g["ring"][(it, "winx", q)] for q in range(4)]
    wzq = [g["ring"][(it, "winz", q)] for q in range(4)]
    zdma_v = g.setdefault("zdma_v", [None])
    xssm_rd = g.setdefault("xssm_rd", [None])
    silu_v = g.setdefault("silu_v", [None])
    xc16_v = g.setdefault("xc16_v", [None])
    xc32_rd = g.setdefault("xc32_rd", [None])
    xcdma_v = g.setdefault("xcdma_v", [None])
    xp_stop = None
    for i in range(KTI):
        slot, vw = wxq[i // 4]
        otl = i % 4
        px = PS[0 + (i % 2)]
        pairs = [(slot[:, kt * 512 + otl * 128: kt * 512 + otl * 128 + 128],
                  xn[:, kt * TC:(kt + 1) * TC]) for kt in range(KTH)]
        tX = mm_group(P, px[:], pairs,
                      wait=pp.wait(px) + ([vw] if otl == 0 else []) +
                           ([(s_evV, ln2["xn_ready"])] if i == 0 else []),
                      inc=s_pe)
        xt = g["xssm"][0]
        wA = [(s_pe, tX)]
        if xssm_rd[0] is not None:
            wA.append((s_evV, xssm_rd[0]))
        vev = P.emit("scalar", "copy", dict(out=xt[:, 3:515], in_=px[:]),
                     wait=wA, inc=s_evA)
        pp.free(px, s_evA, vev)
        vcr = P.emit("vector", "tensor_copy",
                     dict(out=xt[:, 0:3], in_=carry[:, i * 3:i * 3 + 3]),
                     wait=([(s_evV, g["carry_ready"])] if (c == 0 and i < 3) else []) +
                          [(s_evA, vev)],
                     inc=s_evV)
        wT = [(s_evV, vcr)]
        if silu_v[0] is not None:
            wT.append((s_evA, silu_v[0]))
        vt = P.emit("vector", "tensor_scalar",
                    dict(out=t32[i % 2][:], in0=xt[:, 0:512],
                         scalar1=g["convw32"][:, 4 * i:4 * i + 1], scalar2=None,
                         op0=ALU.mult),
                    wait=wT + ([(s_evA, g["t32_act_read"][i % 2])] if i < 2 else []),
                    inc=s_evV)
        for k in range(1, 4):
            vt = P.emit("vector", "scalar_tensor_tensor",
                        dict(out=t32[i % 2][:], in0=xt[:, k:k + 512],
                             scalar=g["convw32"][:, 4 * i + k:4 * i + k + 1],
                             in1=t32[i % 2][:], op0=ALU.mult, op1=ALU.add),
                        wait=[(s_evV, vt)], inc=s_evV)
        vco = P.emit("vector", "tensor_copy",
                     dict(out=carry[:, i * 3:i * 3 + 3], in_=xt[:, 512:515]),
                     inc=s_evV)
        xssm_rd[0] = vco
        wS = [(s_evV, vt)]
        if xcdma_v[0] is not None:
            wS.append((s_sc, xcdma_v[0]))
        if xc32_rd[0] is not None:
            wS.append((s_evV, xc32_rd[0]))
        vs = P.emit("scalar", "activation",
                    dict(out=g["xc32s"][0][:], in_=t32[i % 2][:],
                         func=AF.Silu, bias=g["convb32"][:, i:i + 1]),
                    wait=wS, inc=s_evA)
        silu_v[0] = vs
        g["t32_act_read"][i % 2] = vs
        xcdma_v[0] = P.dma("scalar", xc_r[:, i, cs], g["xc32s"][0][:],
                           s_sc, wait=[(s_evA, vs)])
        vxc = P.emit("vector", "tensor_copy",
                     dict(out=g["xc16s"][0][:], in_=g["xc32s"][0][:]),
                     wait=[(s_evA, vs)] +
                          ([(s_pe, xc16_v[0])] if xc16_v[0] else []),
                     inc=s_evV)
        xc32_rd[0] = vxc
        xp_stop = P.emit("tensor", "matmul",
                         dict(out=PS[6][0:33, 0:512],
                              lhsT=g["wxp16"][:, i * 33:(i + 1) * 33],
                              rhs=g["xc16s"][0][:],
                              start=(i == 0), stop=(i == KTI - 1),
                              skip_group_check=True),
                         wait=[(s_evV, vxc)] +
                              ([(g["s_wxp"], g["wxp_ready"])] if i == 0 else []) +
                              (pp.wait(PS[6]) if i == 0 else []),
                         inc=s_pe)
        xc16_v[0] = xp_stop
        slotz, vwz = wzq[i // 4]
        pz = PS[2 + (i % 2)]
        pairs = [(slotz[:, kt * 512 + otl * 128: kt * 512 + otl * 128 + 128],
                  xn[:, kt * TC:(kt + 1) * TC]) for kt in range(KTH)]
        tZ = mm_group(P, pz[:], pairs,
                      wait=pp.wait(pz) + ([vwz] if otl == 0 else []),
                      inc=s_pe)
        S1["xn_pe_read"] = tZ
        wZ = [(s_pe, tZ)]
        if zdma_v[0] is not None:
            wZ.append((s_sc, zdma_v[0]))
        vz = P.emit("scalar", "activation",
                    dict(out=g["zs16"][0][:], in_=pz[:], func=AF.Silu),
                    wait=wZ, inc=s_evA)
        pp.free(pz, s_evA, vz)
        zdma_v[0] = P.dma("scalar", sz[i * 128:(i + 1) * 128, cs],
                          g["zs16"][0][:], s_sc, wait=[(s_evA, vz)])
        if otl == 3:
            ring_free(P, g, n=2)
    g["sz_store"][c] = zdma_v[0]
    wxpv = [(s_pe, xp_stop)]
    if (c - 1) in g["xp_store"]:
        wxpv.append((s_sc, g["xp_store"][c - 1]))
    vxp = P.emit("vector", "tensor_copy",
                 dict(out=g["xps16"][0:33, 0:512], in_=PS[6][0:33, 0:512]),
                 wait=wxpv, inc=s_evV)
    pp.free(PS[6], s_evV, vxp)
    g["xp_store"][c] = P.dma("scalar", g["xp_hbm"][:, cs], g["xps16"][0:33, 0:512],
                             s_sc, wait=[(s_evV, vxp)])
    S1["xin_war"] = [(s_evV, ln2["xn_ready"]), (s_evA, ln2["last_sq"]),
                     (s_pe, ln2["last_pe"])]



def emit_outproj(P, g, it, xres1_r, xres2_r):
    c = it - 2
    pp = g["pp"]
    s_pe, s_evV, s_sc, s_w = g["s_pe"], g["s_evV"], g["s_sc"], g["s_w"]
    cs = slice(c * TC, (c + 1) * TC)
    xch = g["xch"]
    wq = [g["ring"][(it, "wout", q)] for q in range(4)]
    vxr = P.dma("scalar", xch[:].rearrange("p (kt t) -> p kt t", kt=KTH),
                xres1_r[:, :, cs], s_sc,
                wait=[(s_sc, g["xres1_store"][c]), (s_evV, g["gate_last"])])
    vev = None
    for qp in (0, 1):
        for oh in range(KTH):
            po = ps45(g)
            pairs = []
            for q in (2 * qp, 2 * qp + 1):
                slotq = wq[q][0]
                pairs += [(slotq[:, kt * 1024 + oh * 128: kt * 1024 + oh * 128 + 128],
                           g["ug"][:, (q * 4 + kt) * TC:(q * 4 + kt + 1) * TC])
                          for kt in range(4)]
            tO = mm_group(P, po[:], pairs,
                          wait=pp.wait(po) +
                               ([wq[2 * qp][1], wq[2 * qp + 1][1]] +
                                ([(s_evV, g["gate_last_c"][c])] if qp == 0 else [])
                                if oh == 0 else []),
                          inc=s_pe)
            g["ug_pe_read"] = tO
            vev = P.emit("vector", "tensor_tensor",
                         dict(out=xch[:, oh * TC:(oh + 1) * TC],
                              in0=xch[:, oh * TC:(oh + 1) * TC], in1=po[:],
                              op=ALU.add),
                         wait=[(s_pe, tO)] + ([(s_sc, vxr)] if qp == 0 else []),
                         inc=s_evV)
            pp.free(po, s_evV, vev)
        ring_free(P, g, n=2)
    g["xres2_store"][c] = P.dma("scalar", xres2_r[:, :, cs],
                                xch[:].rearrange("p (kt t) -> p kt t", kt=KTH),
                                s_sc, wait=[(s_evV, vev)])
    g["xch_war"] = [(s_sc, g["xres2_store"][c]), (s_evV, vev)]



class Scan:
    RD = 2

    def __init__(self, P, g, it):
        self.P, self.g, self.it = P, g, it
        self.c = it - 1
        self.cs = slice(self.c * TC, (self.c + 1) * TC)
        self.i = 0
        self.scan_v = {}
        self.da_v = {}
        self.dbx_v = {}
        self.hst_v = {}
        self.vdt = [None, None]
        self.vu = [None, None]
        self.vyb = [None, None]
        self.yadd_v = [{}, {}]
        self.gate_v = [{}, {}]
        self.ymul_guard = {0: (g["s_evP"], g["s_evP"].n),
                           1: (g["s_evP"], g["s_evP"].n)}
        self.h1_prepped = False
        self.carry = 0.0
        self.pe_guard = g["s_pe"].n

    def prep(self):
        P, g, c = self.P, self.g, self.c
        pp = g["pp"]
        s_pe, s_evA, s_evV, s_sc = g["s_pe"], g["s_evA"], g["s_evV"], g["s_sc"]
        xrow, xrow2 = g["xrow"], g["xrow2"]
        Bv_all, Cv_all = g["Bv_all"], g["Cv_all"]
        relay_v = {0: None, 1: None}
        bc_evac = {}
        for r in range(33):
            par = r % 2
            src_row = g["xp_hbm"][r:r + 1, self.cs]
            wld = [(s_sc, g["xp_store"][c])] if r == 0 else []
            if relay_v[par] is not None:
                wld.append((s_evV, relay_v[par]))
            vld = P.dma("scalar", xrow[par][0:1, :], src_row, s_sc, wait=wld)
            wrel = [(s_sc, vld)]
            if (r - 2) in bc_evac:
                wrel.append((s_pe, bc_evac[r - 2][0]))
            rv = P.emit("vector", "tensor_copy",
                        dict(out=xrow2[par][0:1, :], in_=xrow[par][0:1, :]),
                        wait=wrel, inc=s_evV)
            relay_v[par] = rv
            pb = ps45(g)
            tb = P.emit("tensor", "matmul",
                        dict(out=pb[:], lhsT=g["ones16"][0:1, :],
                             rhs=xrow2[par][0:1, :], start=True, stop=True),
                        wait=pp.wait(pb) + [(s_evV, rv)], inc=s_pe)
            if r == 0:
                ve = P.emit("scalar", "copy", dict(out=g["dtr16"][:], in_=pb[:]),
                            wait=[(s_pe, tb)], inc=s_evA)
                self.vdtr_e = ve
            elif r <= 16:
                n = r - 1
                ve = P.emit("scalar", "copy",
                            dict(out=Bv_all[:, n * TC:(n + 1) * TC], in_=pb[:]),
                            wait=[(s_pe, tb)], inc=s_evA)
            else:
                n = r - 17
                ve = P.emit("scalar", "copy",
                            dict(out=Cv_all[:, n * TC:(n + 1) * TC], in_=pb[:]),
                            wait=[(s_pe, tb)], inc=s_evA)
            pp.free(pb, s_evA, ve)
            bc_evac[r] = (tb, ve)
        self.bc_done = bc_evac[32][1]
        self._prep_half(0)

    def _prep_half(self, half):
        P, g, c = self.P, self.g, self.c
        s_pe, s_evA, s_evV, s_sc = g["s_pe"], g["s_evA"], g["s_evV"], g["s_sc"]
        t32, sq32 = g["t32"], g["sq32"]
        xch, uh, ug = g["xch"], g["uh"], g["ug"]
        xc_r = g["xc_hbm"].rearrange("(kt p) t -> p kt t", p=128)
        wx = list(g.get("xch_war", []))
        if half == 1:
            wx = [(s_evV, self.gate_v[0][7]), (s_sc, g["xcdma_sc_last"])]
        else:
            wx.append((s_sc, g["xcdma_v"][0]))
        vxch = P.dma("scalar", xch[:].rearrange("p (kt t) -> p kt t", kt=KTH),
                     xc_r[:, half * 8:(half + 1) * 8, self.cs], s_sc, wait=wx)
        g["xcdma_sc_last"] = vxch
        dth = ug[:, half * 4096:(half + 1) * 4096]
        vdt = {}
        prevdt = {0: None, 1: None}
        pe_snap = self.pe_guard
        for j in range(8):
            d = half * 8 + j
            par = j % 2
            a_, b_ = t32[par][:], sq32[par][:]
            w0 = [(s_evA, self.vdtr_e)]
            if prevdt[par] is not None:
                w0.append((s_evV, prevdt[par]))
            if j < 2:
                w0.append((s_evA, g["t32_act_read"][par]))
                w0.append((s_evA, s_evA.n))
            v0 = P.emit("vector", "tensor_scalar",
                        dict(out=a_, in0=g["dtr16"][:],
                             scalar1=g["dtw32"][:, d:d + 1],
                             scalar2=g["dtb32"][:, d:d + 1],
                             op0=ALU.mult, op1=ALU.add),
                        wait=w0, inc=s_evV)
            v1 = P.emit("scalar", "activation",
                        dict(out=b_, in_=a_, func=AF.Abs),
                        wait=[(s_evV, v0), (s_evV, g["sq_war"][par])] +
                             ([(s_pe, pe_snap)] if j < 2 else []),
                        inc=s_evA)
            g["t32_act_read"][par] = v1
            v2 = P.emit("scalar", "activation",
                        dict(out=b_, in_=b_, func=AF.Exp, scale=-1.0),
                        wait=[(s_evA, v1)], inc=s_evA)
            v3 = P.emit("scalar", "activation",
                        dict(out=b_, in_=b_, func=AF.Ln, bias=1.0),
                        wait=[(s_evA, v2)], inc=s_evA)
            v4 = P.emit("vector", "tensor_scalar",
                        dict(out=a_, in0=a_, scalar1=0.0, scalar2=None,
                             op0=ALU.max),
                        wait=[(s_evA, v1)], inc=s_evV)
            v5 = P.emit("vector", "tensor_tensor",
                        dict(out=dth[:, j * TC:(j + 1) * TC], in0=a_, in1=b_,
                             op=ALU.add),
                        wait=[(s_evV, v4), (s_evA, v3)] +
                             ([(s_pe, g["ug_pe_read"])] if j == 0 else []),
                        inc=s_evV)
            vdt[j] = v5
            prevdt[par] = v5
            g["sq_war"][par] = v5
        vu, vyb = {}, {}
        for j in range(8):
            vu[j] = P.emit("vector", "tensor_tensor",
                           dict(out=uh[:, j * TC:(j + 1) * TC],
                                in0=dth[:, j * TC:(j + 1) * TC],
                                in1=xch[:, j * TC:(j + 1) * TC], op=ALU.mult),
                           wait=[(s_evV, vdt[j]), (s_sc, vxch)], inc=s_evV)
        for j in range(8):
            d = half * 8 + j
            vyb[j] = P.emit("vector", "tensor_scalar",
                            dict(out=xch[:, j * TC:(j + 1) * TC],
                                 in0=xch[:, j * TC:(j + 1) * TC],
                                 scalar1=g["Dd32"][:, d:d + 1], scalar2=None,
                                 op0=ALU.mult),
                            wait=[(s_evV, vu[j])], inc=s_evV)
        self.vdt[half] = vdt
        self.vu[half] = vu
        self.vyb[half] = vyb

    def _tail(self, i):
        P, g = self.P, self.g
        s_evA, s_evV, s_evP = g["s_evA"], g["s_evV"], g["s_evP"]
        half = i // 128
        li = i % 128
        n_, j_ = divmod(li, 8)
        d_ = half * 8 + j_
        ib = i % self.RD
        col = d_ * NST + n_
        dA, dBx, hb = g["dAb"], g["dBxb"], g["hb"]
        xch = g["xch"]
        sv = P.emit("vector", "tensor_tensor_scan",
                    dict(out=hb[ib][:], data0=dA[ib][:], data1=dBx[ib][:],
                         initial=g["hstate"][:, col:col + 1],
                         op0=ALU.mult, op1=ALU.add),
                    wait=[(s_evA, self.da_v[i]), (s_evV, self.dbx_v[i])] +
                         ([(s_evV, self.hst_v[i - self.RD])] if i >= self.RD else
                          ([(s_evV, g["h_ready"])] if self.c == 0 else [])),
                    inc=s_evV)
        self.scan_v[i] = sv
        mv = P.emit("vector", "tensor_tensor",
                    dict(out=g["yst"][i % 2][:], in0=hb[ib][:],
                         in1=g["Cv_all"][:, n_ * TC:(n_ + 1) * TC], op=ALU.mult),
                    wait=[(s_evV, sv), self.ymul_guard[i % 2]],
                    inc=s_evV)
        av = P.emit("gpsimd", "tensor_tensor",
                    dict(out=xch[:, j_ * TC:(j_ + 1) * TC],
                         in0=xch[:, j_ * TC:(j_ + 1) * TC],
                         in1=g["yst"][i % 2][:], op=ALU.add),
                    wait=[(s_evV, mv), (s_evV, self.vyb[half][j_])],
                    inc=s_evP)
        self.yadd_v[half][j_] = av
        self.ymul_guard[i % 2] = (s_evP, av)
        self.hst_v[i] = P.emit("vector", "tensor_copy",
                               dict(out=g["hstate"][:, col:col + 1],
                                    in_=hb[ib][:, 511:512]),
                               wait=[(s_evV, sv)], inc=s_evV)

    def _unit(self, i):
        P, g = self.P, self.g
        s_evA, s_evV = g["s_evA"], g["s_evV"]
        half = i // 128
        li = i % 128
        n, j = divmod(li, 8)
        ib = i % self.RD
        dth = g["ug"][:, half * 4096:(half + 1) * 4096]
        wA = [(s_evV, self.vdt[half][j])]
        if i >= self.RD:
            wA.append((s_evV, self.scan_v[i - self.RD]))
        if self.c == 0 and i == 0:
            wA.append((s_evV, g["A_ready"]))
        if i == 0:
            wA.append((s_evA, self.bc_done))
        self.da_v[i] = P.emit(
            "scalar", "activation",
            dict(out=g["dAb"][ib][:], in_=dth[:, j * TC:(j + 1) * TC],
                 func=AF.Exp,
                 scale=g["A_neg"][:, (half * 8 + j) * NST + n:
                                  (half * 8 + j) * NST + n + 1]),
            wait=wA, inc=s_evA)
        wB = [(s_evV, self.vu[half][j])]
        if i == 0:
            wB.append((s_evA, self.bc_done))
        if i >= self.RD:
            wB.append((s_evV, self.scan_v[i - self.RD]))
        self.dbx_v[i] = P.emit(
            "vector", "tensor_tensor",
            dict(out=g["dBxb"][ib][:], in0=g["uh"][:, j * TC:(j + 1) * TC],
                 in1=g["Bv_all"][:, n * TC:(n + 1) * TC], op=ALU.mult),
            wait=wB, inc=s_evV)

    def _gate(self, half):
        P, g, c = self.P, self.g, self.c
        s_pe, s_evV, s_evP, s_sc = g["s_pe"], g["s_evV"], g["s_evP"], g["s_sc"]
        sz = g["sz_hbm"]
        xch, ug = g["xch"], g["ug"]
        szdma = {}
        for j in range(2):
            d = half * 8 + j
            szdma[j] = P.dma("scalar", g["szst"][j % 2][:],
                             sz[d * 128:(d + 1) * 128, self.cs], s_sc,
                             wait=[(s_sc, g["sz_store"][c])] +
                                  ([(s_evV, g["szst_war"])]
                                   if g.get("szst_war") else []))
        for j in range(8):
            d = half * 8 + j
            self.gate_v[half][j] = P.emit(
                "vector", "tensor_tensor",
                dict(out=ug[:, d * TC:(d + 1) * TC],
                     in0=xch[:, j * TC:(j + 1) * TC],
                     in1=g["szst"][j % 2][:], op=ALU.mult),
                wait=[(s_evP, self.yadd_v[half][j]), (s_sc, szdma[j])],
                inc=s_evV)
            if j + 2 < 8:
                szdma[j + 2] = P.dma("scalar", g["szst"][(j + 2) % 2][:],
                                     sz[(d + 2) * 128:(d + 3) * 128, self.cs],
                                     s_sc, wait=[(s_evV, self.gate_v[half][j])])
        g["szst_war"] = self.gate_v[half][7]
        g["gate_last"] = self.gate_v[half][7]

    def pump_n(self, k):
        while k > 0 and self.i < 256:
            i = self.i
            if i == 128 and not self.h1_prepped:
                self._tail(127)
                self._gate(0)
                self._prep_half(1)
                self.h1_prepped = True
            self._unit(i)
            if i not in (0, 128):
                self._tail(i - 1)
            self.i += 1
            k -= 1

    def make_pump(self, sites, share=1.0):
        per = 256.0 * share / max(sites, 1)

        def pump():
            self.carry += per
            n = int(self.carry)
            if n:
                self.carry -= n
                self.pump_n(n)
        return pump

    def finish(self):
        g = self.g
        self.pump_n(256)
        self._tail(255)
        self._gate(1)
        g["gate_last_c"][self.c] = g["gate_last"]
        g["xch_war"] = [(g["s_evV"], g["gate_last"])]



def build_program():
    nc = bass.Bass("TRN2", target_bir_lowering=False)
    x_T = nc.dram_tensor("x_T", [H, T], F32, kind="ExternalInput").ap()
    out_T = nc.dram_tensor("out_T", [H, T], F32, kind="ExternalOutput").ap()
    xres1 = nc.dram_tensor("xres1", [H, T], F32, kind="Internal").ap()
    xres2 = nc.dram_tensor("xres2", [H, T], F32, kind="Internal").ap()
    from contextlib import ExitStack
    with ExitStack() as stack:
        P = Prog(nc, stack)
        g = alloc_globals(P, nc)
        declare_hbm(P, g, nc)
        emit_prelude(P, g)
        x_r = x_T.rearrange("(kt p) t -> p kt t", p=128)
        out_r = out_T.rearrange("(kt p) t -> p kt t", p=128)
        xr1_r = xres1.rearrange("(kt p) t -> p kt t", p=128)
        xr2_r = xres2.rearrange("(kt p) t -> p kt t", p=128)
        S1, S5 = g["S1"], g["S5"]
        s_sc, s_st, s_evV, s_evA, s_pe = (g["s_sc"], g["s_st"], g["s_evV"],
                                          g["s_evA"], g["s_pe"])
        for it in range(NIT):
            emit_ring_loads(P, g, it)
            ctxs = []
            if it <= NCH - 1:
                emit_ffn_prologue(P, g, S1, x_r, it)
                ctxs.append(S1)
            if 2 <= it <= NCH + 1:
                emit_outproj(P, g, it, xr1_r, xr2_r)
            if 3 <= it:
                c5 = it - 3
                emit_ffn_prologue(P, g, S5, xr2_r, c5,
                                  extra_ld_wait=[(s_sc, g["xres2_store"][c5])],
                                  lo=0)
                ctxs.append(S5)
            scan = None
            if 1 <= it <= NCH:
                scan = Scan(P, g, it)
                scan.prep()
            pump = pump_dn = None
            if scan and INTERLEAVE_SCAN:
                nslots = len(ctxs) * 144
                pump = scan.make_pump(nslots, share=0.85)
                pump_dn = scan.make_pump(nslots, share=0.15)
            emit_expert_pass(P, g, ctxs, it, pump=pump, pump_dn=pump_dn)
            if scan:
                scan.finish()
            if it <= NCH - 1:
                g["xres1_store"][it] = P.dma(
                    "scalar", xr1_r[:, :, it * TC:(it + 1) * TC],
                    S1["xin"][:].rearrange("p (kt t) -> p kt t", kt=KTH),
                    s_sc, wait=[(s_evV, S1["vcomb_last"])])
            if 3 <= it:
                c5 = it - 3
                vo = P.dma("scalar", out_r[:, :, c5 * TC:(c5 + 1) * TC],
                           S5["xin"][:].rearrange("p (kt t) -> p kt t", kt=KTH),
                           s_st, wait=[(s_evV, S5["vcomb_last"])])
                S5["xin_war"] = [(s_st, vo), (s_evV, S5["ln"]["xn_ready"]),
                                 (s_evA, S5["ln"]["last_sq"]),
                                 (s_pe, S5["xn_pe_read"])]
            if it <= NCH - 1:
                emit_p2a(P, g, it)
                S1["xin_war"] = S1["xin_war"] + [(s_sc, g["xres1_store"][it])]
        P.wait("gpsimd", (s_st, s_st.n))
        P.wait("sync", (s_st, s_st.n))
        P.wait("scalar", (s_st, s_st.n))
        block = stack.enter_context(nc.Block())
        P.finalize(block)
    return nc


_NC_CACHE = []


def kernel(L0_attn_x, L0_attn_in_proj_w, L0_attn_conv_w, L0_attn_conv_b,
           L0_attn_x_proj_w, L0_attn_dt_proj_w, L0_attn_dt_proj_b, L0_attn_A_log,
           L0_attn_D, L0_attn_out_proj_w, L0_ffn_shared_gate, L0_ffn_shared_up,
           L0_ffn_shared_down, L0_ffn_gate_w, L0_ffn_up_w, L0_ffn_down_w,
           L0_ffn_router_w, L0_ffn_routing_bias):
    import ml_dtypes
    if not _NC_CACHE:
        _NC_CACHE.append(build_program())
    nc = _NC_CACHE[0]
    f32 = lambda a: np.ascontiguousarray(np.asarray(a, dtype=np.float32))

    def lay(vec):
        return np.ascontiguousarray(np.asarray(vec, np.float32).reshape(16, 128).T)

    WT = f32(L0_attn_in_proj_w).T
    base = {
        "f_su": f32(np.asarray(L0_ffn_shared_up).T),
        "f_sg": f32(np.asarray(L0_ffn_shared_gate).T),
        "f_sd": f32(np.asarray(L0_ffn_shared_down).T),
        "f_router": f32(np.asarray(L0_ffn_router_w).T),
        "c_bias": f32(L0_ffn_routing_bias).reshape(1, 8),
        "c_sel8": np.stack([(np.arange(8)[:, None] == e) * np.ones((8, 128), np.float32)
                            for e in range(8)], axis=1).reshape(8, 1024
                           ).astype(ml_dtypes.bfloat16),
        "c_ident": np.eye(128, dtype=np.float32),
        "c_ones": np.ones((128, 1), np.float32),
        "c_ones_r": np.ones((1, 128), np.float32),
        "f_winx": f32(WT[:, :2048]),
        "f_winz": f32(WT[:, 2048:]),
        "f_wout": f32(np.asarray(L0_attn_out_proj_w).T),
        "f_xp": f32(np.asarray(L0_attn_x_proj_w).T),
        "c_dtw": lay(np.asarray(L0_attn_dt_proj_w)[:, 0]),
        "c_dtb": lay(L0_attn_dt_proj_b),
        "c_Dd": lay(L0_attn_D),
        "c_convb": lay(L0_attn_conv_b),
        "c_convw": np.ascontiguousarray(
            np.asarray(L0_attn_conv_w, np.float32)[:, 0, :]
            .reshape(16, 128, 4).transpose(1, 0, 2).reshape(128, 64)),
        "c_alog": np.ascontiguousarray(
            np.asarray(L0_attn_A_log, np.float32)
            .reshape(16, 128, 16).transpose(1, 0, 2).reshape(128, 256)),
    }
    for e in range(NE):
        base[f"f_up{e}"] = f32(np.asarray(L0_ffn_up_w)[e])
        base[f"f_gate{e}"] = f32(np.asarray(L0_ffn_gate_w)[e])
        base[f"f_down{e}"] = f32(np.asarray(L0_ffn_down_w)[e])
    x = np.asarray(L0_attn_x, np.float32)
    in_maps = [dict(base, x_T=np.ascontiguousarray(x[b].T)) for b in range(8)]
    res = run_bass_kernel_spmd(nc, in_maps, core_ids=list(range(8)))
    out = np.stack([np.asarray(res.results[b]["out_T"]).T for b in range(8)])
    return out.astype(np.float32)


# version 14
# speedup vs baseline: 1.0515x; 1.0264x over previous
import sys
sys.path.insert(0, "/opt/trn_rl_repo")
import numpy as np
import concourse.bass as bass
import concourse.mybir as mybir
from concourse.bass_utils import run_bass_kernel_spmd



F32 = mybir.dt.float32
BF16 = mybir.dt.bfloat16
AF = mybir.ActivationFunctionType
ALU = mybir.AluOpType
AX = mybir.AxisListType

ENGINES = ("sync", "scalar", "vector", "tensor", "gpsimd")


class Sem:
    __slots__ = ("h", "n", "name")

    def __init__(self, h, name):
        self.h = h
        self.n = 0
        self.name = name


class Prog:
    def __init__(self, nc, stack):
        self.nc = nc
        self.stack = stack
        self.streams = {e: [] for e in ENGINES}
        self.ninstr = 0

    def sem(self, name):
        return Sem(self.stack.enter_context(self.nc.semaphore(name)), name)

    def sbuf(self, name, shape, dt):
        return self.stack.enter_context(self.nc.sbuf_tensor(name, shape, dt))

    def psum(self, name, shape, dt=F32):
        return self.stack.enter_context(self.nc.psum_tensor(name, shape, dt))

    def emit(self, eng, method, kwargs, wait=(), inc=None, inc_by=1):
        wait = [(s, v) for (s, v) in wait if s is not None and v is not None]
        tgt = None
        if inc is not None:
            inc.n += inc_by
            tgt = inc.n
        self.streams[eng].append((method, kwargs, wait, inc, inc_by))
        self.ninstr += 1 + len(wait)
        return tgt

    def dma(self, eng, out, in_, sem, wait=(), **kw):
        return self.emit(eng, "dma_start", dict(out=out, in_=in_, **kw),
                         wait=wait, inc=sem, inc_by=16)

    def wait(self, eng, *pairs):
        self.streams[eng].append(
            (None, None, [(s, v) for s, v in pairs if v is not None], None, 0))
        self.ninstr += len(pairs)

    def finalize(self, block):
        for eng in ENGINES:
            ops = self.streams[eng]
            if not ops:
                continue
            section = getattr(block, eng)

            def body(engine, _ops=ops):
                for method, kwargs, wait, inc, inc_by in _ops:
                    for s, v in wait:
                        engine.wait_ge(s.h, v)
                    if method is None:
                        continue
                    ins = getattr(engine, method)(**kwargs)
                    if inc is not None:
                        ins.then_inc(inc.h, inc_by)

            section(body)
        self.streams = {e: [] for e in ENGINES}


def mm_group(P, out, pairs, wait=(), inc=None):
    n = len(pairs)
    v = None
    for i, (l, r) in enumerate(pairs):
        v = P.emit("tensor", "matmul",
                   dict(out=out, lhsT=l, rhs=r, start=(i == 0), stop=(i == n - 1)),
                   wait=list(wait) if i == 0 else (),
                   inc=inc if i == n - 1 else None)
    return v


def rearr_kt(ap, p=128):
    return ap.rearrange("(kt p) o -> p kt o", p=p)


class PsumPool:
    def __init__(self):
        self.freed = {}

    def wait(self, t):
        f = self.freed.get(id(t))
        return [f] if f is not None else []

    def free(self, t, sem, val):
        self.freed[id(t)] = (sem, val)



H, T, DIN, NST, NE, II = 1024, 4096, 2048, 16, 8, 2048
KTH, KTI = 8, 16
TC = 512
NCH = 8
NIT = 11
INTERLEAVE_SCAN = True



def alloc_globals(P, nc):
    g = {"nc": nc}
    g["pp"] = PsumPool()
    for s in ("s_x", "s_w", "s_wfree", "s_pe", "s_evA", "s_evV", "s_st", "s_cast",
              "s_pre", "s_sc", "s_evP", "s_wr", "s_wxp",
              "s_w0", "s_w1", "s_w2"):
        g[s] = P.sem(s)
    g["PS"] = [P.psum(f"PS{i}", [128, 512]) for i in range(8)]
    g["slots"] = [P.sbuf(f"slot{i}", [128, 4096], BF16) for i in range(3)]
    g["hc"] = 0
    g["ring"] = {}
    g["p45"] = 0
    g["p67"] = 0
    g["wb"] = P.sbuf("wb", [128, 512], F32)
    for nm in ("S1", "S5"):
        g[nm] = {
            "name": nm,
            "xin": P.sbuf(nm + "_xin", [128, 4096], F32),
            "xn": P.sbuf(nm + "_xn", [128, 4096], BF16),
            "pgu": P.sbuf(nm + "_pgu", [128, 4096], BF16),
            "wfm": P.sbuf(nm + "_wfm", [8, 512], BF16),
            "xin_war": [],
            "xn_pe_read": None,
            "wfm_pe_read": None,
            "vcomb_last": None,
            "tD_last": None,
        }
    g["sq32"] = [P.sbuf(f"sq32_{i}", [128, 512], F32) for i in range(2)]
    g["t32"] = [P.sbuf(f"t32_{i}", [128, 512], F32) for i in range(2)]
    g["rows"] = [P.sbuf(f"row{i}", [1, 512], F32) for i in range(3)]
    g["sq_war"] = {0: None, 1: None}
    g["t32_act_read"] = {0: None, 1: None}
    g["sel8"] = P.sbuf("sel8", [8, 1024], BF16)
    g["lg32"] = P.sbuf("lg32", [128, 32], F32)
    g["wr16"] = P.sbuf("wr16", [128, 64], BF16)
    g["bias_b32"] = P.sbuf("bias_b32", [128, 8], F32)
    sm = {}
    for n_, w in (("m12", 2), ("msk", 8), ("msked", 8), ("e12", 2), ("d", 1),
                  ("r", 1), ("t1", 1), ("w1", 1), ("w2", 1), ("w8", 8)):
        sm[n_] = P.sbuf("sm_" + n_, [128, w], F32)
    g["sm"] = sm
    g["ident32"] = P.sbuf("ident32", [128, 128], F32)
    g["ones_col32"] = P.sbuf("ones_col32", [128, 1], F32)
    g["ones_row32"] = P.sbuf("ones_row32", [1, 128], F32)
    g["ones16"] = P.sbuf("ones16", [1, 128], BF16)
    g["bias_row"] = P.sbuf("bias_row", [1, 8], F32)
    g["eps_col"] = P.sbuf("eps_col", [128, 1], F32)
    g["A_neg"] = P.sbuf("A_neg", [128, 256], F32)
    g["dtw32"] = P.sbuf("dtw32", [128, 16], F32)
    g["dtb32"] = P.sbuf("dtb32", [128, 16], F32)
    g["Dd32"] = P.sbuf("Dd32", [128, 16], F32)
    g["convw32"] = P.sbuf("convw32", [128, 64], F32)
    g["convb32"] = P.sbuf("convb32", [128, 16], F32)
    g["wxp16"] = P.sbuf("wxp16", [128, 16 * 33], BF16)
    g["xssm"] = [P.sbuf("xssm0", [128, 515], F32)]
    g["carry32"] = P.sbuf("carry32", [128, 48], F32)
    g["zs16"] = [P.sbuf("zs16_0", [128, 512], BF16)]
    g["xc16s"] = [P.sbuf("xc16s_0", [128, 512], BF16)]
    g["xc32s"] = [P.sbuf("xc32s_0", [128, 512], F32)]
    g["xps16"] = P.sbuf("xps16", [33, 512], BF16)
    g["xch"] = P.sbuf("xch", [128, 4096], F32)
    g["uh"] = P.sbuf("uh", [128, 4096], BF16)
    g["ug"] = P.sbuf("ug", [128, 8192], BF16)
    g["Bv_all"] = P.sbuf("Bv_all", [128, 16 * 512], BF16)
    g["Cv_all"] = P.sbuf("Cv_all", [128, 16 * 512], BF16)
    g["dtr16"] = P.sbuf("dtr16", [128, 512], BF16)
    g["dAb"] = [P.sbuf(f"dAb{i}", [128, 512], BF16) for i in range(2)]
    g["dBxb"] = [P.sbuf(f"dBxb{i}", [128, 512], BF16) for i in range(2)]
    g["hb"] = [P.sbuf(f"hb{i}", [128, 512], BF16) for i in range(2)]
    g["yst"] = [P.sbuf(f"yst{i}", [128, 512], BF16) for i in range(2)]
    g["szst"] = [P.sbuf(f"szst{i}", [128, 512], BF16) for i in range(2)]
    g["hstate"] = P.sbuf("hstate", [128, 256], BF16)
    g["xrow"] = [P.sbuf(f"xrow{i}", [1, 512], BF16) for i in range(2)]
    g["xrow2"] = [P.sbuf(f"xrow2_{i}", [1, 512], BF16) for i in range(2)]
    g["gate_last_c"] = {}
    g["ug_pe_read"] = None
    g["xc_store"] = {}
    g["sz_store"] = {}
    g["xp_store"] = {}
    g["xres1_store"] = {}
    g["xres2_store"] = {}
    g["cast_val"] = {}
    g["dth_act_read"] = None
    g["scan_dve_last"] = None
    g["gate_last"] = None
    return g


def declare_hbm(P, g, nc):
    d, b = {}, {}
    shapes = {"su": (H, II), "sg": (H, II), "sd": (II, H),
              "winx": (H, DIN), "winz": (H, DIN), "wout": (DIN, H),
              "xp": (DIN, 33)}
    for e in range(NE):
        shapes[f"up{e}"] = (H, II)
        shapes[f"gate{e}"] = (H, II)
        shapes[f"down{e}"] = (II, H)
    shapes["router"] = (H, 8)
    for name, shp in shapes.items():
        d[name] = nc.dram_tensor("f_" + name, list(shp), F32, kind="ExternalInput").ap()
        b[name] = nc.dram_tensor("b_" + name, list(shp), BF16, kind="Internal").ap()
    g["f_in"], g["wbf"] = d, b
    cn = {}
    for name, shp in (("dtw", (128, 16)), ("dtb", (128, 16)), ("Dd", (128, 16)),
                      ("convw", (128, 64)), ("convb", (128, 16)), ("alog", (128, 256)),
                      ("bias", (1, 8)), ("ident", (128, 128)), ("ones", (128, 1)),
                      ("ones_r", (1, 128))):
        cn[name] = nc.dram_tensor("c_" + name, list(shp), F32, kind="ExternalInput").ap()
    cn["sel8"] = nc.dram_tensor("c_sel8", [8, 1024], BF16, kind="ExternalInput").ap()
    g["c_in"] = cn
    g["xc_hbm"] = nc.dram_tensor("xc_hbm", [2048, 4096], F32, kind="Internal").ap()
    g["sz_hbm"] = nc.dram_tensor("sz_hbm", [2048, 4096], BF16, kind="Internal").ap()
    g["xp_hbm"] = nc.dram_tensor("xp_hbm", [33, 4096], BF16, kind="Internal").ap()


def emit_prelude(P, g):
    nc = g["nc"]
    s_pre, s_evV, s_evA, s_cast, s_pe = (g["s_pre"], g["s_evV"], g["s_evA"],
                                         g["s_cast"], g["s_pe"])
    cn = g["c_in"]
    P.dma("sync", g["ident32"][:], cn["ident"], s_pre)
    P.dma("sync", g["ones_col32"][:], cn["ones"], s_pre)
    P.dma("sync", g["ones_row32"][:], cn["ones_r"], s_pre)
    P.dma("sync", g["bias_row"][:], cn["bias"], s_pre)
    P.dma("sync", g["sel8"][:], cn["sel8"], s_pre)
    for name, dst in (("dtw", "dtw32"), ("dtb", "dtb32"), ("Dd", "Dd32"),
                      ("convw", "convw32"), ("convb", "convb32")):
        P.dma("sync", g[dst][:], cn[name], s_pre)
    v_alog = P.dma("sync", g["A_neg"][:], cn["alog"], s_pre)
    pre_consts = s_pre.n
    P.emit("vector", "memset", dict(ap=g["eps_col"][:], constant=1e-5), inc=s_evV)
    P.emit("vector", "memset", dict(ap=g["ones16"][:], constant=1.0), inc=s_evV)
    g["h_ready"] = P.emit("vector", "memset", dict(ap=g["hstate"][:], constant=0.0),
                          inc=s_evV)
    g["carry_ready"] = P.emit("vector", "memset", dict(ap=g["carry32"][:], constant=0.0),
                              inc=s_evV)
    va = P.emit("scalar", "activation",
                dict(out=g["A_neg"][:], in_=g["A_neg"][:], func=AF.Exp),
                wait=[(s_pre, v_alog)], inc=s_evA)
    g["A_ready"] = P.emit("vector", "tensor_scalar",
                          dict(out=g["A_neg"][:], in0=g["A_neg"][:], scalar1=-1.0,
                               scalar2=None, op0=ALU.mult),
                          wait=[(s_evA, va)], inc=s_evV)
    cast_order = ["router", "su", "sg", "sd"]
    for e in range(NE):
        cast_order += [f"up{e}", f"gate{e}", f"down{e}"]
    cast_order += ["winx", "winz", "wout", "xp"]
    prev_cast = 0
    for name in cast_order:
        g["cast_val"][name] = P.dma("gpsimd", g["wbf"][name], g["f_in"][name],
                                    s_cast,
                                    wait=([(s_cast, prev_cast)] if prev_cast else []))
        prev_cast = g["cast_val"][name]
    g["wr_ready"] = P.dma("sync",
                          g["wr16"][:].rearrange("p (kt e) -> p kt e", kt=KTH),
                          g["wbf"]["router"].rearrange("(kt p) e -> p kt e", p=128),
                          g["s_wr"], wait=[(s_cast, g["cast_val"]["router"])])
    g["wxp_ready"] = P.dma("sync",
                           g["wxp16"][:].rearrange("p (kt o) -> p kt o", kt=KTI),
                           g["wbf"]["xp"].rearrange("(kt p) o -> p kt o", p=128),
                           g["s_wxp"], wait=[(s_cast, g["cast_val"]["xp"])])
    t = P.emit("tensor", "matmul",
               dict(out=g["PS"][0][:, 0:8], lhsT=g["ones_row32"][0:1, :],
                    rhs=g["bias_row"][0:1, :], start=True, stop=True),
               wait=[(s_pre, pre_consts)], inc=s_pe)
    v = P.emit("vector", "tensor_copy", dict(out=g["bias_b32"][:], in_=g["PS"][0][:, 0:8]),
               wait=[(s_pe, t)], inc=s_evV)
    g["pp"].free(g["PS"][0], s_evV, v)
    P.wait("tensor", (g["s_wr"], g["wr_ready"]))



def ps45(g):
    g["p45"] = 1 - g["p45"]
    return g["PS"][4 + g["p45"]]


def ps67(g):
    g["p67"] = 1 - g["p67"]
    return g["PS"][6 + g["p67"]]


NSLOT = 3


def iter_ring_tags(it):
    tags = []
    if 2 <= it <= NCH + 1:
        tags += [("wout", q) for q in range(4)]
    for e in range(-1, NE):
        nU, nG, nD = (("su", "sg", "sd") if e < 0 else
                      (f"up{e}", f"gate{e}", f"down{e}"))
        for h in (0, 1):
            tags += [(nU, 2 * h), (nG, 2 * h), (nU, 2 * h + 1), (nG, 2 * h + 1),
                     (nD, 2 * h), (nD, 2 * h + 1)]
    if it <= NCH - 1:
        for q in range(4):
            tags += [("winx", q), ("winz", q)]
    return tags


def emit_ring_loads(P, g, it):
    s_wfree, s_cast = g["s_wfree"], g["s_cast"]
    subs = [g["s_w0"], g["s_w1"], g["s_w2"]]
    for (name, q) in iter_ring_tags(it):
        hbm = g["wbf"][name]
        if hbm.shape[0] == 1024:
            src = rearr_kt(hbm)[:, :, q * 512:(q + 1) * 512]
            kt = 8
        else:
            src = rearr_kt(hbm)[:, q * 4:(q + 1) * 4, :]
            kt = 4
        m = g["hc"]
        g["hc"] += 1
        sem = subs[m % NSLOT]
        waits = ([(s_wfree, m - (NSLOT - 1))] if m >= NSLOT else [])
        if m >= NSLOT:
            waits.append((sem, 16 * (m // NSLOT)))
        waits.append((s_cast, g["cast_val"][name]))
        slot = g["slots"][m % NSLOT]
        v = P.dma("sync", slot[:].rearrange("p (kt o) -> p kt o", kt=kt), src, sem,
                  wait=waits)
        g["ring"][(it, name, q)] = (slot, (sem, v))


def ring_free(P, g, n=1):
    P.wait("tensor", (g["s_pe"], g["s_pe"].n))
    for _ in range(n):
        P.emit("tensor", "nop", {}, inc=g["s_wfree"])



def emit_ln(P, g, *, xsl, xn_dst, xw, pe_fence, lo=6):
    pp = g["pp"]
    s_pe, s_evA, s_evV = g["s_pe"], g["s_evA"], g["s_evV"]
    PS, sq32, t32, rows = g["PS"], g["sq32"], g["t32"], g["rows"]
    pairs = [(g["ones_col32"][:, 0:1], xsl(kt)) for kt in range(KTH)]
    mm_group(P, PS[lo][0:1, :], pairs, wait=list(xw) + pp.wait(PS[lo]))
    sq_done = []
    last_sq = None
    for kt in range(KTH):
        w = list(xw) + ([(s_pe, pe_fence)] if kt == 0 else [])
        if kt == 0:
            w += [(s_evV, g["sq_war"][0]), (s_evV, g["sq_war"][1])]
        if kt >= 2:
            w.append((s_pe, sq_done[kt - 2]))
        va = P.emit("scalar", "activation",
                    dict(out=sq32[kt % 2][:], in_=xsl(kt), func=AF.Square),
                    wait=w, inc=s_evA)
        last_sq = va
        t = P.emit("tensor", "matmul",
                   dict(out=PS[lo + 1][0:1, :], lhsT=g["ones_col32"][:, 0:1],
                        rhs=sq32[kt % 2][:], start=(kt == 0), stop=(kt == KTH - 1)),
                   wait=[(s_evA, va)] + (pp.wait(PS[lo + 1]) if kt == 0 else []),
                   inc=s_pe)
        sq_done.append(t)
    vm = P.emit("vector", "tensor_scalar",
                dict(out=rows[0][0:1, :], in0=PS[lo][0:1, :], scalar1=1.0 / 1024,
                     scalar2=None, op0=ALU.mult),
                wait=[(s_pe, sq_done[-1])], inc=s_evV)
    vms = P.emit("vector", "tensor_scalar",
                 dict(out=rows[1][0:1, :], in0=PS[lo + 1][0:1, :], scalar1=1.0 / 1024,
                      scalar2=None, op0=ALU.mult), inc=s_evV)
    pp.free(PS[lo], s_evV, vms)
    pp.free(PS[lo + 1], s_evV, vms)
    v2 = P.emit("vector", "tensor_tensor",
                dict(out=rows[2][0:1, :], in0=rows[0][0:1, :], in1=rows[0][0:1, :],
                     op=ALU.mult),
                wait=[(s_evV, vm)], inc=s_evV)
    vvar = P.emit("vector", "tensor_tensor",
                  dict(out=rows[2][0:1, :], in0=rows[1][0:1, :], in1=rows[2][0:1, :],
                       op=ALU.subtract), wait=[(s_evV, v2)], inc=s_evV)
    vln = P.emit("scalar", "activation",
                 dict(out=rows[2][0:1, :], in_=rows[2][0:1, :], func=AF.Ln,
                      bias=g["eps_col"][0:1, 0:1]),
                 wait=[(s_evV, vvar)], inc=s_evA)
    vexp = P.emit("scalar", "activation",
                  dict(out=rows[1][0:1, :], in_=rows[2][0:1, :], func=AF.Exp, scale=-0.5),
                  wait=[(s_evA, vln)], inc=s_evA)
    va = P.emit("vector", "scalar_tensor_tensor",
                dict(out=rows[2][0:1, :], in0=rows[0][0:1, :], scalar=-1.0,
                     in1=rows[1][0:1, :], op0=ALU.mult, op1=ALU.mult),
                wait=[(s_evA, vexp)], inc=s_evV)
    P.emit("tensor", "matmul",
           dict(out=PS[lo][:], lhsT=g["ones_row32"][0:1, :], rhs=rows[1][0:1, :],
                start=True, stop=True),
           wait=[(s_evV, va)] + pp.wait(PS[lo]))
    tb = P.emit("tensor", "matmul",
                dict(out=PS[lo + 1][:], lhsT=g["ones_row32"][0:1, :], rhs=rows[2][0:1, :],
                     start=True, stop=True),
                wait=pp.wait(PS[lo + 1]), inc=s_pe)
    xn_ready = None
    for kt in range(KTH):
        vt = P.emit("vector", "tensor_tensor",
                    dict(out=t32[kt % 2][:], in0=xsl(kt), in1=PS[lo][:], op=ALU.mult),
                    wait=([(s_pe, tb)] if kt == 0 else []) +
                         ([(s_evA, g["t32_act_read"][kt % 2])] if kt < 2 else []),
                    inc=s_evV)
        xn_ready = P.emit("vector", "tensor_tensor",
                          dict(out=xn_dst(kt), in0=t32[kt % 2][:], in1=PS[lo + 1][:],
                               op=ALU.add),
                          wait=[(s_evV, vt)], inc=s_evV)
    g["pp"].free(PS[lo], s_evV, xn_ready)
    g["pp"].free(PS[lo + 1], s_evV, xn_ready)
    return dict(xn_ready=xn_ready, last_sq=last_sq, last_pe=sq_done[-1])



def emit_router(P, g, S, xn_ready, lo=6):
    pp = g["pp"]
    s_pe, s_evA, s_evV = g["s_pe"], g["s_evA"], g["s_evV"]
    PS, sm, lg32 = g["PS"], g["sm"], g["lg32"]
    xn = S["xn"]
    v = None
    for tt in range(TC // 128):
        pt = PS[lo + (tt % 2)]
        pairs = [(xn[:, kt * TC + tt * 128: kt * TC + tt * 128 + 128],
                  g["wr16"][:, kt * 8:(kt + 1) * 8]) for kt in range(KTH)]
        tl = mm_group(P, pt[:, 0:8], pairs,
                      wait=([(s_evV, xn_ready), (g["s_wr"], g["wr_ready"])]
                            if tt == 0 else []) + pp.wait(pt),
                      inc=s_pe)
        lsl = lg32[:, tt * 8:(tt + 1) * 8]
        v = P.emit("vector", "tensor_tensor",
                   dict(out=lsl, in0=pt[:, 0:8], in1=g["bias_b32"][:, 0:8], op=ALU.add),
                   wait=[(s_pe, tl)], inc=s_evV)
        pp.free(pt, s_evV, v)
        m12, msk, w8 = sm["m12"], sm["msk"], sm["w8"]
        vlsl = v
        v = P.emit("vector", "tensor_reduce",
                   dict(out=m12[:, 0:1], in_=lsl, axis=AX.X, op=ALU.max),
                   wait=[(s_evV, vlsl)], inc=s_evV)
        vmsk1 = P.emit("vector", "tensor_scalar",
                       dict(out=msk[:], in0=lsl, scalar1=m12[:, 0:1], scalar2=None,
                            op0=ALU.is_equal), wait=[(s_evV, v)], inc=s_evV)
        vq = P.emit("vector", "scalar_tensor_tensor",
                    dict(out=sm["msked"][:], in0=msk[:], scalar=-1e30, in1=lsl,
                         op0=ALU.mult, op1=ALU.add), wait=[(s_evV, vmsk1)], inc=s_evV)
        v = P.emit("vector", "tensor_reduce",
                   dict(out=m12[:, 1:2], in_=sm["msked"][:], axis=AX.X, op=ALU.max),
                   wait=[(s_evV, vq)], inc=s_evV)
        v = P.emit("scalar", "activation",
                   dict(out=sm["e12"][:], in_=m12[:], func=AF.Exp, scale=-1.0),
                   wait=[(s_evV, v)], inc=s_evA)
        v = P.emit("vector", "tensor_tensor",
                   dict(out=sm["d"][:], in0=sm["e12"][:, 0:1], in1=sm["e12"][:, 1:2],
                        op=ALU.add), wait=[(s_evA, v)], inc=s_evV)
        v = P.emit("vector", "tensor_scalar",
                   dict(out=sm["d"][:], in0=sm["d"][:], scalar1=2.0, scalar2=None,
                        op0=ALU.add), wait=[(s_evV, v)], inc=s_evV)
        v = P.emit("vector", "reciprocal", dict(out=sm["r"][:], in_=sm["d"][:]),
                   wait=[(s_evV, v)], inc=s_evV)
        v = P.emit("vector", "tensor_scalar",
                   dict(out=sm["r"][:], in0=sm["r"][:], scalar1=0.5, scalar2=None,
                        op0=ALU.mult), wait=[(s_evV, v)], inc=s_evV)
        vt1 = P.emit("vector", "tensor_scalar",
                     dict(out=sm["t1"][:], in0=sm["e12"][:, 1:2], scalar1=1.0,
                          scalar2=None, op0=ALU.add), wait=[(s_evV, v)], inc=s_evV)
        vw1 = P.emit("vector", "tensor_tensor",
                     dict(out=sm["w1"][:], in0=sm["t1"][:], in1=sm["r"][:], op=ALU.mult),
                     wait=[(s_evV, vt1)], inc=s_evV)
        vt1 = P.emit("vector", "tensor_scalar",
                     dict(out=sm["t1"][:], in0=sm["e12"][:, 0:1], scalar1=1.0,
                          scalar2=None, op0=ALU.add), wait=[(s_evV, vw1)], inc=s_evV)
        v = P.emit("vector", "tensor_tensor",
                   dict(out=sm["w2"][:], in0=sm["t1"][:], in1=sm["r"][:], op=ALU.mult),
                   wait=[(s_evV, vt1)], inc=s_evV)
        vw8 = P.emit("vector", "tensor_scalar",
                     dict(out=w8[:], in0=msk[:], scalar1=sm["w1"][:, 0:1], scalar2=None,
                          op0=ALU.mult), wait=[(s_evV, v), (s_evV, vmsk1)], inc=s_evV)
        vmsk2 = P.emit("vector", "tensor_scalar",
                       dict(out=msk[:], in0=lsl, scalar1=m12[:, 1:2], scalar2=None,
                            op0=ALU.is_equal), wait=[(s_evV, vw8)], inc=s_evV)
        v = P.emit("vector", "scalar_tensor_tensor",
                   dict(out=w8[:], in0=msk[:], scalar=sm["w2"][:, 0:1], in1=w8[:],
                        op0=ALU.mult, op1=ALU.add), wait=[(s_evV, vmsk2)], inc=s_evV)
        ptp = PS[lo + ((tt + 1) % 2)]
        t2 = P.emit("tensor", "transpose",
                    dict(out=ptp[0:8, 0:128], in_=w8[:], identity=g["ident32"][:]),
                    wait=[(s_evV, v)] + pp.wait(ptp), inc=s_pe)
        v = P.emit("vector", "tensor_copy",
                   dict(out=S["wfm"][:, tt * 128:(tt + 1) * 128], in_=ptp[0:8, 0:128]),
                   wait=[(s_pe, t2)] +
                        ([(s_pe, S["wfm_pe_read"])] if tt == 0 else []),
                   inc=s_evV)
        pp.free(ptp, s_evV, v)
    return v



def emit_ffn_prologue(P, g, S, src_r, c, extra_ld_wait=(), lo=6):
    s_x, s_evV = g["s_x"], g["s_evV"]
    vld = P.dma("scalar",
                S["xin"][:].rearrange("p (kt t) -> p kt t", kt=KTH),
                src_r[:, :, c * TC:(c + 1) * TC], s_x,
                wait=list(S["xin_war"]) + list(extra_ld_wait))
    pe_fence = g["s_pe"].n
    xw = [(s_x, vld)]

    def xsl(kt):
        return S["xin"][:, kt * TC:(kt + 1) * TC]
    ln = emit_ln(P, g, xsl=xsl,
                 xn_dst=lambda kt: S["xn"][:, kt * TC:(kt + 1) * TC],
                 xw=xw, pe_fence=pe_fence, lo=lo)
    wfm_v = emit_router(P, g, S, ln["xn_ready"], lo=lo)
    S["ln"] = ln
    S["wfm_v"] = wfm_v
    return ln, wfm_v



def emit_expert_pass(P, g, ctxs, it, pump=None, pump_dn=None):
    if pump is None:
        pump = lambda: None
    if pump_dn is None:
        pump_dn = pump
    pp = g["pp"]
    s_pe, s_evA, s_evV, s_w = g["s_pe"], g["s_evA"], g["s_evV"], g["s_w"]
    PS, sq32, t32 = g["PS"], g["sq32"], g["t32"]
    for e in range(-1, NE):
        nU, nG, nD = (("su", "sg", "sd") if e < 0 else
                      (f"up{e}", f"gate{e}", f"down{e}"))
        wb_ps = {}
        for S in ctxs:
            if e >= 0:
                pw = ps67(g)
                twb = P.emit("tensor", "matmul",
                             dict(out=pw[:], lhsT=g["sel8"][:, e * 128:(e + 1) * 128],
                                  rhs=S["wfm"][0:8, 0:512], start=True, stop=True),
                             wait=pp.wait(pw) +
                                  ([(s_evV, S["wfm_v"])] if e == 0 else []),
                             inc=s_pe)
                S["wfm_pe_read"] = twb
                wb_ps[id(S)] = (pw, twb)
        for half in (0, 1):
            slotUa, vUa = g["ring"][(it, nU, 2 * half)]
            slotGa, vGa = g["ring"][(it, nG, 2 * half)]
            slotUb, vUb = g["ring"][(it, nU, 2 * half + 1)]
            slotGb, vGb = g["ring"][(it, nG, 2 * half + 1)]
            slotDa, vDa = g["ring"][(it, nD, 2 * half)]
            slotDb, vDb = g["ring"][(it, nD, 2 * half + 1)]
            for qh in (0, 1):
                sU, sG = (slotUa, slotGa) if qh == 0 else (slotUb, slotGb)
                vU, vG = (vUa, vGa) if qh == 0 else (vUb, vGb)
                for S in ctxs:
                    for otl in range(4):
                        ot = qh * 4 + otl
                        pu, pg = PS[0 + (ot % 2)], PS[2 + (ot % 2)]
                        pairs_u = [(sU[:, kt * 512 + otl * 128: kt * 512 + otl * 128 + 128],
                                    S["xn"][:, kt * TC:(kt + 1) * TC]) for kt in range(KTH)]
                        tU = mm_group(P, pu[:], pairs_u,
                                      wait=pp.wait(pu) +
                                           ([vU] if otl == 0 else []),
                                      inc=s_pe)
                        pairs_g = [(sG[:, kt * 512 + otl * 128: kt * 512 + otl * 128 + 128],
                                    S["xn"][:, kt * TC:(kt + 1) * TC]) for kt in range(KTH)]
                        tG = mm_group(P, pg[:], pairs_g,
                                      wait=pp.wait(pg) +
                                           ([vG] if otl == 0 else []),
                                      inc=s_pe)
                        S["xn_pe_read"] = tG
                        vg = P.emit("scalar", "activation",
                                    dict(out=sq32[ot % 2][:], in_=pg[:], func=AF.Silu),
                                    wait=[(s_pe, tG), (s_evV, g["sq_war"][ot % 2])],
                                    inc=s_evA)
                        pp.free(pg, s_evA, vg)
                        wvp = [(s_evA, vg), (s_pe, tU)]
                        if ot == 0:
                            wvp.append((s_pe, S["tD_last"]))
                        vp = P.emit("vector", "tensor_tensor",
                                    dict(out=S["pgu"][:, ot * TC:(ot + 1) * TC],
                                         in0=sq32[ot % 2][:], in1=pu[:], op=ALU.mult),
                                    wait=wvp, inc=s_evV)
                        pp.free(pu, s_evV, vp)
                        g["sq_war"][ot % 2] = vp
                        S["vp_last"] = vp
                        pump()
                ring_free(P, g, n=2)
            for S in ctxs:
                if e >= 0:
                    pw, twb = wb_ps[id(S)]
                    vwb = P.emit("vector", "tensor_copy",
                                 dict(out=g["wb"][:], in_=pw[:]),
                                 wait=[(s_pe, twb)], inc=s_evV)
                    if half == 1:
                        pp.free(pw, s_evV, vwb)
                for oh in range(8):
                    pd = ps45(g)
                    pairs_d = ([(slotDa[:, kt * 1024 + oh * 128: kt * 1024 + oh * 128 + 128],
                                 S["pgu"][:, kt * TC:(kt + 1) * TC]) for kt in range(4)] +
                               [(slotDb[:, kt * 1024 + oh * 128: kt * 1024 + oh * 128 + 128],
                                 S["pgu"][:, (4 + kt) * TC:(5 + kt) * TC]) for kt in range(4)])
                    tD = mm_group(P, pd[:], pairs_d,
                                  wait=pp.wait(pd) + [(s_evV, S["vp_last"])] +
                                       ([vDa, vDb] if oh == 0 else []),
                                  inc=s_pe)
                    S["tD_last"] = tD
                    if e < 0:
                        vd = P.emit("vector", "tensor_scalar",
                                    dict(out=t32[oh % 2][:], in0=pd[:], scalar1=0.5,
                                         scalar2=None, op0=ALU.mult),
                                    wait=[(s_pe, tD),
                                          (s_evA, g["t32_act_read"][oh % 2])],
                                    inc=s_evV)
                    else:
                        vd = P.emit("vector", "tensor_tensor",
                                    dict(out=t32[oh % 2][:], in0=pd[:], in1=g["wb"][:],
                                         op=ALU.mult),
                                    wait=[(s_pe, tD),
                                          (s_evA, g["t32_act_read"][oh % 2])],
                                    inc=s_evV)
                    vcomb = P.emit("vector", "tensor_tensor",
                                   dict(out=S["xin"][:, oh * TC:(oh + 1) * TC],
                                        in0=S["xin"][:, oh * TC:(oh + 1) * TC],
                                        in1=t32[oh % 2][:], op=ALU.add),
                                   wait=[(s_evV, vd)], inc=s_evV)
                    pp.free(pd, s_evV, vcomb)
                    S["vcomb_last"] = vcomb
                    pump_dn()
            ring_free(P, g, n=2)



def emit_p2a(P, g, it):
    c = it
    pp = g["pp"]
    s_pe, s_evA, s_evV, s_sc, s_w = (g["s_pe"], g["s_evA"], g["s_evV"], g["s_sc"],
                                     g["s_w"])
    PS = g["PS"]
    S1 = g["S1"]
    xn = S1["xn"]
    xssm, carry, t32 = g["xssm"], g["carry32"], g["t32"]
    xc_r = g["xc_hbm"].rearrange("(kt p) t -> p kt t", p=128)
    sz = g["sz_hbm"]
    cs = slice(c * TC, (c + 1) * TC)
    pe_fence = s_pe.n
    xw = [(s_evV, S1["vcomb_last"]), (s_pe, S1["xn_pe_read"])]
    ln2 = emit_ln(P, g, xsl=lambda kt: S1["xin"][:, kt * TC:(kt + 1) * TC],
                  xn_dst=lambda kt: xn[:, kt * TC:(kt + 1) * TC],
                  xw=xw, pe_fence=pe_fence)
    wxq = [g["ring"][(it, "winx", q)] for q in range(4)]
    wzq = [g["ring"][(it, "winz", q)] for q in range(4)]
    zdma_v = g.setdefault("zdma_v", [None])
    xssm_rd = g.setdefault("xssm_rd", [None])
    silu_v = g.setdefault("silu_v", [None])
    xc16_v = g.setdefault("xc16_v", [None])
    xc32_rd = g.setdefault("xc32_rd", [None])
    xcdma_v = g.setdefault("xcdma_v", [None])
    xp_stop = None
    for i in range(KTI):
        slot, vw = wxq[i // 4]
        otl = i % 4
        px = PS[0 + (i % 2)]
        pairs = [(slot[:, kt * 512 + otl * 128: kt * 512 + otl * 128 + 128],
                  xn[:, kt * TC:(kt + 1) * TC]) for kt in range(KTH)]
        tX = mm_group(P, px[:], pairs,
                      wait=pp.wait(px) + ([vw] if otl == 0 else []) +
                           ([(s_evV, ln2["xn_ready"])] if i == 0 else []),
                      inc=s_pe)
        xt = g["xssm"][0]
        wA = [(s_pe, tX)]
        if xssm_rd[0] is not None:
            wA.append((s_evV, xssm_rd[0]))
        vev = P.emit("scalar", "copy", dict(out=xt[:, 3:515], in_=px[:]),
                     wait=wA, inc=s_evA)
        pp.free(px, s_evA, vev)
        vcr = P.emit("vector", "tensor_copy",
                     dict(out=xt[:, 0:3], in_=carry[:, i * 3:i * 3 + 3]),
                     wait=([(s_evV, g["carry_ready"])] if (c == 0 and i < 3) else []) +
                          [(s_evA, vev)],
                     inc=s_evV)
        wT = [(s_evV, vcr)]
        if silu_v[0] is not None:
            wT.append((s_evA, silu_v[0]))
        vt = P.emit("vector", "tensor_scalar",
                    dict(out=t32[i % 2][:], in0=xt[:, 0:512],
                         scalar1=g["convw32"][:, 4 * i:4 * i + 1], scalar2=None,
                         op0=ALU.mult),
                    wait=wT + ([(s_evA, g["t32_act_read"][i % 2])] if i < 2 else []),
                    inc=s_evV)
        for k in range(1, 4):
            vt = P.emit("vector", "scalar_tensor_tensor",
                        dict(out=t32[i % 2][:], in0=xt[:, k:k + 512],
                             scalar=g["convw32"][:, 4 * i + k:4 * i + k + 1],
                             in1=t32[i % 2][:], op0=ALU.mult, op1=ALU.add),
                        wait=[(s_evV, vt)], inc=s_evV)
        vco = P.emit("vector", "tensor_copy",
                     dict(out=carry[:, i * 3:i * 3 + 3], in_=xt[:, 512:515]),
                     inc=s_evV)
        xssm_rd[0] = vco
        wS = [(s_evV, vt)]
        if xcdma_v[0] is not None:
            wS.append((s_sc, xcdma_v[0]))
        if xc32_rd[0] is not None:
            wS.append((s_evV, xc32_rd[0]))
        vs = P.emit("scalar", "activation",
                    dict(out=g["xc32s"][0][:], in_=t32[i % 2][:],
                         func=AF.Silu, bias=g["convb32"][:, i:i + 1]),
                    wait=wS, inc=s_evA)
        silu_v[0] = vs
        g["t32_act_read"][i % 2] = vs
        xcdma_v[0] = P.dma("scalar", xc_r[:, i, cs], g["xc32s"][0][:],
                           s_sc, wait=[(s_evA, vs)])
        vxc = P.emit("vector", "tensor_copy",
                     dict(out=g["xc16s"][0][:], in_=g["xc32s"][0][:]),
                     wait=[(s_evA, vs)] +
                          ([(s_pe, xc16_v[0])] if xc16_v[0] else []),
                     inc=s_evV)
        xc32_rd[0] = vxc
        xp_stop = P.emit("tensor", "matmul",
                         dict(out=PS[6][0:33, 0:512],
                              lhsT=g["wxp16"][:, i * 33:(i + 1) * 33],
                              rhs=g["xc16s"][0][:],
                              start=(i == 0), stop=(i == KTI - 1),
                              skip_group_check=True),
                         wait=[(s_evV, vxc)] +
                              ([(g["s_wxp"], g["wxp_ready"])] if i == 0 else []) +
                              (pp.wait(PS[6]) if i == 0 else []),
                         inc=s_pe)
        xc16_v[0] = xp_stop
        slotz, vwz = wzq[i // 4]
        pz = PS[2 + (i % 2)]
        pairs = [(slotz[:, kt * 512 + otl * 128: kt * 512 + otl * 128 + 128],
                  xn[:, kt * TC:(kt + 1) * TC]) for kt in range(KTH)]
        tZ = mm_group(P, pz[:], pairs,
                      wait=pp.wait(pz) + ([vwz] if otl == 0 else []),
                      inc=s_pe)
        S1["xn_pe_read"] = tZ
        wZ = [(s_pe, tZ)]
        if zdma_v[0] is not None:
            wZ.append((s_sc, zdma_v[0]))
        vz = P.emit("scalar", "activation",
                    dict(out=g["zs16"][0][:], in_=pz[:], func=AF.Silu),
                    wait=wZ, inc=s_evA)
        pp.free(pz, s_evA, vz)
        zdma_v[0] = P.dma("scalar", sz[i * 128:(i + 1) * 128, cs],
                          g["zs16"][0][:], s_sc, wait=[(s_evA, vz)])
        if otl == 3:
            ring_free(P, g, n=2)
    g["sz_store"][c] = zdma_v[0]
    wxpv = [(s_pe, xp_stop)]
    if (c - 1) in g["xp_store"]:
        wxpv.append((s_sc, g["xp_store"][c - 1]))
    vxp = P.emit("vector", "tensor_copy",
                 dict(out=g["xps16"][0:33, 0:512], in_=PS[6][0:33, 0:512]),
                 wait=wxpv, inc=s_evV)
    pp.free(PS[6], s_evV, vxp)
    g["xp_store"][c] = P.dma("scalar", g["xp_hbm"][:, cs], g["xps16"][0:33, 0:512],
                             s_sc, wait=[(s_evV, vxp)])
    S1["xin_war"] = [(s_evV, ln2["xn_ready"]), (s_evA, ln2["last_sq"]),
                     (s_pe, ln2["last_pe"])]



def emit_outproj(P, g, it, xres1_r, xres2_r):
    c = it - 2
    pp = g["pp"]
    s_pe, s_evV, s_sc, s_w = g["s_pe"], g["s_evV"], g["s_sc"], g["s_w"]
    cs = slice(c * TC, (c + 1) * TC)
    xch = g["xch"]
    wq = [g["ring"][(it, "wout", q)] for q in range(4)]
    vxr = P.dma("scalar", xch[:].rearrange("p (kt t) -> p kt t", kt=KTH),
                xres1_r[:, :, cs], s_sc,
                wait=[(s_sc, g["xres1_store"][c]), (s_evV, g["gate_last"])])
    vev = None
    for qp in (0, 1):
        for oh in range(KTH):
            po = ps45(g)
            pairs = []
            for q in (2 * qp, 2 * qp + 1):
                slotq = wq[q][0]
                pairs += [(slotq[:, kt * 1024 + oh * 128: kt * 1024 + oh * 128 + 128],
                           g["ug"][:, (q * 4 + kt) * TC:(q * 4 + kt + 1) * TC])
                          for kt in range(4)]
            tO = mm_group(P, po[:], pairs,
                          wait=pp.wait(po) +
                               ([wq[2 * qp][1], wq[2 * qp + 1][1]] +
                                ([(s_evV, g["gate_last_c"][c])] if qp == 0 else [])
                                if oh == 0 else []),
                          inc=s_pe)
            g["ug_pe_read"] = tO
            vev = P.emit("vector", "tensor_tensor",
                         dict(out=xch[:, oh * TC:(oh + 1) * TC],
                              in0=xch[:, oh * TC:(oh + 1) * TC], in1=po[:],
                              op=ALU.add),
                         wait=[(s_pe, tO)] + ([(s_sc, vxr)] if qp == 0 else []),
                         inc=s_evV)
            pp.free(po, s_evV, vev)
        ring_free(P, g, n=2)
    g["xres2_store"][c] = P.dma("scalar", xres2_r[:, :, cs],
                                xch[:].rearrange("p (kt t) -> p kt t", kt=KTH),
                                s_sc, wait=[(s_evV, vev)])
    g["xch_war"] = [(s_sc, g["xres2_store"][c]), (s_evV, vev)]



class Scan:
    RD = 2

    def __init__(self, P, g, it):
        self.P, self.g, self.it = P, g, it
        self.c = it - 1
        self.cs = slice(self.c * TC, (self.c + 1) * TC)
        self.i = 0
        self.scan_v = {}
        self.ymul_v = {}
        self.da_v = {}
        self.dbx_v = {}
        self.hst_v = {}
        self.vdt = [None, None]
        self.vu = [None, None]
        self.vyb = [None, None]
        self.yadd_v = [{}, {}]
        self.gate_v = [{}, {}]
        self.ymul_guard = {0: (g["s_evP"], g["s_evP"].n),
                           1: (g["s_evP"], g["s_evP"].n)}
        self.h1_prepped = False
        self.carry = 0.0
        self.pe_guard = g["s_pe"].n

    def prep(self):
        P, g, c = self.P, self.g, self.c
        pp = g["pp"]
        s_pe, s_evA, s_evV, s_sc = g["s_pe"], g["s_evA"], g["s_evV"], g["s_sc"]
        xrow, xrow2 = g["xrow"], g["xrow2"]
        Bv_all, Cv_all = g["Bv_all"], g["Cv_all"]
        relay_v = {0: None, 1: None}
        bc_evac = {}
        for r in range(33):
            par = r % 2
            src_row = g["xp_hbm"][r:r + 1, self.cs]
            wld = [(s_sc, g["xp_store"][c])] if r == 0 else []
            if relay_v[par] is not None:
                wld.append((s_evV, relay_v[par]))
            vld = P.dma("scalar", xrow[par][0:1, :], src_row, s_sc, wait=wld)
            wrel = [(s_sc, vld)]
            if (r - 2) in bc_evac:
                wrel.append((s_pe, bc_evac[r - 2][0]))
            rv = P.emit("vector", "tensor_copy",
                        dict(out=xrow2[par][0:1, :], in_=xrow[par][0:1, :]),
                        wait=wrel, inc=s_evV)
            relay_v[par] = rv
            pb = ps45(g)
            tb = P.emit("tensor", "matmul",
                        dict(out=pb[:], lhsT=g["ones16"][0:1, :],
                             rhs=xrow2[par][0:1, :], start=True, stop=True),
                        wait=pp.wait(pb) + [(s_evV, rv)], inc=s_pe)
            if r == 0:
                ve = P.emit("scalar", "copy", dict(out=g["dtr16"][:], in_=pb[:]),
                            wait=[(s_pe, tb)], inc=s_evA)
                self.vdtr_e = ve
            elif r <= 16:
                n = r - 1
                ve = P.emit("scalar", "copy",
                            dict(out=Bv_all[:, n * TC:(n + 1) * TC], in_=pb[:]),
                            wait=[(s_pe, tb)], inc=s_evA)
            else:
                n = r - 17
                ve = P.emit("scalar", "copy",
                            dict(out=Cv_all[:, n * TC:(n + 1) * TC], in_=pb[:]),
                            wait=[(s_pe, tb)], inc=s_evA)
            pp.free(pb, s_evA, ve)
            bc_evac[r] = (tb, ve)
        self.bc_done = bc_evac[32][1]
        self._prep_half(0)

    def _prep_half(self, half):
        P, g, c = self.P, self.g, self.c
        s_pe, s_evA, s_evV, s_sc = g["s_pe"], g["s_evA"], g["s_evV"], g["s_sc"]
        t32, sq32 = g["t32"], g["sq32"]
        xch, uh, ug = g["xch"], g["uh"], g["ug"]
        xc_r = g["xc_hbm"].rearrange("(kt p) t -> p kt t", p=128)
        wx = list(g.get("xch_war", []))
        if half == 1:
            wx = [(s_evV, self.gate_v[0][7]), (s_sc, g["xcdma_sc_last"])]
        else:
            wx.append((s_sc, g["xcdma_v"][0]))
        vxch = P.dma("scalar", xch[:].rearrange("p (kt t) -> p kt t", kt=KTH),
                     xc_r[:, half * 8:(half + 1) * 8, self.cs], s_sc, wait=wx)
        g["xcdma_sc_last"] = vxch
        dth = ug[:, half * 4096:(half + 1) * 4096]
        vdt = {}
        prevdt = {0: None, 1: None}
        pe_snap = self.pe_guard
        for j in range(8):
            d = half * 8 + j
            par = j % 2
            a_, b_ = t32[par][:], sq32[par][:]
            w0 = [(s_evA, self.vdtr_e)]
            if prevdt[par] is not None:
                w0.append((s_evV, prevdt[par]))
            if j < 2:
                w0.append((s_evA, g["t32_act_read"][par]))
                w0.append((s_evA, s_evA.n))
            v0 = P.emit("vector", "tensor_scalar",
                        dict(out=a_, in0=g["dtr16"][:],
                             scalar1=g["dtw32"][:, d:d + 1],
                             scalar2=g["dtb32"][:, d:d + 1],
                             op0=ALU.mult, op1=ALU.add),
                        wait=w0, inc=s_evV)
            v1 = P.emit("scalar", "activation",
                        dict(out=b_, in_=a_, func=AF.Abs),
                        wait=[(s_evV, v0), (s_evV, g["sq_war"][par])] +
                             ([(s_pe, pe_snap)] if j < 2 else []),
                        inc=s_evA)
            g["t32_act_read"][par] = v1
            v2 = P.emit("scalar", "activation",
                        dict(out=b_, in_=b_, func=AF.Exp, scale=-1.0),
                        wait=[(s_evA, v1)], inc=s_evA)
            v3 = P.emit("scalar", "activation",
                        dict(out=b_, in_=b_, func=AF.Ln, bias=1.0),
                        wait=[(s_evA, v2)], inc=s_evA)
            v4 = P.emit("vector", "tensor_scalar",
                        dict(out=a_, in0=a_, scalar1=0.0, scalar2=None,
                             op0=ALU.max),
                        wait=[(s_evA, v1)], inc=s_evV)
            v5 = P.emit("vector", "tensor_tensor",
                        dict(out=dth[:, j * TC:(j + 1) * TC], in0=a_, in1=b_,
                             op=ALU.add),
                        wait=[(s_evV, v4), (s_evA, v3)] +
                             ([(s_pe, g["ug_pe_read"])] if j == 0 else []),
                        inc=s_evV)
            vdt[j] = v5
            prevdt[par] = v5
            g["sq_war"][par] = v5
        vu, vyb = {}, {}
        for j in range(8):
            vu[j] = P.emit("vector", "tensor_tensor",
                           dict(out=uh[:, j * TC:(j + 1) * TC],
                                in0=dth[:, j * TC:(j + 1) * TC],
                                in1=xch[:, j * TC:(j + 1) * TC], op=ALU.mult),
                           wait=[(s_evV, vdt[j]), (s_sc, vxch)], inc=s_evV)
        for j in range(8):
            d = half * 8 + j
            vyb[j] = P.emit("vector", "tensor_scalar",
                            dict(out=xch[:, j * TC:(j + 1) * TC],
                                 in0=xch[:, j * TC:(j + 1) * TC],
                                 scalar1=g["Dd32"][:, d:d + 1], scalar2=None,
                                 op0=ALU.mult),
                            wait=[(s_evV, vu[j])], inc=s_evV)
        self.vdt[half] = vdt
        self.vu[half] = vu
        self.vyb[half] = vyb

    def _tail(self, i):
        P, g = self.P, self.g
        s_evA, s_evV, s_evP = g["s_evA"], g["s_evV"], g["s_evP"]
        half = i // 128
        li = i % 128
        n_, j_ = divmod(li, 8)
        d_ = half * 8 + j_
        ib = i % self.RD
        col = d_ * NST + n_
        dA, dBx, hb = g["dAb"], g["dBxb"], g["hb"]
        xch = g["xch"]
        sv = P.emit("vector", "tensor_tensor_scan",
                    dict(out=hb[ib][:], data0=dA[ib][:], data1=dBx[ib][:],
                         initial=g["hstate"][:, col:col + 1],
                         op0=ALU.mult, op1=ALU.add),
                    wait=[(s_evA, self.da_v[i]), (s_evV, self.dbx_v[i])] +
                         ([(s_evV, self.hst_v[i - self.RD]),
                           (s_evP, self.ymul_v[i - self.RD])] if i >= self.RD else
                          ([(s_evV, g["h_ready"])] if self.c == 0 else [])),
                    inc=s_evV)
        self.scan_v[i] = sv
        mv = P.emit("gpsimd", "tensor_tensor",
                    dict(out=g["yst"][i % 2][:], in0=hb[ib][:],
                         in1=g["Cv_all"][:, n_ * TC:(n_ + 1) * TC], op=ALU.mult),
                    wait=[(s_evV, sv), self.ymul_guard[i % 2]],
                    inc=s_evP)
        self.ymul_v[i] = mv
        av = P.emit("gpsimd", "tensor_tensor",
                    dict(out=xch[:, j_ * TC:(j_ + 1) * TC],
                         in0=xch[:, j_ * TC:(j_ + 1) * TC],
                         in1=g["yst"][i % 2][:], op=ALU.add),
                    wait=[(s_evP, mv), (s_evV, self.vyb[half][j_])],
                    inc=s_evP)
        self.yadd_v[half][j_] = av
        self.ymul_guard[i % 2] = (s_evP, av)
        self.hst_v[i] = P.emit("vector", "tensor_copy",
                               dict(out=g["hstate"][:, col:col + 1],
                                    in_=hb[ib][:, 511:512]),
                               wait=[(s_evV, sv)], inc=s_evV)

    def _unit(self, i):
        P, g = self.P, self.g
        s_evA, s_evV = g["s_evA"], g["s_evV"]
        half = i // 128
        li = i % 128
        n, j = divmod(li, 8)
        ib = i % self.RD
        dth = g["ug"][:, half * 4096:(half + 1) * 4096]
        wA = [(s_evV, self.vdt[half][j])]
        if i >= self.RD:
            wA.append((s_evV, self.scan_v[i - self.RD]))
        if self.c == 0 and i == 0:
            wA.append((s_evV, g["A_ready"]))
        if i == 0:
            wA.append((s_evA, self.bc_done))
        self.da_v[i] = P.emit(
            "scalar", "activation",
            dict(out=g["dAb"][ib][:], in_=dth[:, j * TC:(j + 1) * TC],
                 func=AF.Exp,
                 scale=g["A_neg"][:, (half * 8 + j) * NST + n:
                                  (half * 8 + j) * NST + n + 1]),
            wait=wA, inc=s_evA)
        wB = [(s_evV, self.vu[half][j])]
        if i == 0:
            wB.append((s_evA, self.bc_done))
        if i >= self.RD:
            wB.append((s_evV, self.scan_v[i - self.RD]))
        self.dbx_v[i] = P.emit(
            "vector", "tensor_tensor",
            dict(out=g["dBxb"][ib][:], in0=g["uh"][:, j * TC:(j + 1) * TC],
                 in1=g["Bv_all"][:, n * TC:(n + 1) * TC], op=ALU.mult),
            wait=wB, inc=s_evV)

    def _gate(self, half):
        P, g, c = self.P, self.g, self.c
        s_pe, s_evV, s_evP, s_sc = g["s_pe"], g["s_evV"], g["s_evP"], g["s_sc"]
        sz = g["sz_hbm"]
        xch, ug = g["xch"], g["ug"]
        szdma = {}
        for j in range(2):
            d = half * 8 + j
            szdma[j] = P.dma("scalar", g["szst"][j % 2][:],
                             sz[d * 128:(d + 1) * 128, self.cs], s_sc,
                             wait=[(s_sc, g["sz_store"][c])] +
                                  ([(s_evV, g["szst_war"])]
                                   if g.get("szst_war") else []))
        for j in range(8):
            d = half * 8 + j
            self.gate_v[half][j] = P.emit(
                "vector", "tensor_tensor",
                dict(out=ug[:, d * TC:(d + 1) * TC],
                     in0=xch[:, j * TC:(j + 1) * TC],
                     in1=g["szst"][j % 2][:], op=ALU.mult),
                wait=[(s_evP, self.yadd_v[half][j]), (s_sc, szdma[j])],
                inc=s_evV)
            if j + 2 < 8:
                szdma[j + 2] = P.dma("scalar", g["szst"][(j + 2) % 2][:],
                                     sz[(d + 2) * 128:(d + 3) * 128, self.cs],
                                     s_sc, wait=[(s_evV, self.gate_v[half][j])])
        g["szst_war"] = self.gate_v[half][7]
        g["gate_last"] = self.gate_v[half][7]

    def pump_n(self, k):
        while k > 0 and self.i < 256:
            i = self.i
            if i == 128 and not self.h1_prepped:
                self._tail(127)
                self._gate(0)
                self._prep_half(1)
                self.h1_prepped = True
            self._unit(i)
            if i not in (0, 128):
                self._tail(i - 1)
            self.i += 1
            k -= 1

    def make_pump(self, sites, share=1.0):
        per = 256.0 * share / max(sites, 1)

        def pump():
            self.carry += per
            n = int(self.carry)
            if n:
                self.carry -= n
                self.pump_n(n)
        return pump

    def finish(self):
        g = self.g
        self.pump_n(256)
        self._tail(255)
        self._gate(1)
        g["gate_last_c"][self.c] = g["gate_last"]
        g["xch_war"] = [(g["s_evV"], g["gate_last"])]



def build_program():
    nc = bass.Bass("TRN2", target_bir_lowering=False)
    x_T = nc.dram_tensor("x_T", [H, T], F32, kind="ExternalInput").ap()
    out_T = nc.dram_tensor("out_T", [H, T], F32, kind="ExternalOutput").ap()
    xres1 = nc.dram_tensor("xres1", [H, T], F32, kind="Internal").ap()
    xres2 = nc.dram_tensor("xres2", [H, T], F32, kind="Internal").ap()
    from contextlib import ExitStack
    with ExitStack() as stack:
        P = Prog(nc, stack)
        g = alloc_globals(P, nc)
        declare_hbm(P, g, nc)
        emit_prelude(P, g)
        x_r = x_T.rearrange("(kt p) t -> p kt t", p=128)
        out_r = out_T.rearrange("(kt p) t -> p kt t", p=128)
        xr1_r = xres1.rearrange("(kt p) t -> p kt t", p=128)
        xr2_r = xres2.rearrange("(kt p) t -> p kt t", p=128)
        S1, S5 = g["S1"], g["S5"]
        s_sc, s_st, s_evV, s_evA, s_pe = (g["s_sc"], g["s_st"], g["s_evV"],
                                          g["s_evA"], g["s_pe"])
        for it in range(NIT):
            emit_ring_loads(P, g, it)
            if 2 <= it <= NCH + 1:
                emit_outproj(P, g, it, xr1_r, xr2_r)
            ctxs = []
            if it <= NCH - 1:
                emit_ffn_prologue(P, g, S1, x_r, it)
                ctxs.append(S1)
            if 3 <= it:
                c5 = it - 3
                emit_ffn_prologue(P, g, S5, xr2_r, c5,
                                  extra_ld_wait=[(s_sc, g["xres2_store"][c5])],
                                  lo=0)
                ctxs.append(S5)
            scan = None
            if 1 <= it <= NCH:
                scan = Scan(P, g, it)
                scan.prep()
            pump = pump_dn = None
            if scan and INTERLEAVE_SCAN:
                nslots = len(ctxs) * 144
                pump = scan.make_pump(nslots, share=0.85)
                pump_dn = scan.make_pump(nslots, share=0.15)
            emit_expert_pass(P, g, ctxs, it, pump=pump, pump_dn=pump_dn)
            if scan:
                scan.finish()
            if it <= NCH - 1:
                g["xres1_store"][it] = P.dma(
                    "scalar", xr1_r[:, :, it * TC:(it + 1) * TC],
                    S1["xin"][:].rearrange("p (kt t) -> p kt t", kt=KTH),
                    s_sc, wait=[(s_evV, S1["vcomb_last"])])
            if 3 <= it:
                c5 = it - 3
                vo = P.dma("scalar", out_r[:, :, c5 * TC:(c5 + 1) * TC],
                           S5["xin"][:].rearrange("p (kt t) -> p kt t", kt=KTH),
                           s_st, wait=[(s_evV, S5["vcomb_last"])])
                S5["xin_war"] = [(s_st, vo), (s_evV, S5["ln"]["xn_ready"]),
                                 (s_evA, S5["ln"]["last_sq"]),
                                 (s_pe, S5["xn_pe_read"])]
            if it <= NCH - 1:
                emit_p2a(P, g, it)
                S1["xin_war"] = S1["xin_war"] + [(s_sc, g["xres1_store"][it])]
        P.wait("gpsimd", (s_st, s_st.n))
        P.wait("sync", (s_st, s_st.n))
        P.wait("scalar", (s_st, s_st.n))
        block = stack.enter_context(nc.Block())
        P.finalize(block)
    return nc


_NC_CACHE = []


def kernel(L0_attn_x, L0_attn_in_proj_w, L0_attn_conv_w, L0_attn_conv_b,
           L0_attn_x_proj_w, L0_attn_dt_proj_w, L0_attn_dt_proj_b, L0_attn_A_log,
           L0_attn_D, L0_attn_out_proj_w, L0_ffn_shared_gate, L0_ffn_shared_up,
           L0_ffn_shared_down, L0_ffn_gate_w, L0_ffn_up_w, L0_ffn_down_w,
           L0_ffn_router_w, L0_ffn_routing_bias):
    import ml_dtypes
    if not _NC_CACHE:
        _NC_CACHE.append(build_program())
    nc = _NC_CACHE[0]
    f32 = lambda a: np.ascontiguousarray(np.asarray(a, dtype=np.float32))

    def lay(vec):
        return np.ascontiguousarray(np.asarray(vec, np.float32).reshape(16, 128).T)

    WT = f32(L0_attn_in_proj_w).T
    base = {
        "f_su": f32(np.asarray(L0_ffn_shared_up).T),
        "f_sg": f32(np.asarray(L0_ffn_shared_gate).T),
        "f_sd": f32(np.asarray(L0_ffn_shared_down).T),
        "f_router": f32(np.asarray(L0_ffn_router_w).T),
        "c_bias": f32(L0_ffn_routing_bias).reshape(1, 8),
        "c_sel8": np.stack([(np.arange(8)[:, None] == e) * np.ones((8, 128), np.float32)
                            for e in range(8)], axis=1).reshape(8, 1024
                           ).astype(ml_dtypes.bfloat16),
        "c_ident": np.eye(128, dtype=np.float32),
        "c_ones": np.ones((128, 1), np.float32),
        "c_ones_r": np.ones((1, 128), np.float32),
        "f_winx": f32(WT[:, :2048]),
        "f_winz": f32(WT[:, 2048:]),
        "f_wout": f32(np.asarray(L0_attn_out_proj_w).T),
        "f_xp": f32(np.asarray(L0_attn_x_proj_w).T),
        "c_dtw": lay(np.asarray(L0_attn_dt_proj_w)[:, 0]),
        "c_dtb": lay(L0_attn_dt_proj_b),
        "c_Dd": lay(L0_attn_D),
        "c_convb": lay(L0_attn_conv_b),
        "c_convw": np.ascontiguousarray(
            np.asarray(L0_attn_conv_w, np.float32)[:, 0, :]
            .reshape(16, 128, 4).transpose(1, 0, 2).reshape(128, 64)),
        "c_alog": np.ascontiguousarray(
            np.asarray(L0_attn_A_log, np.float32)
            .reshape(16, 128, 16).transpose(1, 0, 2).reshape(128, 256)),
    }
    for e in range(NE):
        base[f"f_up{e}"] = f32(np.asarray(L0_ffn_up_w)[e])
        base[f"f_gate{e}"] = f32(np.asarray(L0_ffn_gate_w)[e])
        base[f"f_down{e}"] = f32(np.asarray(L0_ffn_down_w)[e])
    x = np.asarray(L0_attn_x, np.float32)
    in_maps = [dict(base, x_T=np.ascontiguousarray(x[b].T)) for b in range(8)]
    res = run_bass_kernel_spmd(nc, in_maps, core_ids=list(range(8)))
    out = np.stack([np.asarray(res.results[b]["out_T"]).T for b in range(8)])
    return out.astype(np.float32)
